# version 13
# baseline (speedup 1.0000x reference)
import contextlib
import numpy as np
import ml_dtypes
import concourse.bass as bass
import concourse.mybir as mybir
from concourse.bass_utils import run_bass_kernel_spmd

F32 = mybir.dt.float32
BF16 = mybir.dt.bfloat16
AF = mybir.ActivationFunctionType
ALU = mybir.AluOpType
AX = mybir.AxisListType
BF = ml_dtypes.bfloat16

NCORE = 8
SEQ = 2048
DM = 1024
SPC = 2
T = SPC * SEQ
NH_A = 16
HD = 64
NH_B = 8
WIN = 512
INNER = 512
HID = 2730
HIDP = 2816
W2C = HIDP // 128
NEG = -1.0e30
EPS = 1e-5
YW = SPC * (SEQ + 1)

ENGINES = ("tensor", "vector", "scalar", "gpsimd", "sync")
SEMS = ("pe", "ve", "se", "ld0", "ld1", "ld2", "ld3", "ld4", "ld5", "ld6", "ld7",
        "gp0", "gp1", "gp2", "sc0", "sc1", "sc2", "sc3")

DBG_STOP = "full"
A3LIM = None
A3SUB = 9
A3QT = 4


class Sched:
    def __init__(self):
        self.st = {e: [] for e in ENGINES}
        self.cnt = {s: 0 for s in SEMS}
        self.pend = {e: [] for e in ENGINES}

    def op(self, eng, fn, wait=(), inc=None, inc_amt=1, tag=""):
        w = list(self.pend[eng]) + list(wait)
        self.pend[eng] = []
        incs = []
        if inc is not None:
            incs.append((inc, inc_amt))
            self.cnt[inc] += inc_amt
        self.st[eng].append((fn, tuple(w), tuple(incs), tag))
        return self.cnt[inc] if inc is not None else None

    def t(self, fn, wait=(), inc=False, tag=""):
        return self.op("tensor", fn, wait, inc="pe" if inc else None, tag=tag)

    def v(self, fn, wait=(), inc=False, tag=""):
        w = list(wait) + [("ve", self.cnt["ve"])]
        return self.op("vector", fn, w, inc="ve", tag=tag)

    def s(self, fn, wait=(), inc=False, tag=""):
        w = list(wait) + [("se", self.cnt["se"])]
        return self.op("scalar", fn, w, inc="se", tag=tag)

    def dma(self, queue, out, in_, sem, wait=(), tag="", slow=False):
        fn = lambda e, out=out, in_=in_, slow=slow: e.dma_start(
            out=out, in_=in_, allow_slow_non_contiguous=slow)
        c = self.op(queue, fn, wait, inc=sem, inc_amt=16, tag=tag)
        return (sem, c)

    def now(self, sem):
        return self.cnt[sem]

    def tok(self):
        return dict(self.cnt)

    def fence(self, engines, token):
        w = [(s, v) for s, v in token.items() if v > 0]
        for e in engines:
            self.pend[e].extend(w)

    def barrier(self):
        self.fence(ENGINES, self.tok())

    def check(self):
        ptr = {e: 0 for e in ENGINES}
        cnt = {s: 0 for s in SEMS}
        total = sum(len(v) for v in self.st.values())
        done = 0
        progress = True
        strict = []
        est = {}
        while progress:
            progress = False
            for e in ENGINES:
                while ptr[e] < len(self.st[e]):
                    fn, waits, incs, tag = self.st[e][ptr[e]]
                    if any(cnt[s] < v for s, v in waits):
                        break
                    for s, v in waits:
                        if s[0:2] in ("ld", "gp", "sc"):
                            if v < cnt[s] and v > est.get(s, 0) and len(strict) < 20:
                                strict.append(f"{e}@{ptr[e]} tag={tag} waits {s}>={v} "
                                              f"issued={cnt[s]} est={est.get(s, 0)}")
                            est[s] = max(est.get(s, 0), v)
                    for s, a in incs:
                        cnt[s] += a
                    ptr[e] += 1
                    done += 1
                    progress = True
        if strict:
            raise RuntimeError("Strict DMA-wait violations:\n  " + "\n  ".join(strict))
        if done != total:
            lines = []
            for e in ENGINES:
                if ptr[e] < len(self.st[e]):
                    fn, waits, incs, tag = self.st[e][ptr[e]]
                    unmet = [(s, v, cnt[s]) for s, v in waits if cnt[s] < v]
                    lines.append(f"  {e}@{ptr[e]}/{len(self.st[e])} tag={tag} unmet={unmet}")
            raise RuntimeError("Sched deadlock:\n" + "\n".join(lines))

    def emit(self, block, sems):
        for ename in ENGINES:
            items = self.st[ename]

            def body(eng, items=items):
                waited = {}
                for fn, waits, incs, tag in items:
                    for s, v in waits:
                        if v > 0 and waited.get(s, -1) < v:
                            eng.wait_ge(sems[s], v)
                            waited[s] = v
                    ins = fn(eng)
                    for s, a in incs:
                        ins.then_inc(sems[s], a)

            getattr(block, ename)(body)


_NAME_CTR = [0]


class Region:
    def __init__(self, nc):
        self.nc = nc
        self.stack = contextlib.ExitStack()

    def sb(self, name, shape, dtype):
        _NAME_CTR[0] += 1
        uname = f"{name}_{_NAME_CTR[0]}"
        return self.stack.enter_context(self.nc.sbuf_tensor(uname, shape, dtype)).ap()

    def close(self):
        self.stack.close()


def _prep_weights(inp):
    w = {}
    w["wqkvA"] = np.ascontiguousarray(np.asarray(inp["L0_attn_Wqkv_w"]).T)
    w["woA"] = np.ascontiguousarray(np.asarray(inp["L0_attn_Wo_w"]).T)
    w["bqkvA"] = np.ascontiguousarray(np.asarray(inp["L0_attn_Wqkv_b"]).reshape(24, 128).T)
    w["boA"] = np.ascontiguousarray(np.asarray(inp["L0_attn_Wo_b"]).reshape(8, 128).T)
    w["ghost"] = np.ascontiguousarray(np.tile(np.asarray(inp["L0_attn_ghost"])[None, :], (128, 1)))
    w["wqkvB"] = np.ascontiguousarray(np.asarray(inp["L0_ffn_wqkv"]))
    w["woB"] = np.ascontiguousarray(np.asarray(inp["L0_ffn_wo"]))
    w["qs"] = np.ascontiguousarray(np.tile(np.asarray(inp["L0_ffn_q_scale"]), (1, 2)).T)
    w["ks"] = np.ascontiguousarray(np.tile(np.asarray(inp["L0_ffn_k_scale"]), (1, 2)).T)
    w["lng"] = np.ascontiguousarray(np.asarray(inp["L0_ffn_ln_g"]).reshape(2, 8, 128).transpose(2, 0, 1).reshape(128, 16))
    w["lnb"] = np.ascontiguousarray(np.asarray(inp["L0_ffn_ln_b"]).reshape(2, 8, 128).transpose(2, 0, 1).reshape(128, 16))
    w1 = np.asarray(inp["L0_ffn_w1"])
    a = np.zeros((2, DM, HIDP), np.float32)
    g = np.zeros((2, DM, HIDP), np.float32)
    a[:, :, :HID] = w1[:, :, :HID]
    g[:, :, :HID] = w1[:, :, HID:]
    w1p = np.zeros((2, DM, 2 * HIDP), np.float32)
    for c in range(W2C):
        w1p[:, :, c * 256:c * 256 + 128] = a[:, :, c * 128:(c + 1) * 128]
        w1p[:, :, c * 256 + 128:c * 256 + 256] = g[:, :, c * 128:(c + 1) * 128]
    w["w1"] = np.ascontiguousarray(w1p)
    b1 = np.asarray(inp["L0_ffn_b1"])
    b1a = np.zeros((2, HIDP), np.float32)
    b1g = np.zeros((2, HIDP), np.float32)
    b1a[:, :HID] = b1[:, :HID]
    b1g[:, :HID] = b1[:, HID:]
    b1p = np.zeros((2, 128, 2 * W2C), np.float32)
    for c in range(W2C):
        b1p[:, :, 2 * c] = b1a[:, c * 128:(c + 1) * 128]
        b1p[:, :, 2 * c + 1] = b1g[:, c * 128:(c + 1) * 128]
    w["b1"] = np.ascontiguousarray(b1p.transpose(1, 0, 2).reshape(128, 2 * 2 * W2C))
    w2 = np.zeros((2, HIDP, DM), np.float32)
    w2[:, :HID, :] = np.asarray(inp["L0_ffn_w2"])
    w["w2"] = np.ascontiguousarray(w2)
    w["b2"] = np.ascontiguousarray(np.asarray(inp["L0_ffn_b2"]).reshape(2, 8, 128).transpose(2, 0, 1).reshape(128, 16))
    return w


def _consts():
    c = {}
    c["identf"] = np.eye(128, dtype=np.float32)
    c["identb"] = np.eye(128, dtype=BF)
    i = np.arange(128)[:, None]
    j = np.arange(128)[None, :]
    c["mcu"] = np.where(j > i, NEG, 0.0).astype(BF)
    c["mcl"] = np.where(j < i, NEG, 0.0).astype(BF)
    c["mcf"] = np.full((128, 128), NEG, BF)
    r = np.zeros((128, 128), np.float32)
    for o in (0, 64):
        for m in range(32):
            r[o + m + 32, o + m] = -1.0
            r[o + m, o + m + 32] = 1.0
    c["r32"] = r.astype(BF)
    c["onescol"] = np.ones((128, 1), BF)
    c["ones1b"] = np.ones((1, 128), BF)
    o64 = np.zeros((65, 64), BF)
    o64[64, :] = 1.0
    c["ones64"] = o64
    e2 = np.zeros((128, 2), BF)
    e2[0:64, 0] = 1.0
    e2[64:128, 1] = 1.0
    c["e2b"] = e2
    bo2 = np.zeros((2, 128), BF)
    bo2[0, 0:64] = 1.0
    bo2[1, 64:128] = 1.0
    c["bo2b"] = bo2
    half = np.arange(0, HD, 2, dtype=np.float64)
    inv_freq = 1.0 / 10000.0 ** (half / HD)
    t = np.arange(SEQ, dtype=np.float64)
    freqs = t[:, None] * inv_freq[None, :]
    freqs = np.concatenate([freqs, freqs], -1)
    base = (half + 0.4 * HD) / (1.4 * HD)
    power = (t - SEQ // 2) / (WIN // 2)
    scale = base[None, :] ** power[:, None]
    scale = np.concatenate([scale, scale], -1)
    cos, sin = np.cos(freqs), np.sin(freqs)
    tabs = np.stack([cos * scale, sin * scale, cos / scale, sin / scale])
    tt = np.zeros((4, 128, 4096), np.float64)
    for wdw in range(4):
        sl = tabs[:, wdw * 512:(wdw + 1) * 512, :].transpose(0, 2, 1)
        sl = np.tile(sl, (1, 2, 1))
        tt[:, :, wdw * 1024:wdw * 1024 + 512] = sl
        tt[:, :, wdw * 1024 + 512:wdw * 1024 + 1024] = sl
    c["tabs"] = tt.astype(BF)
    return c


CONSTS = _consts()


def build_program(stop=None):
    stop = stop or DBG_STOP
    nc = bass.Bass("TRN2", target_bir_lowering=False)
    p = Sched()

    xin = nc.dram_tensor("xin", [T, DM], F32, kind="ExternalInput").ap()
    ew = {}
    for name, shape, dt in [
        ("wqkvA", [DM, 3072], F32), ("woA", [DM, DM], F32),
        ("bqkvA", [128, 24], F32), ("boA", [128, 8], F32), ("ghost", [128, 16], F32),
        ("wqkvB", [2, DM, 1536], F32), ("woB", [2, INNER, DM], F32),
        ("qs", [128, 2], F32), ("ks", [128, 2], F32),
        ("lng", [128, 16], F32), ("lnb", [128, 16], F32),
        ("w1", [2, DM, 2 * HIDP], F32), ("b1", [128, 2 * 2 * W2C], F32),
        ("w2", [2, HIDP, DM], F32), ("b2", [128, 16], F32),
        ("identf", [128, 128], F32), ("identb", [128, 128], BF16),
        ("mcu", [128, 128], BF16), ("mcl", [128, 128], BF16), ("mcf", [128, 128], BF16),
        ("r32", [128, 128], BF16), ("onescol", [128, 1], BF16),
        ("ones1b", [1, 128], BF16), ("ones64", [65, 64], BF16),
        ("e2b", [128, 2], BF16), ("bo2b", [2, 128], BF16),
        ("tabs", [4, 128, 4096], BF16),
    ]:
        ew[name] = nc.dram_tensor(name, shape, dt, kind="ExternalInput").ap()
    out = nc.dram_tensor("out", [T, DM], F32, kind="ExternalOutput").ap()

    xa_d = nc.dram_tensor("xa_d", [8, 128, T], F32, kind="Internal").ap()
    yA_d = nc.dram_tensor("yA_d", [8, 128, YW], F32, kind="Internal").ap()
    yB_d = nc.dram_tensor("yB_d", [8, 128, YW], F32, kind="Internal").ap()
    wbA_q = nc.dram_tensor("wbA_q", [24, 128, 1024], BF16, kind="Internal").ap()
    wbA_o = nc.dram_tensor("wbA_o", [8, 128, 1024], BF16, kind="Internal").ap()
    wbB_q = nc.dram_tensor("wbB_q", [2, 12, 128, 1024], BF16, kind="Internal").ap()
    hg_d = nc.dram_tensor("hg_d", [W2C, 128, T], BF16, kind="Internal").ap()

    dbg = {}
    if stop != "full":
        dbg["xa"] = nc.dram_tensor("dbg_xa", [8, 128, T], F32, kind="ExternalOutput").ap()
        dbg["y"] = nc.dram_tensor("dbg_y", [8, 128, YW], F32, kind="ExternalOutput").ap()

    def finish(E, sems, ydump):
        p.barrier()
        for c in range(8):
            p.dma("sync", dbg["xa"][c], xa_d[c], sem="ld0", tag=f"dxa{c}")
            p.dma("sync", dbg["y"][c], ydump[c], sem="ld0", tag=f"dy{c}")
        p.op("sync", lambda e: e.nop(),
             wait=[(s, p.now(s)) for s in SEMS if s[0:2] in ("ld", "gp", "sc")], tag="fin")
        p.check()
        block = E(nc.Block())
        p.emit(block, sems)

    with contextlib.ExitStack() as ctx:
        E = ctx.enter_context
        sems = {s: E(nc.semaphore(s)) for s in SEMS}

        pr = Region(nc)
        identf = pr.sb("c_identf", [128, 128], F32)
        identb = pr.sb("c_identb", [128, 128], BF16)
        mcu = pr.sb("c_mcu", [128, 128], BF16)
        mcl = pr.sb("c_mcl", [128, 128], BF16)
        mcf = pr.sb("c_mcf", [128, 128], BF16)
        r32 = pr.sb("c_r32", [128, 128], BF16)
        onescol = pr.sb("c_onescol", [128, 1], BF16)
        ones1b = pr.sb("c_ones1b", [1, 128], BF16)
        ones64 = pr.sb("c_ones64", [65, 64], BF16)
        e2b = pr.sb("c_e2b", [128, 2], BF16)
        bo2b = pr.sb("c_bo2b", [2, 128], BF16)
        bqkvA = pr.sb("c_bqkvA", [128, 24], F32)
        boA = pr.sb("c_boA", [128, 8], F32)
        ghost = pr.sb("c_ghost", [128, 16], F32)
        qs_sb = pr.sb("c_qs", [128, 2], F32)
        ks_sb = pr.sb("c_ks", [128, 2], F32)
        lng_sb = pr.sb("c_lng", [128, 16], F32)
        lnb_sb = pr.sb("c_lnb", [128, 16], F32)
        b1_sb = pr.sb("c_b1", [128, 2 * 2 * W2C], F32)
        b2_sb = pr.sb("c_b2", [128, 16], F32)
        zcol = pr.sb("c_zcol", [128, 1], F32)
        epscol = pr.sb("c_eps", [128, 1], F32)

        pA = E(nc.psum_tensor("pA", [128, 1024], F32)).ap()
        pB = E(nc.psum_tensor("pB", [128, 1024], F32)).ap()
        pC = E(nc.psum_tensor("pC", [128, 512], F32)).ap()
        pD = E(nc.psum_tensor("pD", [128, 512], F32)).ap()
        pE = E(nc.psum_tensor("pE", [128, 512], F32)).ap()
        pFb = E(nc.psum_tensor("pFb", [128, 1024], BF16)).ap()
        pFf = pFb.bitcast(F32)

        for name, dst in [("identf", identf), ("identb", identb), ("mcu", mcu),
                          ("mcl", mcl), ("mcf", mcf), ("r32", r32),
                          ("onescol", onescol), ("ones1b", ones1b), ("ones64", ones64),
                          ("e2b", e2b), ("bo2b", bo2b), ("bqkvA", bqkvA),
                          ("boA", boA), ("ghost", ghost)]:
            p.dma("sync", dst, ew[name], sem="ld0", tag="const")
        p.dma("sync", qs_sb, ew["qs"], sem="ld0", tag="const")
        p.dma("sync", ks_sb, ew["ks"], sem="ld0", tag="const")
        p.dma("sync", lng_sb, ew["lng"], sem="ld0", tag="const")
        p.dma("sync", lnb_sb, ew["lnb"], sem="ld0", tag="const")
        p.dma("sync", b1_sb, ew["b1"], sem="ld0", tag="const")
        ld_consts = p.dma("sync", b2_sb, ew["b2"], sem="ld0", tag="const")

        zv = p.v(lambda e: e.memset(zcol, 0.0), wait=[ld_consts], inc=True, tag="zcol")
        p.v(lambda e: e.memset(epscol, EPS), inc=True, tag="eps")
        for yd in (yA_d, yB_d):
            for c in range(8):
                for s in range(SPC):
                    p.dma("sync", yd[c, :, s * (SEQ + 1):s * (SEQ + 1) + 1], zcol,
                          sem="ld1", wait=[("ve", zv)], tag="guard", slow=True)

        wreg = Region(nc)
        wst = [wreg.sb(f"wst{i}", [128, 3072], BF16) for i in range(2)]
        store_tok = []
        for idx, (src, dstb, nd) in enumerate(
            [(ew["wqkvA"], wbA_q, 24), (ew["woA"], wbA_o, 8),
             (ew["wqkvB"][0], wbB_q[0], 12), (ew["wqkvB"][1], wbB_q[1], 12)]):
            for k in range(8):
                n = idx * 8 + k
                buf = wst[n % 2]
                ncols = nd * 128
                w = [store_tok[-2]] if len(store_tok) >= 2 else []
                gpt = p.dma("gpsimd", buf[:, 0:ncols], src[k * 128:(k + 1) * 128, 0:ncols],
                            sem="gp0" if n % 2 == 0 else "gp1", wait=w, tag=f"wbc{idx}_{k}")
                store_tok.append(p.dma("sync",
                      dstb[:, :, k * 128:(k + 1) * 128].rearrange("d p j -> p d j"),
                      buf[:, 0:ncols].rearrange("p (d j) -> p d j", j=128),
                      sem="ld2" if n % 2 == 0 else "ld3",
                      wait=[gpt], tag=f"wbs{idx}_{k}"))
        wbank_tok = [("ld2", p.now("ld2")), ("ld3", p.now("ld3"))]
        wreg.close()

        if stop == "init":
            p.op("sync", lambda e: e.nop(),
                 wait=[(s, p.now(s)) for s in SEMS if s[0:2] in ("ld", "gp", "sc")], tag="fin")
            p.check()
            block = E(nc.Block())
            p.emit(block, sems)
            return nc
        if stop == "dump0":
            finish(E, sems, yA_d)
            return nc

        def emit_ln(ln, in_tiles, out_tiles, gb=None, apply_wait=(), tag="ln"):
            cps, sqs = [], []
            for c in range(8):
                cps.append(p.v(lambda e, c=c: e.tensor_copy(out=ln["xb"][c], in_=in_tiles[c]),
                               inc=True, tag=f"{tag}cp{c}"))
            for c in range(8):
                sqs.append(p.s(lambda e, c=c: e.activation(out=ln["sq"][c], in_=ln["xb"][c],
                                                           func=AF.Square),
                               wait=[("ve", cps[c])], inc=True, tag=f"{tag}sq{c}"))
            for c in range(8):
                p.t(lambda e, c=c: e.matmul(pA[0:1, 0:512], onescol, ln["xb"][c],
                                            start=(c == 0), stop=(c == 7)),
                    wait=[("ve", cps[c])], tag=f"{tag}mmA{c}")
            for c in range(8):
                p.t(lambda e, c=c: e.matmul(pA[0:1, 512:1024], onescol, ln["sq"][c],
                                            start=(c == 0), stop=(c == 7)),
                    wait=[("se", sqs[c])], inc=(c == 7), tag=f"{tag}mmB{c}")
            pe_stats = p.now("pe")
            v1 = p.v(lambda e: e.tensor_scalar_mul(out=ln["rU"], in0=pA[0:1, 0:512],
                                                   scalar1=1.0 / DM),
                     wait=[("pe", pe_stats)], inc=True, tag=f"{tag}mu")
            v2 = p.v(lambda e: e.tensor_scalar_mul(out=ln["rV"], in0=pA[0:1, 512:1024],
                                                   scalar1=1.0 / DM),
                     inc=True, tag=f"{tag}ms")
            v3 = p.v(lambda e: e.scalar_tensor_tensor(out=ln["rA"], in0=ln["rU"], scalar=-1.0,
                                                      in1=ln["rU"], op0=ALU.mult, op1=ALU.mult),
                     wait=[("ve", v1)], inc=True, tag=f"{tag}mu2")
            v4 = p.v(lambda e: e.tensor_add(out=ln["rV"], in0=ln["rV"], in1=ln["rA"]),
                     wait=[("ve", v3), ("ve", v2)], inc=True, tag=f"{tag}var")
            s0 = p.s(lambda e: e.activation(out=ln["rB"], in_=ln["rV"], func=AF.Ln,
                                            bias=epscol[0:1, :]),
                     wait=[("ve", v4)], inc=True, tag=f"{tag}lg")
            s1 = p.s(lambda e: e.activation(out=ln["rV"], in_=ln["rB"], func=AF.Exp,
                                            scale=-0.5),
                     wait=[("se", s0)], inc=True, tag=f"{tag}sd")
            v6 = p.v(lambda e: e.tensor_copy(out=ln["rM"], in_=ln["rV"]),
                     wait=[("se", s1)], inc=True, tag=f"{tag}rstdb")
            v7 = p.v(lambda e: e.tensor_mul(out=ln["rA"], in0=ln["rU"], in1=ln["rV"]),
                     wait=[("ve", v6)], inc=True, tag=f"{tag}mrs")
            v8 = p.v(lambda e: e.tensor_copy(out=ln["rN"], in_=ln["rA"]),
                     wait=[("ve", v7)], inc=True, tag=f"{tag}mrsb")
            p.t(lambda e: e.matmul(pB[:, 0:512], ones1b, ln["rM"], start=True, stop=True),
                wait=[("ve", v6)], tag=f"{tag}bcA")
            p.t(lambda e: e.matmul(pB[:, 512:1024], ones1b, ln["rN"], start=True, stop=True),
                wait=[("ve", v8)], inc=True, tag=f"{tag}bcB")
            pe_bc = p.now("pe")
            outs = []
            for c in range(8):
                w = [("pe", pe_bc)] + (list(apply_wait) if c == 0 else [])
                p.v(lambda e, c=c: e.tensor_mul(out=in_tiles[c], in0=in_tiles[c],
                                                in1=pB[:, 0:512]),
                    wait=w, tag=f"{tag}ap1_{c}")
                if gb is None:
                    outs.append(p.v(
                        lambda e, c=c: e.scalar_tensor_tensor(
                            out=out_tiles[c], in0=pB[:, 512:1024], scalar=-1.0,
                            in1=in_tiles[c], op0=ALU.mult, op1=ALU.add),
                        inc=True, tag=f"{tag}ap2_{c}"))
                else:
                    p.v(lambda e, c=c: e.scalar_tensor_tensor(
                        out=in_tiles[c], in0=pB[:, 512:1024], scalar=-1.0,
                        in1=in_tiles[c], op0=ALU.mult, op1=ALU.add),
                        tag=f"{tag}ap2_{c}")
                    outs.append(p.v(
                        lambda e, c=c: e.tensor_scalar(
                            out=out_tiles[c], in0=in_tiles[c], scalar1=gb[0][c],
                            scalar2=gb[1][c], op0=ALU.mult, op1=ALU.add),
                        inc=True, tag=f"{tag}ap3_{c}"))
            return outs

        def mk_ln_scratch(rg, pfx):
            return {
                "xb": [rg.sb(f"{pfx}xb{c}", [128, 512], BF16) for c in range(8)],
                "sq": [rg.sb(f"{pfx}sq{c}", [128, 512], BF16) for c in range(8)],
                "rU": rg.sb(f"{pfx}rU", [1, 512], F32),
                "rV": rg.sb(f"{pfx}rV", [1, 512], F32),
                "rA": rg.sb(f"{pfx}rA", [1, 512], F32),
                "rB": rg.sb(f"{pfx}rB", [1, 512], F32),
                "rM": rg.sb(f"{pfx}rM", [1, 512], BF16),
                "rN": rg.sb(f"{pfx}rN", [1, 512], BF16),
            }

        def _stageA(s):
            rh = Region(nc)
            QT = [rh.sb(f"QT{c}", [128, SEQ], BF16) for c in range(8)]
            KT = [rh.sb(f"KT{c}", [128, SEQ], BF16) for c in range(8)]
            Vaug = [rh.sb(f"Vg{tc}", [128, NH_A * 65], BF16) for tc in range(16)]

            r1 = Region(nc)
            hlnA = [r1.sb(f"hlA{c}", [128, SEQ], BF16) for c in range(8)]
            inst = [r1.sb(f"inst{i}", [128, 1024], F32) for i in range(4)]
            xf = [r1.sb(f"xfA{c}", [128, 512], F32) for c in range(8)]
            ln1 = mk_ln_scratch(r1, "lnA")
            p.barrier()
            prev_ln = None
            for tt in range(4):
                t0 = s * SEQ + tt * 512
                lds = []
                for i in range(4):
                    wv = [("pe", p.now("pe")), ("ve", p.now("ve"))] if tt > 0 else []
                    lds.append(p.dma("sync", inst[i], xin[t0 + i * 128:t0 + (i + 1) * 128, :],
                               sem="ld4", wait=wv if i == 0 else (), tag=f"A1ld{tt}{i}"))
                evs = []
                for c in range(8):
                    bank = pA if c % 2 == 0 else pB
                    w = [lds[3]]
                    if c >= 2:
                        w.append(("ve", evs[c - 2]))
                    elif prev_ln is not None:
                        w.append(("ve", prev_ln))
                    for i in range(4):
                        p.t(lambda e, c=c, i=i, bank=bank: e.transpose(
                            bank[:, i * 128:(i + 1) * 128],
                            inst[i][:, c * 128:(c + 1) * 128], identf),
                            wait=w if i == 0 else (), inc=(i == 3), tag=f"A1tr{tt}_{c}_{i}")
                    pe_tr = p.now("pe")
                    evs.append(p.v(lambda e, c=c, bank=bank: e.tensor_copy(
                        out=xf[c], in_=bank[:, 0:512]),
                        wait=[("pe", pe_tr)], inc=True, tag=f"A1ev{tt}{c}"))
                for c in range(8):
                    st_xf = p.dma("sync", xa_d[c, :, t0:t0 + 512], xf[c],
                                  sem="ld5", wait=[("ve", evs[c])], tag=f"A1st{tt}{c}")
                p.fence(("tensor",), {"ve": evs[-1]})
                lnouts = emit_ln(ln1, xf, [hlnA[c][:, tt * 512:(tt + 1) * 512] for c in range(8)],
                                 apply_wait=[st_xf], tag=f"A1n{s}{tt}")
                prev_ln = lnouts[-1]
            if stop == "A1":
                finish(E, sems, yA_d)
                return True

            r2 = Region(nc)
            wcol = [r2.sb(f"wcA{i}", [128, 1024], BF16) for i in range(2)]
            vstage = [r2.sb(f"vsA{i}", [128, 512], BF16) for i in range(4)]
            p.barrier()
            prev_mm_done = {}
            prev_ev = {}
            last_vaug = None
            for d in range(24):
                buf = wcol[d % 2]
                w = list(wbank_tok)
                if (d % 2) in prev_mm_done:
                    w.append(("pe", prev_mm_done[d % 2]))
                ld_w = p.dma("sync", buf, wbA_q[d], sem="ld2" if d % 2 == 0 else "ld3",
                             wait=w, tag=f"A2w{d}")
                for tt in range(4):
                    bank = [pA[:, 0:512], pA[:, 512:1024], pB[:, 0:512], pB[:, 512:1024]][tt]
                    wmm = [ld_w]
                    if tt in prev_ev:
                        wmm.append(("ve", prev_ev[tt]))
                    for k in range(8):
                        p.t(lambda e, tt=tt, k=k, bank=bank, buf=buf: e.matmul(
                            bank, buf[:, k * 128:(k + 1) * 128],
                            hlnA[k][:, tt * 512:(tt + 1) * 512],
                            start=(k == 0), stop=(k == 7)),
                            wait=wmm if k == 0 else (), inc=(k == 7), tag=f"A2m{d}_{tt}_{k}")
                    pe_mm = p.now("pe")
                    if d < 8:
                        dst = QT[d][:, tt * 512:(tt + 1) * 512]
                    elif d < 16:
                        dst = KT[d - 8][:, tt * 512:(tt + 1) * 512]
                    else:
                        dst = vstage[tt]
                    wev = [("pe", pe_mm)]
                    if d >= 16 and last_vaug is not None:
                        wev.append(("ve", last_vaug))
                    prev_ev[tt] = p.v(lambda e, dst=dst, bank=bank, d=d: e.tensor_scalar(
                        out=dst, in0=bank, scalar1=bqkvA[:, d:d + 1], scalar2=None,
                        op0=ALU.add),
                        wait=wev, inc=True, tag=f"A2e{d}_{tt}")
                prev_mm_done[d % 2] = p.now("pe")
                if d >= 16:
                    vc = d - 16
                    for tt in range(4):
                        for i in range(4):
                            tc = tt * 4 + i
                            wtr = [("ve", prev_ev[tt])]
                            if last_vaug is not None:
                                wtr.append(("ve", last_vaug))
                            p.t(lambda e, tt=tt, i=i: e.transpose(
                                pFb[:, i * 128:(i + 1) * 128],
                                vstage[tt][:, i * 128:(i + 1) * 128], identb),
                                wait=wtr, inc=True, tag=f"A2vt{d}_{tt}_{i}")
                            pe_t = p.now("pe")
                            last_vaug = p.v(
                                lambda e, tc=tc, vc=vc, i=i: e.tensor_copy(
                                    out=Vaug[tc].rearrange("p (h x) -> p h x", h=NH_A)
                                    [:, 2 * vc:2 * vc + 2, 0:64],
                                    in_=pFb[:, i * 128:(i + 1) * 128]
                                    .rearrange("p (h x) -> p h x", h=2)),
                                wait=[("pe", pe_t)], inc=True, tag=f"A2ve{d}_{tt}_{i}")
            ones_tok = None
            for tc in range(16):
                ones_tok = p.v(lambda e, tc=tc: e.memset(
                    Vaug[tc].rearrange("p (h x) -> p h x", h=NH_A)[:, :, 64:65], 1.0),
                    wait=[("ve", last_vaug)] if tc == 0 else (), inc=True, tag=f"A2on{tc}")
            r2.close()
            r1.close()
            if stop == "A2":
                finish(E, sems, yA_d)
                return True

            roA = Region(nc)
            o_all = [roA.sb(f"oal{c}", [128, SEQ], BF16) for c in range(8)]
            r3 = Region(nc)
            ztA = [r3.sb(f"ztA{i}", [128, 512], BF16) for i in range(16)]
            ktl2 = [r3.sb(f"ktlA{i}", [65, SEQ], BF16) for i in range(2)]
            qtl2 = [r3.sb(f"qtlA{i}", [65, SEQ], BF16) for i in range(2)]
            ost2 = [r3.sb(f"ostA{i}", [64, SEQ], BF16) for i in range(2)]
            mst2 = [r3.sb(f"mstA{i}", [128, 8], F32) for i in range(2)]
            m4b2 = [r3.sb(f"m4A{i}", [4, 128], BF16) for i in range(2)]
            zc65 = r3.sb("zc65", [65, 65], BF16)
            drow = r3.sb("drowA", [65, 512], F32)
            rrow = r3.sb("rrowA", [65, 512], F32)
            rrowb = r3.sb("rrbA", [65, 512], BF16)
            rbcs = r3.sb("rbcA", [64, 512], F32)
            p.barrier()
            zc_tok = p.v(lambda e: e.memset(zc65[:, :], 0.0), inc=True, tag="A3z0")
            k1a = p.v(lambda e: e.memset(ktl2[0][64:65, :], 1.0), inc=True, tag="A3k1")
            k1b = p.v(lambda e: e.memset(ktl2[1][64:65, :], 1.0), inc=True, tag="A3k2")
            qkv_done = p.now("ve")

            units = [(h, qb) for h in range(NH_A) for qb in range(4)]
            bk4 = [pA[:, 0:512], pA[:, 512:1024], pB[:, 0:512], pB[:, 512:1024]]
            gtoks = []
            U = {}
            fin_tok = None
            ost_dma = {0: None, 1: None}
            asm_tok = {}
            prev_mplumb = None
            prev_f4 = [None]
            prev_exp = [None]
            cur_mev = [None]
            m4_dma = {0: None, 1: None}

            def emit_asm(h):
                ch = h // 2
                hb = h % 2
                ktl, qtl = ktl2[hb], qtl2[hb]
                wasm = [("ve", qkv_done), ("pe", p.now("pe")), ("se", p.now("se"))]
                if h % 2 == 0:
                    p.s(lambda e, ch=ch, ktl=ktl: e.activation(
                        out=ktl[0:64, :], in_=KT[ch][0:64, :], func=AF.Copy),
                        wait=wasm, inc=True, tag=f"A3kc{h}")
                    qa = p.s(lambda e, ch=ch, qtl=qtl: e.activation(
                        out=qtl[0:64, :], in_=QT[ch][0:64, :], func=AF.Copy),
                        inc=True, tag=f"A3qc{h}")
                    asm_tok[h] = [("se", qa)]
                else:
                    p.dma("scalar", ktl[0:64, :], KT[ch][64:128, :], sem="sc0",
                          wait=wasm, tag=f"A3kd{h}")
                    asm_tok[h] = [p.dma("scalar", qtl[0:64, :], QT[ch][64:128, :],
                                        sem="sc0", tag=f"A3qd{h}")]

            def emit_p1(u):
                h, qb = units[u]
                ch = h // 2
                rbase = 64 * (h % 2)
                ktl, qtl = ktl2[h % 2], qtl2[h % 2]
                mst = mst2[u % 2]
                st = {}
                qk_red = {}
                for qt in range(4):
                    qtg = qb * 4 + qt
                    width = qb * 512 + (qt + 1) * 128
                    nmst = 0
                    pos = 0
                    red = None
                    while pos < width:
                        seg = min(512, width - pos)
                        pq = bk4[len(gtoks) % 4]
                        wseg = []
                        if len(gtoks) >= 4:
                            wseg = [("ve", gtoks[-4])]
                        final_seg = (pos + seg == width)
                        pieces = []
                        off = 0
                        while off < seg:
                            nn = min(512, seg - off)
                            if final_seg and off + nn == seg and nn > 128:
                                pieces.append((off, nn - 128, False))
                                pieces.append((off + nn - 128, 128, True))
                            elif final_seg and off + nn == seg:
                                pieces.append((off, nn, True))
                            else:
                                pieces.append((off, nn, False))
                            off += nn
                        first = True
                        for off, nn, msk in pieces:
                            last = (off + nn == seg) and not msk
                            p.t(lambda e, pq=pq, off=off, nn=nn, pos=pos, qtg=qtg,
                                ch=ch, rbase=rbase, msk=msk: e.matmul(
                                    pq[:, off:off + nn],
                                    QT[ch][rbase:rbase + 64, qtg * 128:(qtg + 1) * 128],
                                    KT[ch][rbase:rbase + 64, pos + off:pos + off + nn],
                                    start=True, stop=not msk),
                                wait=wseg if first else (), inc=last,
                                tag=f"A3s{h}_{qtg}_{pos+off}")
                            first = False
                            if msk:
                                p.t(lambda e, pq=pq, off=off, nn=nn: e.matmul(
                                    pq[:, off:off + nn], identb, mcu,
                                    start=False, stop=True),
                                    inc=True, tag=f"A3dm{h}_{qtg}")
                        pe_seg = p.now("pe")
                        single = (width <= 512)
                        red = p.v(lambda e, pq=pq, seg=seg, nmst=nmst, mst=mst, qt=qt,
                                  single=single: e.reduce_max(
                            out=mst[:, 4 + qt:5 + qt] if single else mst[:, nmst:nmst + 1],
                            in_=pq[:, 0:seg], axis=AX.X, negate=True),
                            wait=[("pe", pe_seg)], inc=True, tag=f"A3r{h}_{qtg}_{nmst}")
                        gtoks.append(red)
                        nmst += 1
                        pos += seg
                    if nmst == 1:
                        cmb = red
                    else:
                        cmb = p.v(lambda e, nmst=nmst, mst=mst, qt=qt: e.tensor_reduce(
                            out=mst[:, 4 + qt:5 + qt], in_=mst[:, 0:nmst],
                            axis=AX.X, op=ALU.min),
                            wait=[("ve", red)], inc=True, tag=f"A3c{h}_{qtg}")
                    qk_red[qt] = cmb
                st["qk_red"] = qk_red
                st["mst"] = mst
                U[u] = st

            def emit_p1b(u):
                nonlocal prev_mplumb
                h, qb = units[u]
                qtl = qtl2[h % 2]
                mst = U[u]["mst"]
                qk_red = U[u]["qk_red"]
                m4 = m4b2[u % 2]
                wtm = [("ve", qk_red[qt]) for qt in range(4)]
                if cur_mev[0] is not None:
                    wtm.append(("ve", cur_mev[0]))
                if prev_f4[0] is not None:
                    wtm.append(("ve", prev_f4[0]))
                ptm = p.t(lambda e, mst=mst: e.transpose(
                    pFf[0:4, 0:128], mst[:, 4:8], identf),
                    wait=wtm, inc=True, tag=f"A3t{h}_{qb}")
                wm4 = [("pe", ptm)]
                if m4_dma[u % 2] is not None:
                    wm4.append(m4_dma[u % 2])
                mev = p.v(lambda e, m4=m4: e.tensor_copy(out=m4, in_=pFf[0:4, 0:128]),
                          wait=wm4, inc=True, tag=f"A3m{h}_{qb}")
                cur_mev[0] = mev
                m_done = p.dma("gpsimd", qtl[64:65, qb * 512:(qb + 1) * 512], m4,
                               sem="gp0" if u % 2 == 0 else "gp1",
                               wait=[("ve", mev)], tag=f"A3mv{h}_{qb}")
                m4_dma[u % 2] = m_done
                prev_mplumb = m_done
                U[u]["m_done"] = m_done

            def emit_p2(u):
                nonlocal fin_tok, prev_mplumb
                h, qb = units[u]
                ch = h // 2
                rbase = 64 * (h % 2)
                ktl, qtl = ktl2[h % 2], qtl2[h % 2]
                ost = ost2[h % 2]
                m_done = U[u]["m_done"]
                prev_fin = fin_tok
                nkc = 4 * qb + 4
                exp_tok = {}
                for kc in range(nkc):
                    pz = pC if kc % 2 == 0 else pD
                    wz = [m_done] + asm_tok[h]
                    if kc == 0:
                        wz.append(("ve", qkv_done))
                    if kc >= 2:
                        wz.append(("se", exp_tok[kc - 2]))
                    elif prev_exp[0] is not None:
                        wz.append(("se", prev_exp[0]))
                    rel = kc - 4 * qb
                    first_z = True
                    q0 = max(0, rel + 1)
                    if q0 < 4:
                        p.t(lambda e, pz=pz, kc=kc, qb=qb, q0=q0, ktl=ktl, qtl=qtl: e.matmul(
                            pz[:, q0 * 128:512], ktl[:, kc * 128:(kc + 1) * 128],
                            qtl[:, qb * 512 + q0 * 128:(qb + 1) * 512],
                            start=True, stop=True),
                            wait=wz if first_z else (), inc=(rel < 0),
                            tag=f"A3z{h}_{qb}_{kc}")
                        first_z = False
                    if 0 <= rel < 4:
                        p.t(lambda e, pz=pz, kc=kc, qb=qb, rel=rel, ktl=ktl, qtl=qtl: e.matmul(
                            pz[:, rel * 128:(rel + 1) * 128],
                            ktl[:, kc * 128:(kc + 1) * 128],
                            qtl[:, qb * 512 + rel * 128:qb * 512 + (rel + 1) * 128],
                            start=True, stop=False),
                            wait=wz if first_z else (), tag=f"A3zd{h}_{qb}_{kc}")
                        p.t(lambda e, pz=pz, rel=rel: e.matmul(
                            pz[:, rel * 128:(rel + 1) * 128], identb, mcl,
                            start=False, stop=True),
                            inc=True, tag=f"A3xd{h}_{qb}_{kc}")
                    pe_z = p.now("pe")
                    col0 = max(0, rel) * 128
                    exp_tok[kc] = p.s(lambda e, pz=pz, kc=kc, col0=col0: e.activation(
                        out=ztA[kc][:, col0:512], in_=pz[:, col0:512],
                        func=AF.Exp, scale=0.125),
                        wait=[("pe", pe_z)], inc=True, tag=f"A3e{h}_{qb}_{kc}")
                prev_exp[0] = exp_tok[nkc - 1]
                for kc in range(nkc):
                    wav = [("se", exp_tok[kc])]
                    if kc == 0:
                        wav.append(("ve", ones_tok))
                        if fin_tok is not None:
                            wav.append(("ve", fin_tok))
                    col0 = max(0, kc - 4 * qb) * 128
                    p.t(lambda e, kc=kc, h=h, col0=col0: e.matmul(
                        pE[0:65, col0:512],
                        Vaug[kc][:, h * 65:(h + 1) * 65], ztA[kc][:, col0:512],
                        start=(kc == 0), stop=False),
                        wait=wav, tag=f"A3av{h}_{qb}_{kc}")
                p.t(lambda e: e.matmul(
                    pE[0:65, :], zc65[64:65, 0:65], ktl2[0][64:65, 0:512],
                    start=False, stop=True),
                    inc=True, tag=f"A3avz{h}_{qb}")
                pe_av = p.now("pe")
                f2a = p.s(lambda e, h=h: e.activation(
                    out=rrow[64:65, :], in_=pE[64:65, :], func=AF.Ln,
                    bias=ghost[64:65, h:h + 1]),
                    wait=[("pe", pe_av)] + ([ost_dma[h % 2]] if (qb == 0 and ost_dma[h % 2])
                                            else []),
                    inc=True, tag=f"A3lg{h}_{qb}")
                f2 = p.s(lambda e: e.activation(out=rrowb[64:65, :], in_=rrow[64:65, :],
                                                func=AF.Exp, scale=-1.0),
                         wait=[("se", f2a)], inc=True, tag=f"A3rc{h}_{qb}")
                wfb = [("se", f2)]
                if cur_mev[0] is not None:
                    wfb.append(("ve", cur_mev[0]))
                fb = p.t(lambda e: e.matmul(pFf[0:64, 0:512], ones64[64:65, :],
                                            rrowb[64:65, :], start=True, stop=True),
                         wait=wfb, inc=True, tag=f"A3bc{h}_{qb}")
                f4 = p.v(lambda e: e.tensor_copy(out=rbcs, in_=pFf[0:64, 0:512]),
                         wait=[("pe", fb)], inc=True, tag=f"A3be{h}_{qb}")
                prev_f4[0] = f4
                fin_tok = p.v(lambda e, qb=qb, ost=ost: e.tensor_mul(
                    out=ost[:, qb * 512:(qb + 1) * 512], in0=pE[0:64, :], in1=rbcs),
                    wait=[("ve", f4)], inc=True, tag=f"A3o{h}_{qb}")
                if qb == 3:
                    ost_dma[h % 2] = p.dma(
                        "scalar", o_all[ch][rbase:rbase + 64, :], ost,
                        sem="sc1" if h % 2 == 0 else "sc2",
                        wait=[("ve", fin_tok)], tag=f"A3om{h}")

            for u in range(len(units)):
                h, qb = units[u]
                if u == 0:
                    emit_asm(0)
                emit_p1(u)
                if u >= 1:
                    emit_p1b(u - 1)
                if u >= 2:
                    emit_p2(u - 2)
                if qb == 3 and h + 1 < NH_A:
                    emit_asm(h + 1)
            emit_p1b(len(units) - 1)
            emit_p2(len(units) - 2)
            emit_p2(len(units) - 1)
            oall_tok = ("sc1", p.now("sc1"))
            oall_tok2 = ("sc2", p.now("sc2"))
            r3.close()
            if stop == "A3":
                finish(E, sems, yA_d)
                return True

            r4 = Region(nc)
            woAb = [r4.sb(f"woAb{k}", [128, 1024], BF16) for k in range(8)]
            xft = r4.sb("xft", [128, 8 * 512], F32)
            y0t = r4.sb("y0t", [128, 8 * 512], F32)
            ln4 = mk_ln_scratch(r4, "lnY")
            p.barrier()
            for k in range(8):
                ld_wo = p.dma("sync", woAb[k], wbA_o[k], sem="ld4",
                              wait=wbank_tok if k == 0 else (), tag=f"A4w{k}")
            prev_lnout = None
            for tt in range(4):
                t0 = s * SEQ + tt * 512
                wx = [("ve", p.now("ve")), ("ld5", p.now("ld5"))] if tt > 0 else []
                ld_x = p.dma("sync", xft.rearrange("p (c j) -> p c j", j=512),
                             xa_d[:, :, t0:t0 + 512].rearrange("c p j -> p c j"),
                             sem="ld5", wait=wx, tag=f"A4x{tt}")
                ev4 = {}
                for d in range(8):
                    bank = [pA[:, 0:512], pA[:, 512:1024], pB[:, 0:512], pB[:, 512:1024]][d % 4]
                    wmm = [ld_wo, oall_tok, oall_tok2]
                    if prev_lnout is not None:
                        wmm.append(("ve", prev_lnout))
                    if d >= 4:
                        wmm.append(("ve", ev4[d - 4]))
                    for k in range(8):
                        p.t(lambda e, d=d, k=k, bank=bank, tt=tt: e.matmul(
                            bank, woAb[d][:, k * 128:(k + 1) * 128],
                            o_all[k][:, tt * 512:(tt + 1) * 512],
                            start=(k == 0), stop=(k == 7)),
                            wait=wmm if k == 0 else (), inc=(k == 7), tag=f"A4m{tt}_{d}_{k}")
                    pe_mm = p.now("pe")
                    ev4[d] = p.v(lambda e, d=d, bank=bank: e.scalar_tensor_tensor(
                        out=xft[:, d * 512:(d + 1) * 512], in0=bank,
                        scalar=boA[:, d:d + 1], in1=xft[:, d * 512:(d + 1) * 512],
                        op0=ALU.add, op1=ALU.add),
                        wait=[("pe", pe_mm), ld_x], inc=True, tag=f"A4e{tt}_{d}")
                st_x = p.dma("sync", xa_d[:, :, t0:t0 + 512].rearrange("c p j -> p c j"),
                             xft.rearrange("p (c j) -> p c j", j=512),
                             sem="ld5", wait=[("ve", ev4[7])], tag=f"A4sx{tt}")
                p.fence(("tensor",), {"ve": ev4[7]})
                ln_out = emit_ln(ln4, [xft[:, d * 512:(d + 1) * 512] for d in range(8)],
                                 [y0t[:, d * 512:(d + 1) * 512] for d in range(8)],
                                 apply_wait=[st_x], tag=f"A4n{s}{tt}")
                prev_lnout = ln_out[-1]
                p.dma("sync",
                      yA_d[:, :, s * (SEQ + 1) + 1 + tt * 512:s * (SEQ + 1) + 1 + tt * 512 + 512]
                      .rearrange("c p j -> p c j"),
                      y0t.rearrange("p (c j) -> p c j", j=512),
                      sem="ld5", wait=[("ve", ln_out[-1])], tag=f"A4sy{tt}")
            r4.close()
            roA.close()
            rh.close()
            return False

        for s in range(SPC):
            if _stageA(s):
                return nc

        if stop == "A":
            finish(E, sems, yA_d)
            return nc

        def _layer(l):
            p1src, p1dst = yA_d, yB_d
            p2src, p2dst = yB_d, yA_d

            rb = Region(nc)
            ytil = [rb.sb(f"ytB{s}", [128, 8 * 513], F32) for s in range(SPC)]
            hq = [rb.sb(f"hqB{c}", [128, 1024], BF16) for c in range(8)]
            qraw = [rb.sb(f"qrB{c}", [128, 1024], BF16) for c in range(4)]
            kraw = [rb.sb(f"krB{c}", [128, 1024], BF16) for c in range(4)]
            vraw = [rb.sb(f"vrB{c}", [128, 1024], BF16) for c in range(4)]
            kring = [[rb.sb(f"kgB{sl}_{c}", [128, 1024], BF16) for c in range(4)]
                     for sl in range(2)]
            vaugB = [[rb.sb(f"vgB{sl}_{i}", [128, NH_B * 65], BF16) for i in range(8)]
                     for sl in range(2)]
            tabst = [rb.sb(f"tbB{i}", [128, 1024], BF16) for i in range(4)]
            sqB = rb.sb("sqB", [128, 1024], BF16)
            rnr = rb.sb("rnr", [2, 1024], F32)
            rnr2 = rb.sb("rnr2", [2, 1024], F32)
            rnrb = rb.sb("rnrb", [2, 1024], BF16)
            scr1 = rb.sb("scr1", [128, 1024], BF16)
            scr2 = rb.sb("scr2", [128, 1024], BF16)
            ktlB2 = [rb.sb(f"ktlB{i}", [65, 1024], BF16) for i in range(2)]
            zcB = rb.sb("zcB", [65, 65], BF16)
            qtlB2 = [rb.sb(f"qtlB{i}", [65, 512], BF16) for i in range(2)]
            ztB = [rb.sb(f"ztB{i}", [128, 512], BF16) for i in range(8)]
            ostB2 = [rb.sb(f"ostB{i}", [64, NH_B * 512], BF16) for i in range(2)]
            ocat = rb.sb("ocat", [128, 4 * 1024], BF16)
            ynew = rb.sb("ynew", [128, 8 * 512], F32)
            mstB2 = [rb.sb(f"mstB{i}", [128, 8], F32) for i in range(2)]
            m4bB2 = [rb.sb(f"m4B{i}", [4, 128], BF16) for i in range(2)]
            drB = rb.sb("drB", [65, 512], F32)
            rrB = rb.sb("rrB", [65, 512], F32)
            rrbB = rb.sb("rrbB", [65, 512], BF16)
            rbcB = rb.sb("rbcB", [64, 512], F32)
            wcolB = [rb.sb(f"wcB{i}", [128, 1024], BF16) for i in range(2)]
            woBb = [rb.sb(f"woBb{k}", [128, 1024], BF16) for k in range(4)]
            p.barrier()
            p.v(lambda e: e.memset(zcB[:, :], 0.0), inc=True, tag=f"B{l}z0")
            p.v(lambda e: e.memset(ktlB2[0][64:65, :], 1.0), inc=True, tag=f"B{l}k1")
            konceB = p.v(lambda e: e.memset(ktlB2[1][64:65, :], 1.0), inc=True, tag=f"B{l}k2")
            ostB_dma = {0: None, 1: None}
            for k in range(4):
                ld_woB = p.dma("gpsimd", woBb[k], ew["woB"][l][k * 128:(k + 1) * 128, :],
                               sem="gp2", tag=f"B{l}wo{k}")

            fin_tokB = None
            mm_doneB = {}
            UB = {}
            asmB = {}
            prev_mpl = [None]
            prev_f4B = [None]
            prev_expB = [None]
            cur_mevB = [None]
            vt_hist = []
            m4_dmaB = {0: None, 1: None}
            ystore_c = [None]
            kq_cell = [None]
            for w in range(4):
                pe_all = p.now("pe")
                se_all = p.now("se")
                ve_all = p.now("ve")
                sc0T = ("sc0", p.now("sc0"))
                sc1T = ("sc1", p.now("sc1"))
                wtab = [("ve", kq_cell[0])] if kq_cell[0] is not None else []
                for i in range(4):
                    tab_ld = p.dma("gpsimd", tabst[i], ew["tabs"][i][:, w * 1024:(w + 1) * 1024],
                                   sem="gp2", wait=wtab if i == 0 else (), tag=f"B{l}t{w}{i}")
                for s in range(SPC):
                    ld_y = p.dma("sync", ytil[s].rearrange("p (c j) -> p c j", j=513),
                                 p1src[:, :, s * (SEQ + 1) + w * 512:s * (SEQ + 1) + w * 512 + 513]
                                 .rearrange("c p j -> p c j"), sem="ld5",
                                 wait=[("ve", ve_all), ("se", se_all)]
                                 if (w > 0 and s == 0) else (),
                                 tag=f"B{l}y{w}{s}")
                hq_done = None
                for c in range(8):
                    off = 0 if c >= 4 else 1
                    for s in range(SPC):
                        hq_done = p.s(lambda e, c=c, s=s, off=off: e.activation(
                            out=hq[c][:, s * 512:(s + 1) * 512],
                            in_=ytil[s][:, c * 513 + off:c * 513 + off + 512],
                            func=AF.Copy),
                            wait=[ld_y, ("pe", pe_all)] if (c == 0 and s == 0) else (),
                            inc=True, tag=f"B{l}h{w}{c}{s}")
                evB = {}
                for d in range(12):
                    buf = wcolB[d % 2]
                    wv = list(wbank_tok)
                    if (d % 2) in mm_doneB:
                        wv.append(("pe", mm_doneB[d % 2]))
                    ld_w = p.dma("sync", buf, wbB_q[l, d], sem="ld2" if d % 2 == 0 else "ld3",
                                 wait=wv, tag=f"B{l}w{w}{d}")
                    for s in range(SPC):
                        bank = [pA[:, 0:512], pA[:, 512:1024], pB[:, 0:512],
                                pB[:, 512:1024]][2 * (d % 2) + s]
                        wmm = [ld_w, ("se", hq_done)]
                        if (d - 2, s) in evB:
                            wmm.append(("ve", evB[(d - 2, s)]))
                        for k in range(8):
                            p.t(lambda e, k=k, bank=bank, buf=buf, s=s: e.matmul(
                                bank, buf[:, k * 128:(k + 1) * 128],
                                hq[k][:, s * 512:(s + 1) * 512],
                                start=(k == 0), stop=(k == 7)),
                                wait=wmm if k == 0 else (), inc=(k == 7),
                                tag=f"B{l}m{w}_{d}_{s}_{k}")
                        pe_mm = p.now("pe")
                        dst = (qraw[d] if d < 4 else kraw[d - 4] if d < 8 else vraw[d - 8])
                        wevac = [("pe", pe_mm)]
                        if d == 0 and s == 0 and w > 0:
                            wevac += [("se", se_all), sc0T, sc1T]
                        evB[(d, s)] = p.v(lambda e, dst=dst, bank=bank, s=s: e.tensor_copy(
                            out=dst[:, s * 512:(s + 1) * 512], in_=bank),
                            wait=wevac, inc=True, tag=f"B{l}e{w}_{d}_{s}")
                    mm_doneB[d % 2] = p.now("pe")
                qkv_doneB = p.now("ve")

                def rotnorm(raw, dst_tiles, sc_ap, tcs, tss, ev_base, prev, pfx,
                            extra0=()):
                    outs = []
                    for c in range(4):
                        wsq = [("ve", evB[(ev_base + c, 1)])]
                        if outs:
                            wsq.append(("ve", outs[-1]))
                        elif prev is not None:
                            wsq.append(("ve", prev))
                        sq_t = p.s(lambda e, c=c: e.activation(out=sqB, in_=raw[c],
                                                               func=AF.Square),
                                   wait=wsq, inc=True, tag=f"{pfx}sq{c}")
                        p.t(lambda e: e.matmul(pA[0:2, 0:512], e2b, sqB[:, 0:512],
                                               start=True, stop=True),
                            wait=[("se", sq_t), ("ve", qkv_doneB)], tag=f"{pfx}s1{c}")
                        p.t(lambda e: e.matmul(pA[0:2, 512:1024], e2b, sqB[:, 512:1024],
                                               start=True, stop=True),
                            inc=True, tag=f"{pfx}s2{c}")
                        pe_ss = p.now("pe")
                        r1t = p.v(lambda e: e.tensor_scalar_max(out=rnr, in0=pA[0:2, 0:1024],
                                                                scalar1=1e-24),
                                  wait=[("pe", pe_ss)], inc=True, tag=f"{pfx}cl{c}")
                        r2t = p.s(lambda e: e.activation(out=rnr2, in_=rnr, func=AF.Ln),
                                  wait=[("ve", r1t)], inc=True, tag=f"{pfx}sr{c}")
                        r3t = p.s(lambda e: e.activation(out=rnr, in_=rnr2, func=AF.Exp,
                                                         scale=-0.5),
                                  wait=[("se", r2t)], inc=True, tag=f"{pfx}rc{c}")
                        r4t = p.v(lambda e: e.tensor_copy(out=rnrb, in_=rnr),
                                  wait=[("se", r3t)], inc=True, tag=f"{pfx}rb{c}")
                        p.t(lambda e: e.matmul(pB[:, 0:512], bo2b, rnrb[:, 0:512],
                                               start=True, stop=True),
                            wait=[("ve", r4t)], tag=f"{pfx}b1{c}")
                        p.t(lambda e: e.matmul(pB[:, 512:1024], bo2b, rnrb[:, 512:1024],
                                               start=True, stop=True),
                            inc=True, tag=f"{pfx}b2{c}")
                        pe_bc = p.now("pe")
                        n1 = p.v(lambda e, c=c: e.scalar_tensor_tensor(
                            out=raw[c], in0=raw[c], scalar=sc_ap, in1=pB[:, 0:1024],
                            op0=ALU.mult, op1=ALU.mult),
                            wait=[("pe", pe_bc)] + (list(extra0) if c == 0 else []),
                            inc=True, tag=f"{pfx}nm{c}")
                        p.t(lambda e, c=c: e.matmul(pA[:, 0:512], r32, raw[c][:, 0:512],
                                                    start=True, stop=True),
                            wait=[("ve", n1)], tag=f"{pfx}r1{c}")
                        p.t(lambda e, c=c: e.matmul(pA[:, 512:1024], r32,
                                                    raw[c][:, 512:1024],
                                                    start=True, stop=True),
                            inc=True, tag=f"{pfx}r2{c}")
                        pe_rot = p.now("pe")
                        c1 = p.v(lambda e, c=c: e.tensor_mul(out=scr1, in0=raw[c], in1=tcs),
                                 wait=[tab_ld], inc=True, tag=f"{pfx}c1{c}")
                        c2 = p.v(lambda e: e.tensor_mul(out=scr2[:, 0:512], in0=pA[:, 0:512],
                                                        in1=tss[:, 0:512]),
                                 wait=[("pe", pe_rot)], inc=True, tag=f"{pfx}c2{c}")
                        c3 = p.v(lambda e: e.tensor_mul(out=scr2[:, 512:1024],
                                                        in0=pA[:, 512:1024],
                                                        in1=tss[:, 512:1024]),
                                 inc=True, tag=f"{pfx}c3{c}")
                        outs.append(p.v(lambda e, c=c: e.tensor_add(
                            out=dst_tiles[c], in0=scr1, in1=scr2),
                            wait=[("ve", c1), ("ve", c3)] +
                            (list(extra0) if c == 0 else []),
                            inc=True, tag=f"{pfx}fn{c}"))
                    return outs

                exq = [sc0T, sc1T] if w > 0 else []
                exk = ([("pe", pe_all), sc0T, sc1T] if w > 0 else [])
                q_out = rotnorm(qraw, qraw, qs_sb[:, l:l + 1], tabst[0], tabst[1],
                                0, None, f"B{l}q{w}_", extra0=exq)
                k_out = rotnorm(kraw, kring[w % 2], ks_sb[:, l:l + 1], tabst[2], tabst[3],
                                4, q_out[-1], f"B{l}K{w}_", extra0=exk)
                kq_cell[0] = k_out[-1]
                vt_tok = None
                for c in range(4):
                    for s in range(SPC):
                        for i in range(4):
                            wv = [("ve", evB[(8 + c, s)])]
                            if vt_hist:
                                wv.append(("ve", vt_hist[-1]))
                            if (c, s, i) == (0, 0, 0) and fin_tokB is not None:
                                wv.append(("ve", fin_tokB))
                            p.t(lambda e, c=c, s=s, i=i: e.transpose(
                                pFb[:, i * 128:(i + 1) * 128],
                                vraw[c][:, s * 512 + i * 128:s * 512 + (i + 1) * 128],
                                identb),
                                wait=wv, inc=True, tag=f"B{l}vt{w}_{c}{s}{i}")
                            pe_t = p.now("pe")
                            vt_tok = p.v(lambda e, c=c, s=s, i=i, w=w: e.tensor_copy(
                                out=vaugB[w % 2][s * 4 + i]
                                .rearrange("p (h x) -> p h x", h=NH_B)[:, 2 * c:2 * c + 2, 0:64],
                                in_=pFb[:, i * 128:(i + 1) * 128]
                                .rearrange("p (h x) -> p h x", h=2)),
                                wait=[("pe", pe_t)], inc=True, tag=f"B{l}ve{w}_{c}{s}{i}")
                            vt_hist.append(vt_tok)
                vones = None
                for i in range(8):
                    vones = p.v(lambda e, i=i, w=w: e.memset(
                        vaugB[w % 2][i].rearrange("p (h x) -> p h x", h=NH_B)[:, :, 64:65], 1.0),
                        wait=[("ve", vt_tok)] if i == 0 else (), inc=True, tag=f"B{l}vo{w}{i}")
                rot_done = p.now("ve")
                kq_done = {"q": q_out[-1], "k": k_out[-1]}

                def emit_asmB(u):
                    s, h = divmod(u, NH_B)
                    ch = h // 2
                    ktlB, qtlB = ktlB2[u % 2], qtlB2[u % 2]
                    wk = [("ve", rot_done), ("pe", p.now("pe"))]
                    if fin_tokB is not None:
                        wk.append(("ve", fin_tokB))
                    if h % 2 == 0:
                        if w > 0:
                            p.s(lambda e, ch=ch, s=s, w=w, ktlB=ktlB: e.activation(
                                out=ktlB[0:64, 0:512],
                                in_=kring[(w - 1) % 2][ch][0:64, s * 512:(s + 1) * 512],
                                func=AF.Copy),
                                wait=wk, inc=True, tag=f"B{l}ka{w}{s}{h}")
                        p.s(lambda e, ch=ch, s=s, w=w, ktlB=ktlB: e.activation(
                            out=ktlB[0:64, 512:1024],
                            in_=kring[w % 2][ch][0:64, s * 512:(s + 1) * 512],
                            func=AF.Copy),
                            wait=wk, inc=True, tag=f"B{l}kb{w}{s}{h}")
                        qa = p.s(lambda e, ch=ch, s=s, qtlB=qtlB: e.activation(
                            out=qtlB[0:64, :], in_=qraw[ch][0:64, s * 512:(s + 1) * 512],
                            func=AF.Copy),
                            inc=True, tag=f"B{l}qa{w}{s}{h}")
                        asmB[u] = [("se", qa)]
                    else:
                        sem = "sc0" if (u % 2 == 0) else "sc1"
                        if w > 0:
                            p.dma("scalar", ktlB[0:64, 0:512],
                                  kring[(w - 1) % 2][ch][64:128, s * 512:(s + 1) * 512],
                                  sem=sem, wait=wk, tag=f"B{l}ka{w}{s}{h}")
                        p.dma("scalar", ktlB[0:64, 512:1024],
                              kring[w % 2][ch][64:128, s * 512:(s + 1) * 512],
                              sem=sem, wait=wk, tag=f"B{l}kb{w}{s}{h}")
                        asmB[u] = [p.dma("scalar", qtlB[0:64, :],
                                         qraw[ch][64:128, s * 512:(s + 1) * 512],
                                         sem=sem, tag=f"B{l}qa{w}{s}{h}")]

                def emit_p1B(u):
                    s, h = divmod(u, NH_B)
                    ch = h // 2
                    rbase = 64 * (h % 2)
                    mstB = mstB2[u % 2]
                    qk_redB = {}
                    prev_qk = UB[u - 1]["qk"] if u > 0 else None
                    for qt in range(4):
                        pq = pA if qt % 2 == 0 else pB
                        width = 640 if w > 0 else qt * 128 + 128
                        wseg = [("ve", rot_done)]
                        if qt >= 2:
                            wseg.append(("ve", qk_redB[qt - 2]))
                        elif prev_qk is not None:
                            wseg.append(("ve", prev_qk[qt + 2]))
                        segs = []
                        if w > 0:
                            segs.append((kring[(w - 1) % 2][ch], qt * 128, 0, 512 - qt * 128))
                            segs.append((kring[w % 2][ch], 0, 512 - qt * 128, qt * 128 + 128))
                        else:
                            segs.append((kring[0][ch], 0, 0, qt * 128 + 128))
                        cuts = {0, width, width - 128, 512}
                        if w > 0:
                            cuts.add(128)
                        mm_list = []
                        for (src, so, po, n) in segs:
                            bnds = sorted(c for c in cuts if po < c < po + n)
                            prev = po
                            for c in bnds + [po + n]:
                                mm_list.append((src, so + prev - po, prev, c - prev))
                                prev = c
                        masked0 = (0, 128) if w > 0 else None
                        masked1 = (width - 128, width)
                        plain = [t_ for t_ in mm_list
                                 if not ((masked0 and t_[2] == masked0[0]) or
                                         t_[2] == masked1[0])]
                        pairs = [t_ for t_ in mm_list if t_ not in plain]
                        first = True
                        for (src, so, po, nn) in plain:
                            p.t(lambda e, pq=pq, src=src, so=so, po=po, nn=nn, qt=qt,
                                s=s, ch=ch, rbase=rbase: e.matmul(
                                    pq[:, po:po + nn],
                                    qraw[ch][rbase:rbase + 64,
                                             s * 512 + qt * 128:s * 512 + (qt + 1) * 128],
                                    src[rbase:rbase + 64, s * 512 + so:s * 512 + so + nn],
                                    start=True, stop=True),
                                wait=wseg if first else (), tag=f"B{l}s{w}_{s}{h}{qt}_{po}")
                            first = False
                        for (src, so, po, nn) in pairs:
                            mask = mcu if (masked0 and po == masked0[0]) else mcl
                            p.t(lambda e, pq=pq, src=src, so=so, po=po, nn=nn, qt=qt,
                                s=s, ch=ch, rbase=rbase: e.matmul(
                                    pq[:, po:po + nn],
                                    qraw[ch][rbase:rbase + 64,
                                             s * 512 + qt * 128:s * 512 + (qt + 1) * 128],
                                    src[rbase:rbase + 64, s * 512 + so:s * 512 + so + nn],
                                    start=True, stop=False),
                                wait=wseg if first else (), tag=f"B{l}sm{w}_{s}{h}{qt}_{po}")
                            first = False
                            p.t(lambda e, pq=pq, po=po, mask=mask: e.matmul(
                                pq[:, po:po + 128], identb, mask,
                                start=False, stop=True),
                                inc=(po == masked1[0]), tag=f"B{l}dm{w}_{s}{h}{qt}_{po}")
                        pe_seg = p.now("pe")
                        red = p.v(lambda e, pq=pq, width=width, qt=qt, mstB=mstB: e.reduce_max(
                            out=mstB[:, qt:qt + 1], in_=pq[:, 0:width],
                            axis=AX.X, negate=True),
                            wait=[("pe", pe_seg)], inc=True, tag=f"B{l}r{w}_{s}{h}{qt}")
                        qk_redB[qt] = red
                    UB[u] = {"qk": qk_redB, "mst": mstB}

                def emit_p1bB(u):
                    s, h = divmod(u, NH_B)
                    qtlB = qtlB2[u % 2]
                    mstB = UB[u]["mst"]
                    qk_redB = UB[u]["qk"]
                    m4 = m4bB2[u % 2]
                    wtm = [("ve", qk_redB[qt]) for qt in range(4)]
                    if cur_mevB[0] is not None:
                        wtm.append(("ve", cur_mevB[0]))
                    if prev_f4B[0] is not None:
                        wtm.append(("ve", prev_f4B[0]))
                    ptm = p.t(lambda e, mstB=mstB: e.transpose(
                        pFf[0:4, 0:128], mstB[:, 0:4], identf),
                        wait=wtm, inc=True, tag=f"B{l}tm{w}_{s}{h}")
                    wm4 = [("pe", ptm)]
                    if m4_dmaB[u % 2] is not None:
                        wm4.append(m4_dmaB[u % 2])
                    mev = p.v(lambda e, m4=m4: e.tensor_copy(out=m4, in_=pFf[0:4, 0:128]),
                              wait=wm4, inc=True, tag=f"B{l}mm{w}_{s}{h}")
                    cur_mevB[0] = mev
                    m_doneB = p.dma("gpsimd", qtlB[64:65, 0:512], m4,
                                    sem="gp0" if u % 2 == 0 else "gp1",
                                    wait=[("ve", mev)], tag=f"B{l}mv{w}_{s}{h}")
                    m4_dmaB[u % 2] = m_doneB
                    prev_mpl[0] = m_doneB
                    UB[u]["m"] = m_doneB

                def emit_p2B(u):
                    nonlocal fin_tokB
                    s, h = divmod(u, NH_B)
                    ch = h // 2
                    rbase = 64 * (h % 2)
                    ktlB, qtlB = ktlB2[u % 2], qtlB2[u % 2]
                    ostB = ostB2[s % 2]
                    m_doneB = UB[u]["m"]
                    prev_fin = fin_tokB
                    kc0 = 0 if w > 0 else 4
                    exp_tokB = {}
                    for kc in range(kc0, 8):
                        pz = pC if kc % 2 == 0 else pD
                        wz = [m_doneB] + asmB[u]
                        if kc == kc0:
                            wz.append(("ve", rot_done))
                        if kc - kc0 >= 2:
                            wz.append(("se", exp_tokB[kc - 2]))
                        elif prev_expB[0] is not None:
                            wz.append(("se", prev_expB[0]))
                        kinds = []
                        for qt in range(4):
                            if kc < qt or kc > qt + 4:
                                kinds.append("fill")
                            elif kc == qt and w > 0:
                                kinds.append("mcu")
                            elif kc == qt + 4:
                                kinds.append("mcl")
                            else:
                                kinds.append("clean")
                        first_z = True
                        q0 = 0
                        while q0 < 4:
                            if kinds[q0] != "clean":
                                q0 += 1
                                continue
                            q1 = q0
                            while q1 + 1 < 4 and kinds[q1 + 1] == "clean":
                                q1 += 1
                            p.t(lambda e, pz=pz, kc=kc, q0=q0, q1=q1, ktlB=ktlB,
                                qtlB=qtlB: e.matmul(
                                    pz[:, q0 * 128:(q1 + 1) * 128],
                                    ktlB[:, kc * 128:(kc + 1) * 128],
                                    qtlB[:, q0 * 128:(q1 + 1) * 128],
                                    start=True, stop=True),
                                wait=wz if first_z else (), tag=f"B{l}z{w}_{s}{h}{kc}{q0}")
                            first_z = False
                            q0 = q1 + 1
                        ndiag = sum(1 for k_ in kinds if k_ in ("mcu", "mcl"))
                        di = 0
                        for qt in range(4):
                            if kinds[qt] not in ("mcu", "mcl"):
                                continue
                            mask = mcu if kinds[qt] == "mcu" else mcl
                            p.t(lambda e, pz=pz, kc=kc, qt=qt, ktlB=ktlB, qtlB=qtlB: e.matmul(
                                pz[:, qt * 128:(qt + 1) * 128],
                                ktlB[:, kc * 128:(kc + 1) * 128],
                                qtlB[:, qt * 128:(qt + 1) * 128],
                                start=True, stop=False),
                                wait=wz if first_z else (), tag=f"B{l}zd{w}_{s}{h}{kc}{qt}")
                            first_z = False
                            di += 1
                            p.t(lambda e, pz=pz, qt=qt, mask=mask: e.matmul(
                                pz[:, qt * 128:(qt + 1) * 128], identb, mask,
                                start=False, stop=True),
                                inc=(di == ndiag), tag=f"B{l}fd{w}_{s}{h}{kc}{qt}")
                        pe_z = p.now("pe")
                        col0 = max(0, kc - 4) * 128
                        col1 = min(4, kc + 1) * 128
                        exp_tokB[kc] = p.s(lambda e, pz=pz, kc=kc, col0=col0,
                                           col1=col1: e.activation(
                            out=ztB[kc][:, col0:col1], in_=pz[:, col0:col1],
                            func=AF.Exp, scale=8.0),
                            wait=[("pe", pe_z)], inc=True, tag=f"B{l}x{w}_{s}{h}{kc}")
                    prev_expB[0] = exp_tokB[7]
                    if w > 0:
                        wop = [("ve", vones)]
                        if prev_fin is not None:
                            wop.append(("ve", prev_fin))
                        p.t(lambda e: e.matmul(
                            pE[0:65, :], zcB[64:65, 0:65], ktlB2[0][64:65, 0:512],
                            start=True, stop=False),
                            wait=wop, tag=f"B{l}ao{w}_{s}{h}")
                    for kc in range(kc0, 8):
                        slot = (w - 1) % 2 if kc < 4 else w % 2
                        tile = s * 4 + (kc if kc < 4 else kc - 4)
                        wav = [("se", exp_tokB[kc])]
                        if kc == kc0 and w == 0:
                            wav.append(("ve", vones))
                            if prev_fin is not None:
                                wav.append(("ve", prev_fin))
                        col0 = max(0, kc - 4) * 128
                        col1 = min(4, kc + 1) * 128
                        p.t(lambda e, kc=kc, slot=slot, tile=tile, h=h, kc0=kc0,
                            col0=col0, col1=col1, w=w: e.matmul(
                            pE[0:65, col0:col1],
                            vaugB[slot][tile][:, h * 65:(h + 1) * 65],
                            ztB[kc][:, col0:col1],
                            start=(kc == kc0 and w == 0), stop=False),
                            wait=wav, tag=f"B{l}a{w}_{s}{h}{kc}")
                    p.t(lambda e: e.matmul(
                        pE[0:65, :], zcB[64:65, 0:65], ktlB2[0][64:65, 0:512],
                        start=False, stop=True),
                        inc=True, tag=f"B{l}az{w}_{s}{h}")
                    pe_av = p.now("pe")
                    wf1 = [("pe", pe_av)]
                    if h == 0 and ostB_dma[s % 2] is not None:
                        wf1.append(ostB_dma[s % 2])
                    f2a = p.s(lambda e: e.activation(out=rrB[64:65, :], in_=pE[64:65, :],
                                                     func=AF.Ln),
                              wait=wf1, inc=True, tag=f"B{l}lg{w}_{s}{h}")
                    f2 = p.s(lambda e: e.activation(out=rrbB[64:65, :], in_=rrB[64:65, :],
                                                    func=AF.Exp, scale=-1.0),
                             wait=[("se", f2a)], inc=True, tag=f"B{l}rc{w}_{s}{h}")
                    wfb = [("se", f2)]
                    if cur_mevB[0] is not None:
                        wfb.append(("ve", cur_mevB[0]))
                    fb = p.t(lambda e: e.matmul(pFf[0:64, 0:512], ones64[64:65, :],
                                                rrbB[64:65, :], start=True, stop=True),
                             wait=wfb, inc=True, tag=f"B{l}bc{w}_{s}{h}")
                    f4 = p.v(lambda e: e.tensor_copy(out=rbcB, in_=pFf[0:64, 0:512]),
                             wait=[("pe", fb)], inc=True, tag=f"B{l}be{w}_{s}{h}")
                    prev_f4B[0] = f4
                    fin_tokB = p.v(lambda e, h=h, ostB=ostB: e.tensor_mul(
                        out=ostB[:, h * 512:(h + 1) * 512], in0=pE[0:64, :], in1=rbcB),
                        wait=[("ve", f4)], inc=True, tag=f"B{l}o{w}_{s}{h}")
                    if h == NH_B - 1:
                        sem = "sc2" if s % 2 == 0 else "sc3"
                        p.dma("scalar",
                              ocat[0:64, :].rearrange("p (c j) -> p c j", j=1024)
                              [:, :, s * 512:(s + 1) * 512],
                              ostB.rearrange("p (c x j) -> p c x j", c=4, x=2)[:, :, 0, :],
                              sem=sem, wait=[("ve", fin_tokB)], tag=f"B{l}oe{w}{s}")
                        ostB_dma[s % 2] = p.dma("scalar",
                              ocat[64:128, :].rearrange("p (c j) -> p c j", j=1024)
                              [:, :, s * 512:(s + 1) * 512],
                              ostB.rearrange("p (c x j) -> p c x j", c=4, x=2)[:, :, 1, :],
                              sem=sem, tag=f"B{l}oo{w}{s}")

                nuB = SPC * NH_B
                for u in range(nuB):
                    emit_p1B(u)
                    if u >= 1:
                        emit_p1bB(u - 1)
                    if u >= 2:
                        emit_p2B(u - 2)
                    emit_asmB(u)
                emit_p1bB(nuB - 1)
                emit_p2B(nuB - 2)
                emit_p2B(nuB - 1)
                ocat_tok = [("sc2", p.now("sc2")), ("sc3", p.now("sc3"))]

                ev_wo = {}
                ystore = ystore_c[0]
                for s in range(SPC):
                    for d in range(8):
                        key = d % 4
                        bank = [pA[:, 0:512], pA[:, 512:1024], pB[:, 0:512],
                                pB[:, 512:1024]][key]
                        wmm = [ld_woB] + ocat_tok
                        if key in ev_wo:
                            wmm.append(("ve", ev_wo[key]))
                        for k in range(4):
                            p.t(lambda e, k=k, d=d, s=s, bank=bank: e.matmul(
                                bank, woBb[k][:, d * 128:(d + 1) * 128],
                                ocat[:, k * 1024 + s * 512:k * 1024 + (s + 1) * 512],
                                start=(k == 0), stop=(k == 3)),
                                wait=wmm if k == 0 else (), inc=(k == 3),
                                tag=f"B{l}wm{w}_{s}{d}_{k}")
                        pe_mm = p.now("pe")
                        wev = [("pe", pe_mm)]
                        if ystore is not None and d < 4:
                            wev.append(ystore)
                        ev_wo[key] = p.v(lambda e, d=d, s=s, bank=bank: e.tensor_add(
                            out=ynew[:, d * 512:(d + 1) * 512],
                            in0=bank, in1=ytil[s][:, d * 513 + 1:d * 513 + 513]),
                            wait=wev, inc=True, tag=f"B{l}we{w}_{s}{d}")
                    ynew_tok = p.now("ve")
                    ystore = p.dma("sync",
                          p1dst[:, :, s * (SEQ + 1) + 1 + w * 512:s * (SEQ + 1) + 1 + w * 512 + 512]
                          .rearrange("c p j -> p c j"),
                          ynew.rearrange("p (c j) -> p c j", j=512),
                          sem="ld7", wait=[("ve", ynew_tok)], tag=f"B{l}ys{w}{s}")
                ystore_c[0] = ystore
            rb.close()

            if stop == f"p1_{l}":
                finish(E, sems, p1dst)
                return True

            r2a = Region(nc)
            hlnB = [r2a.sb(f"hlB{c}", [128, T], BF16) for c in range(8)]
            ytilB = r2a.sb("ytilB", [128, 8 * 513], F32)
            ln2 = mk_ln_scratch(r2a, "lnB")
            wcol2 = [r2a.sb(f"w1c{i}", [128, 8 * 256], BF16) for i in range(2)]
            g2st = r2a.sb("g2st", [128, 512], BF16)
            hgst = r2a.sb("hgst", [128, T], BF16)
            p.barrier()
            prev_apply = None
            for s in range(SPC):
                for tt in range(4):
                    wv = [("ve", prev_apply)] if prev_apply is not None else []
                    ld_y = p.dma("sync", ytilB.rearrange("p (c j) -> p c j", j=513),
                          p2src[:, :, s * (SEQ + 1) + tt * 512:s * (SEQ + 1) + tt * 512 + 513]
                          .rearrange("c p j -> p c j"), sem="ld5", wait=wv, tag=f"C{l}y{s}{tt}")
                    ints = []
                    for c in range(8):
                        off = 0 if c >= 4 else 1
                        ints.append(ytilB[:, c * 513 + off:c * 513 + off + 512])
                    p.fence(("vector", "scalar", "tensor"), {ld_y[0]: ld_y[1]})
                    louts = emit_ln(ln2, ints,
                                    [hlnB[c][:, (s * 4 + tt) * 512:(s * 4 + tt + 1) * 512]
                                     for c in range(8)],
                                    gb=([lng_sb[:, 8 * l + c:8 * l + c + 1] for c in range(8)],
                                        [lnb_sb[:, 8 * l + c:8 * l + c + 1] for c in range(8)]),
                                    tag=f"C{l}n{s}{tt}")
                    prev_apply = louts[-1]
            hln_done = p.now("ve")
            w1_mm_done = {}
            prev_hg = None
            prev_st = None
            for c2 in range(W2C):
                buf = wcol2[c2 % 2]
                wv = []
                if (c2 % 2) in w1_mm_done:
                    wv.append(("pe", w1_mm_done[c2 % 2]))
                ld_w = p.dma("gpsimd", buf.rearrange("p (k j) -> p k j", j=256),
                      ew["w1"][l][:, c2 * 256:(c2 + 1) * 256]
                      .rearrange("(k p) j -> p k j", p=128),
                      sem="gp0" if c2 % 2 == 0 else "gp1", wait=wv, tag=f"C{l}w{c2}")
                hg_tok = {}
                for tt8 in range(8):
                    bka = pA[:, 0:512] if tt8 % 2 == 0 else pB[:, 0:512]
                    bkg = pA[:, 512:1024] if tt8 % 2 == 0 else pB[:, 512:1024]
                    wmm = [ld_w, ("ve", hln_done)]
                    if tt8 >= 2:
                        wmm.append(("ve", hg_tok[tt8 - 2]))
                    elif prev_hg is not None:
                        wmm.append(("ve", prev_hg[6 + tt8]))
                    for k in range(8):
                        p.t(lambda e, k=k, buf=buf, tt8=tt8, bka=bka: e.matmul(
                            bka, buf[:, k * 256:k * 256 + 128],
                            hlnB[k][:, tt8 * 512:(tt8 + 1) * 512],
                            start=(k == 0), stop=(k == 7)),
                            wait=wmm if k == 0 else (), tag=f"C{l}a{c2}_{tt8}_{k}")
                    for k in range(8):
                        p.t(lambda e, k=k, buf=buf, tt8=tt8, bkg=bkg: e.matmul(
                            bkg, buf[:, k * 256 + 128:k * 256 + 256],
                            hlnB[k][:, tt8 * 512:(tt8 + 1) * 512],
                            start=(k == 0), stop=(k == 7)),
                            inc=(k == 7), tag=f"C{l}g{c2}_{tt8}_{k}")
                    pe_mm = p.now("pe")
                    wsg = [("pe", pe_mm)]
                    if tt8 >= 1:
                        wsg.append(("ve", hg_tok[tt8 - 1]))
                    elif prev_hg is not None:
                        wsg.append(("ve", prev_hg[7]))
                    sg = p.s(lambda e, bkg=bkg, c2=c2: e.activation(
                        out=g2st, in_=bkg, func=AF.Gelu,
                        bias=b1_sb[:, 2 * W2C * l + 2 * c2 + 1:2 * W2C * l + 2 * c2 + 2]),
                        wait=wsg, inc=True, tag=f"C{l}gg{c2}_{tt8}")
                    whg = [("se", sg)]
                    if tt8 < 2 and c2 > 0 and prev_st is not None:
                        whg.append(prev_st)
                    hg_tok[tt8] = p.v(lambda e, bka=bka, c2=c2, tt8=tt8: e.scalar_tensor_tensor(
                        out=hgst[:, tt8 * 512:(tt8 + 1) * 512], in0=bka,
                        scalar=b1_sb[:, 2 * W2C * l + 2 * c2:2 * W2C * l + 2 * c2 + 1],
                        in1=g2st, op0=ALU.add, op1=ALU.mult),
                        wait=whg, inc=True, tag=f"C{l}hg{c2}_{tt8}")
                w1_mm_done[c2 % 2] = p.now("pe")
                prev_hg = hg_tok
                prev_st = p.dma("sync", hg_d[c2], hgst, sem="ld7",
                                wait=[("ve", hg_tok[7])], tag=f"C{l}hs{c2}")
            hg_st_tok = prev_st
            r2a.close()

            r2b = Region(nc)
            w2b = [r2b.sb(f"w2b{k}", [128, 1024], BF16) for k in range(W2C)]
            hgt2 = [r2b.sb(f"hgt{i}", [128, W2C * 512], BF16) for i in range(2)]
            ytil22 = [r2b.sb(f"ytil2{i}", [128, 8 * 513], F32) for i in range(2)]
            ynew2 = r2b.sb("ynew2", [128, 8 * 512], F32)
            xat = r2b.sb("xat", [128, 8 * 512], F32)
            outst = r2b.sb("outst", [128, 1024], F32)
            p.barrier()
            for k in range(W2C):
                ld_w2 = p.dma("gpsimd", w2b[k], ew["w2"][l][k * 128:(k + 1) * 128, :],
                              sem="gp2", tag=f"D{l}w{k}")
            out_dma = None
            mm_pe_tok = {}
            ev7_tok = {}
            fadd7_tok = {}
            for tt8 in range(8):
                s, tt = tt8 // 4, tt8 % 4
                t0 = tt8 * 512
                hgt = hgt2[tt8 % 2]
                ytil2 = ytil22[tt8 % 2]
                wv = [hg_st_tok]
                if tt8 >= 2:
                    wv.append(("pe", mm_pe_tok[tt8 - 2]))
                ld_hgt = p.dma("sync", hgt.rearrange("p (c j) -> p c j", j=512),
                        hg_d[:, :, t0:t0 + 512].rearrange("c p j -> p c j"),
                        sem="ld4", wait=wv, tag=f"D{l}h{tt8}")
                wy = [("ve", ev7_tok[tt8 - 2])] if tt8 >= 2 else []
                ld_y2 = p.dma("sync", ytil2.rearrange("p (c j) -> p c j", j=513),
                            p2src[:, :, s * (SEQ + 1) + tt * 512:s * (SEQ + 1) + tt * 512 + 513]
                            .rearrange("c p j -> p c j"), sem="ld5", wait=wy,
                            tag=f"D{l}y{tt8}")
                ld_x2 = None
                if l == 1:
                    wx = [("ve", fadd7_tok[tt8 - 1])] if tt8 >= 1 else []
                    ld_x2 = p.dma("sync", xat.rearrange("p (c j) -> p c j", j=512),
                                xa_d[:, :, t0:t0 + 512].rearrange("c p j -> p c j"),
                                sem="ld6", wait=wx, tag=f"D{l}x{tt8}")
                ev2 = {}
                for d in range(8):
                    bank = [pA[:, 0:512], pA[:, 512:1024], pB[:, 0:512], pB[:, 512:1024]][d % 4]
                    wmm = [ld_w2, ld_hgt]
                    if d >= 4:
                        wmm.append(("ve", ev2[d - 4]))
                    for k in range(W2C):
                        p.t(lambda e, k=k, d=d, bank=bank, hgt=hgt: e.matmul(
                            bank, w2b[k][:, d * 128:(d + 1) * 128],
                            hgt[:, k * 512:(k + 1) * 512],
                            start=(k == 0), stop=(k == W2C - 1)),
                            wait=wmm if k == 0 else (), inc=(k == W2C - 1),
                            tag=f"D{l}m{tt8}_{d}_{k}")
                    pe_mm = p.now("pe")
                    wev2 = [("pe", pe_mm)]
                    if d == 0:
                        wev2.append(ld_y2)
                        if tt8 > 0:
                            wev2.append(("ld7", p.now("ld7")))
                    ev2[d] = p.v(lambda e, d=d, bank=bank, l=l, ytil2=ytil2:
                                 e.scalar_tensor_tensor(
                        out=ynew2[:, d * 512:(d + 1) * 512], in0=bank,
                        scalar=b2_sb[:, 8 * l + d:8 * l + d + 1],
                        in1=ytil2[:, d * 513 + 1:d * 513 + 513],
                        op0=ALU.add, op1=ALU.add),
                        wait=wev2, inc=True, tag=f"D{l}e{tt8}_{d}")
                mm_pe_tok[tt8] = p.now("pe")
                ev7_tok[tt8] = ev2[7]
                if l == 0:
                    p.dma("sync",
                          p2dst[:, :, s * (SEQ + 1) + 1 + tt * 512:s * (SEQ + 1) + 1 + tt * 512 + 512]
                          .rearrange("c p j -> p c j"),
                          ynew2.rearrange("p (c j) -> p c j", j=512),
                          sem="ld7", wait=[("ve", ev2[7])], tag=f"D{l}s{tt8}")
                else:
                    fadd = {}
                    for d in range(8):
                        wfa = [("ve", ev2[d])]
                        if d == 0:
                            wfa.append(ld_x2)
                        fadd[d] = p.v(lambda e, d=d: e.tensor_add(
                            out=ynew2[:, d * 512:(d + 1) * 512],
                            in0=ynew2[:, d * 512:(d + 1) * 512],
                            in1=xat[:, d * 512:(d + 1) * 512]),
                            wait=wfa, inc=True, tag=f"D{l}fa{tt8}_{d}")
                    fadd7_tok[tt8] = fadd[7]
                    for i in range(4):
                        evo = {}
                        for chalf in range(2):
                            bank = pA if chalf == 0 else pB
                            for cc in range(4):
                                c = chalf * 4 + cc
                                wtr = []
                                if cc == 0:
                                    wtr = [("ve", fadd[7])]
                                    if out_dma is not None:
                                        wtr.append(out_dma)
                                p.t(lambda e, c=c, cc=cc, i=i, bank=bank: e.transpose(
                                    bank[:, cc * 128:(cc + 1) * 128],
                                    ynew2[:, c * 512 + i * 128:c * 512 + (i + 1) * 128],
                                    identf),
                                    wait=wtr, inc=(cc == 3), tag=f"D{l}tr{tt8}_{i}_{c}")
                            pe_tr = p.now("pe")
                            evo[chalf] = p.v(lambda e, chalf=chalf, bank=bank: e.tensor_copy(
                                out=outst[:, chalf * 512:(chalf + 1) * 512], in_=bank[:, 0:512]),
                                wait=[("pe", pe_tr)] + ([out_dma]
                                                        if out_dma is not None else []),
                                inc=True, tag=f"D{l}te{tt8}_{i}_{chalf}")
                        out_dma = p.dma("sync", out[t0 + i * 128:t0 + (i + 1) * 128, :], outst,
                                        sem="ld7", wait=[("ve", evo[1])], tag=f"D{l}os{tt8}_{i}")
            r2b.close()

            if stop == f"p2_{l}":
                finish(E, sems, p2dst)
                return True

            return False

        for l in range(2):
            if _layer(l):
                return nc

        p.op("sync", lambda e: e.nop(),
             wait=[(s, p.now(s)) for s in SEMS if s[0:2] in ("ld", "gp", "sc")], tag="fin")
        p.check()
        block = E(nc.Block())
        p.emit(block, sems)
    return nc


def kernel(**inputs):
    w = _prep_weights(inputs)
    x = np.asarray(inputs["L0_attn_inputs"], np.float32)
    nc = build_program()
    in_maps = []
    for core in range(NCORE):
        m = {"xin": np.ascontiguousarray(x[core * SPC:(core + 1) * SPC].reshape(T, DM))}
        m.update(w)
        m.update(CONSTS)
        in_maps.append(m)
    res = run_bass_kernel_spmd(nc, in_maps, core_ids=list(range(NCORE)))
    outs = [np.asarray(res.results[c]["out"]).reshape(SPC, SEQ, DM) for c in range(NCORE)]
    return np.concatenate(outs, axis=0)


# version 14
# speedup vs baseline: 1.0226x; 1.0226x over previous
import contextlib
import numpy as np
import ml_dtypes
import concourse.bass as bass
import concourse.mybir as mybir
from concourse.bass_utils import run_bass_kernel_spmd

F32 = mybir.dt.float32
BF16 = mybir.dt.bfloat16
AF = mybir.ActivationFunctionType
ALU = mybir.AluOpType
AX = mybir.AxisListType
BF = ml_dtypes.bfloat16

NCORE = 8
SEQ = 2048
DM = 1024
SPC = 2
T = SPC * SEQ
NH_A = 16
HD = 64
NH_B = 8
WIN = 512
INNER = 512
HID = 2730
HIDP = 2816
W2C = HIDP // 128
NEG = -1.0e30
EPS = 1e-5
YW = SPC * (SEQ + 1)

ENGINES = ("tensor", "vector", "scalar", "gpsimd", "sync")
SEMS = ("pe", "ve", "se", "ld0", "ld1", "ld2", "ld3", "ld4", "ld5", "ld6", "ld7",
        "gp0", "gp1", "gp2", "sc0", "sc1", "sc2", "sc3")

DBG_STOP = "full"
A3LIM = None
A3SUB = 9
A3QT = 4


class Sched:
    def __init__(self):
        self.st = {e: [] for e in ENGINES}
        self.cnt = {s: 0 for s in SEMS}
        self.pend = {e: [] for e in ENGINES}

    def op(self, eng, fn, wait=(), inc=None, inc_amt=1, tag=""):
        w = list(self.pend[eng]) + list(wait)
        self.pend[eng] = []
        incs = []
        if inc is not None:
            incs.append((inc, inc_amt))
            self.cnt[inc] += inc_amt
        self.st[eng].append((fn, tuple(w), tuple(incs), tag))
        return self.cnt[inc] if inc is not None else None

    def t(self, fn, wait=(), inc=False, tag=""):
        return self.op("tensor", fn, wait, inc="pe" if inc else None, tag=tag)

    def v(self, fn, wait=(), inc=False, tag=""):
        w = list(wait) + [("ve", self.cnt["ve"])]
        return self.op("vector", fn, w, inc="ve", tag=tag)

    def s(self, fn, wait=(), inc=False, tag=""):
        w = list(wait) + [("se", self.cnt["se"])]
        return self.op("scalar", fn, w, inc="se", tag=tag)

    def dma(self, queue, out, in_, sem, wait=(), tag="", slow=False):
        fn = lambda e, out=out, in_=in_, slow=slow: e.dma_start(
            out=out, in_=in_, allow_slow_non_contiguous=slow)
        c = self.op(queue, fn, wait, inc=sem, inc_amt=16, tag=tag)
        return (sem, c)

    def now(self, sem):
        return self.cnt[sem]

    def tok(self):
        return dict(self.cnt)

    def fence(self, engines, token):
        w = [(s, v) for s, v in token.items() if v > 0]
        for e in engines:
            self.pend[e].extend(w)

    def barrier(self):
        self.fence(ENGINES, self.tok())

    def check(self):
        ptr = {e: 0 for e in ENGINES}
        cnt = {s: 0 for s in SEMS}
        total = sum(len(v) for v in self.st.values())
        done = 0
        progress = True
        strict = []
        est = {}
        while progress:
            progress = False
            for e in ENGINES:
                while ptr[e] < len(self.st[e]):
                    fn, waits, incs, tag = self.st[e][ptr[e]]
                    if any(cnt[s] < v for s, v in waits):
                        break
                    for s, v in waits:
                        if s[0:2] in ("ld", "gp", "sc"):
                            if v < cnt[s] and v > est.get(s, 0) and len(strict) < 20:
                                strict.append(f"{e}@{ptr[e]} tag={tag} waits {s}>={v} "
                                              f"issued={cnt[s]} est={est.get(s, 0)}")
                            est[s] = max(est.get(s, 0), v)
                    for s, a in incs:
                        cnt[s] += a
                    ptr[e] += 1
                    done += 1
                    progress = True
        if strict:
            raise RuntimeError("Strict DMA-wait violations:\n  " + "\n  ".join(strict))
        if done != total:
            lines = []
            for e in ENGINES:
                if ptr[e] < len(self.st[e]):
                    fn, waits, incs, tag = self.st[e][ptr[e]]
                    unmet = [(s, v, cnt[s]) for s, v in waits if cnt[s] < v]
                    lines.append(f"  {e}@{ptr[e]}/{len(self.st[e])} tag={tag} unmet={unmet}")
            raise RuntimeError("Sched deadlock:\n" + "\n".join(lines))

    def emit(self, block, sems):
        for ename in ENGINES:
            items = self.st[ename]

            def body(eng, items=items):
                waited = {}
                for fn, waits, incs, tag in items:
                    for s, v in waits:
                        if v > 0 and waited.get(s, -1) < v:
                            eng.wait_ge(sems[s], v)
                            waited[s] = v
                    ins = fn(eng)
                    for s, a in incs:
                        ins.then_inc(sems[s], a)

            getattr(block, ename)(body)


_NAME_CTR = [0]


class Region:
    def __init__(self, nc):
        self.nc = nc
        self.stack = contextlib.ExitStack()

    def sb(self, name, shape, dtype):
        _NAME_CTR[0] += 1
        uname = f"{name}_{_NAME_CTR[0]}"
        return self.stack.enter_context(self.nc.sbuf_tensor(uname, shape, dtype)).ap()

    def close(self):
        self.stack.close()


def _prep_weights(inp):
    w = {}
    w["wqkvA"] = np.ascontiguousarray(np.asarray(inp["L0_attn_Wqkv_w"]).T)
    w["woA"] = np.ascontiguousarray(np.asarray(inp["L0_attn_Wo_w"]).T)
    w["bqkvA"] = np.ascontiguousarray(np.asarray(inp["L0_attn_Wqkv_b"]).reshape(24, 128).T)
    w["boA"] = np.ascontiguousarray(np.asarray(inp["L0_attn_Wo_b"]).reshape(8, 128).T)
    w["ghost"] = np.ascontiguousarray(np.tile(np.asarray(inp["L0_attn_ghost"])[None, :], (128, 1)))
    w["wqkvB"] = np.ascontiguousarray(np.asarray(inp["L0_ffn_wqkv"]))
    w["woB"] = np.ascontiguousarray(np.asarray(inp["L0_ffn_wo"]))
    w["qs"] = np.ascontiguousarray(np.tile(np.asarray(inp["L0_ffn_q_scale"]), (1, 2)).T)
    w["ks"] = np.ascontiguousarray(np.tile(np.asarray(inp["L0_ffn_k_scale"]), (1, 2)).T)
    w["lng"] = np.ascontiguousarray(np.asarray(inp["L0_ffn_ln_g"]).reshape(2, 8, 128).transpose(2, 0, 1).reshape(128, 16))
    w["lnb"] = np.ascontiguousarray(np.asarray(inp["L0_ffn_ln_b"]).reshape(2, 8, 128).transpose(2, 0, 1).reshape(128, 16))
    w1 = np.asarray(inp["L0_ffn_w1"])
    a = np.zeros((2, DM, HIDP), np.float32)
    g = np.zeros((2, DM, HIDP), np.float32)
    a[:, :, :HID] = w1[:, :, :HID]
    g[:, :, :HID] = w1[:, :, HID:]
    w1p = np.zeros((2, DM, 2 * HIDP), np.float32)
    for c in range(W2C):
        w1p[:, :, c * 256:c * 256 + 128] = a[:, :, c * 128:(c + 1) * 128]
        w1p[:, :, c * 256 + 128:c * 256 + 256] = g[:, :, c * 128:(c + 1) * 128]
    w["w1"] = np.ascontiguousarray(w1p)
    b1 = np.asarray(inp["L0_ffn_b1"])
    b1a = np.zeros((2, HIDP), np.float32)
    b1g = np.zeros((2, HIDP), np.float32)
    b1a[:, :HID] = b1[:, :HID]
    b1g[:, :HID] = b1[:, HID:]
    b1p = np.zeros((2, 128, 2 * W2C), np.float32)
    for c in range(W2C):
        b1p[:, :, 2 * c] = b1a[:, c * 128:(c + 1) * 128]
        b1p[:, :, 2 * c + 1] = b1g[:, c * 128:(c + 1) * 128]
    w["b1"] = np.ascontiguousarray(b1p.transpose(1, 0, 2).reshape(128, 2 * 2 * W2C))
    w2 = np.zeros((2, HIDP, DM), np.float32)
    w2[:, :HID, :] = np.asarray(inp["L0_ffn_w2"])
    w["w2"] = np.ascontiguousarray(w2)
    w["b2"] = np.ascontiguousarray(np.asarray(inp["L0_ffn_b2"]).reshape(2, 8, 128).transpose(2, 0, 1).reshape(128, 16))
    return w


def _consts():
    c = {}
    c["identf"] = np.eye(128, dtype=np.float32)
    c["identb"] = np.eye(128, dtype=BF)
    i = np.arange(128)[:, None]
    j = np.arange(128)[None, :]
    c["mcu"] = np.where(j > i, NEG, 0.0).astype(BF)
    c["mcl"] = np.where(j < i, NEG, 0.0).astype(BF)
    c["mcf"] = np.full((128, 128), NEG, BF)
    r = np.zeros((128, 128), np.float32)
    for o in (0, 64):
        for m in range(32):
            r[o + m + 32, o + m] = -1.0
            r[o + m, o + m + 32] = 1.0
    c["r32"] = r.astype(BF)
    c["onescol"] = np.ones((128, 1), BF)
    c["ones1b"] = np.ones((1, 128), BF)
    o64 = np.zeros((65, 64), BF)
    o64[64, :] = 1.0
    c["ones64"] = o64
    e2 = np.zeros((128, 2), BF)
    e2[0:64, 0] = 1.0
    e2[64:128, 1] = 1.0
    c["e2b"] = e2
    bo2 = np.zeros((2, 128), BF)
    bo2[0, 0:64] = 1.0
    bo2[1, 64:128] = 1.0
    c["bo2b"] = bo2
    half = np.arange(0, HD, 2, dtype=np.float64)
    inv_freq = 1.0 / 10000.0 ** (half / HD)
    t = np.arange(SEQ, dtype=np.float64)
    freqs = t[:, None] * inv_freq[None, :]
    freqs = np.concatenate([freqs, freqs], -1)
    base = (half + 0.4 * HD) / (1.4 * HD)
    power = (t - SEQ // 2) / (WIN // 2)
    scale = base[None, :] ** power[:, None]
    scale = np.concatenate([scale, scale], -1)
    cos, sin = np.cos(freqs), np.sin(freqs)
    tabs = np.stack([cos * scale, sin * scale, cos / scale, sin / scale])
    tt = np.zeros((4, 128, 4096), np.float64)
    for wdw in range(4):
        sl = tabs[:, wdw * 512:(wdw + 1) * 512, :].transpose(0, 2, 1)
        sl = np.tile(sl, (1, 2, 1))
        tt[:, :, wdw * 1024:wdw * 1024 + 512] = sl
        tt[:, :, wdw * 1024 + 512:wdw * 1024 + 1024] = sl
    c["tabs"] = tt.astype(BF)
    return c


CONSTS = _consts()


def build_program(stop=None):
    stop = stop or DBG_STOP
    nc = bass.Bass("TRN2", target_bir_lowering=False)
    p = Sched()

    xin = nc.dram_tensor("xin", [T, DM], F32, kind="ExternalInput").ap()
    ew = {}
    for name, shape, dt in [
        ("wqkvA", [DM, 3072], F32), ("woA", [DM, DM], F32),
        ("bqkvA", [128, 24], F32), ("boA", [128, 8], F32), ("ghost", [128, 16], F32),
        ("wqkvB", [2, DM, 1536], F32), ("woB", [2, INNER, DM], F32),
        ("qs", [128, 2], F32), ("ks", [128, 2], F32),
        ("lng", [128, 16], F32), ("lnb", [128, 16], F32),
        ("w1", [2, DM, 2 * HIDP], F32), ("b1", [128, 2 * 2 * W2C], F32),
        ("w2", [2, HIDP, DM], F32), ("b2", [128, 16], F32),
        ("identf", [128, 128], F32), ("identb", [128, 128], BF16),
        ("mcu", [128, 128], BF16), ("mcl", [128, 128], BF16), ("mcf", [128, 128], BF16),
        ("r32", [128, 128], BF16), ("onescol", [128, 1], BF16),
        ("ones1b", [1, 128], BF16), ("ones64", [65, 64], BF16),
        ("e2b", [128, 2], BF16), ("bo2b", [2, 128], BF16),
        ("tabs", [4, 128, 4096], BF16),
    ]:
        ew[name] = nc.dram_tensor(name, shape, dt, kind="ExternalInput").ap()
    out = nc.dram_tensor("out", [T, DM], F32, kind="ExternalOutput").ap()

    xa_d = nc.dram_tensor("xa_d", [8, 128, T], F32, kind="Internal").ap()
    yA_d = nc.dram_tensor("yA_d", [8, 128, YW], F32, kind="Internal").ap()
    yB_d = nc.dram_tensor("yB_d", [8, 128, YW], F32, kind="Internal").ap()
    wbA_q = nc.dram_tensor("wbA_q", [24, 128, 1024], BF16, kind="Internal").ap()
    wbA_o = nc.dram_tensor("wbA_o", [8, 128, 1024], BF16, kind="Internal").ap()
    wbB_q = nc.dram_tensor("wbB_q", [2, 12, 128, 1024], BF16, kind="Internal").ap()
    hg_d = nc.dram_tensor("hg_d", [W2C, 128, T], BF16, kind="Internal").ap()

    dbg = {}
    if stop != "full":
        dbg["xa"] = nc.dram_tensor("dbg_xa", [8, 128, T], F32, kind="ExternalOutput").ap()
        dbg["y"] = nc.dram_tensor("dbg_y", [8, 128, YW], F32, kind="ExternalOutput").ap()

    def finish(E, sems, ydump):
        p.barrier()
        for c in range(8):
            p.dma("sync", dbg["xa"][c], xa_d[c], sem="ld0", tag=f"dxa{c}")
            p.dma("sync", dbg["y"][c], ydump[c], sem="ld0", tag=f"dy{c}")
        p.op("sync", lambda e: e.nop(),
             wait=[(s, p.now(s)) for s in SEMS if s[0:2] in ("ld", "gp", "sc")], tag="fin")
        p.check()
        block = E(nc.Block())
        p.emit(block, sems)

    with contextlib.ExitStack() as ctx:
        E = ctx.enter_context
        sems = {s: E(nc.semaphore(s)) for s in SEMS}

        pr = Region(nc)
        identf = pr.sb("c_identf", [128, 128], F32)
        identb = pr.sb("c_identb", [128, 128], BF16)
        mcu = pr.sb("c_mcu", [128, 128], BF16)
        mcl = pr.sb("c_mcl", [128, 128], BF16)
        mcf = pr.sb("c_mcf", [128, 128], BF16)
        r32 = pr.sb("c_r32", [128, 128], BF16)
        onescol = pr.sb("c_onescol", [128, 1], BF16)
        ones1b = pr.sb("c_ones1b", [1, 128], BF16)
        ones64 = pr.sb("c_ones64", [65, 64], BF16)
        e2b = pr.sb("c_e2b", [128, 2], BF16)
        bo2b = pr.sb("c_bo2b", [2, 128], BF16)
        bqkvA = pr.sb("c_bqkvA", [128, 24], F32)
        boA = pr.sb("c_boA", [128, 8], F32)
        ghost = pr.sb("c_ghost", [128, 16], F32)
        qs_sb = pr.sb("c_qs", [128, 2], F32)
        ks_sb = pr.sb("c_ks", [128, 2], F32)
        lng_sb = pr.sb("c_lng", [128, 16], F32)
        lnb_sb = pr.sb("c_lnb", [128, 16], F32)
        b1_sb = pr.sb("c_b1", [128, 2 * 2 * W2C], F32)
        b2_sb = pr.sb("c_b2", [128, 16], F32)
        zcol = pr.sb("c_zcol", [128, 1], F32)
        epscol = pr.sb("c_eps", [128, 1], F32)

        pA = E(nc.psum_tensor("pA", [128, 1024], F32)).ap()
        pB = E(nc.psum_tensor("pB", [128, 1024], F32)).ap()
        pC = E(nc.psum_tensor("pC", [128, 512], F32)).ap()
        pD = E(nc.psum_tensor("pD", [128, 512], F32)).ap()
        pE = E(nc.psum_tensor("pE", [128, 512], F32)).ap()
        pFb = E(nc.psum_tensor("pFb", [128, 1024], BF16)).ap()
        pFf = pFb.bitcast(F32)

        for name, dst in [("identf", identf), ("identb", identb), ("mcu", mcu),
                          ("mcl", mcl), ("mcf", mcf), ("r32", r32),
                          ("onescol", onescol), ("ones1b", ones1b), ("ones64", ones64),
                          ("e2b", e2b), ("bo2b", bo2b), ("bqkvA", bqkvA),
                          ("boA", boA), ("ghost", ghost)]:
            p.dma("sync", dst, ew[name], sem="ld0", tag="const")
        p.dma("sync", qs_sb, ew["qs"], sem="ld0", tag="const")
        p.dma("sync", ks_sb, ew["ks"], sem="ld0", tag="const")
        p.dma("sync", lng_sb, ew["lng"], sem="ld0", tag="const")
        p.dma("sync", lnb_sb, ew["lnb"], sem="ld0", tag="const")
        p.dma("sync", b1_sb, ew["b1"], sem="ld0", tag="const")
        ld_consts = p.dma("sync", b2_sb, ew["b2"], sem="ld0", tag="const")

        zv = p.v(lambda e: e.memset(zcol, 0.0), wait=[ld_consts], inc=True, tag="zcol")
        p.v(lambda e: e.memset(epscol, EPS), inc=True, tag="eps")
        for yd in (yA_d, yB_d):
            for c in range(8):
                for s in range(SPC):
                    p.dma("sync", yd[c, :, s * (SEQ + 1):s * (SEQ + 1) + 1], zcol,
                          sem="ld1", wait=[("ve", zv)], tag="guard", slow=True)

        wreg = Region(nc)
        wst = [wreg.sb(f"wst{i}", [128, 3072], BF16) for i in range(2)]
        store_tok = []
        for idx, (src, dstb, nd) in enumerate(
            [(ew["wqkvA"], wbA_q, 24), (ew["woA"], wbA_o, 8),
             (ew["wqkvB"][0], wbB_q[0], 12), (ew["wqkvB"][1], wbB_q[1], 12)]):
            for k in range(8):
                n = idx * 8 + k
                buf = wst[n % 2]
                ncols = nd * 128
                w = [store_tok[-2]] if len(store_tok) >= 2 else []
                gpt = p.dma("gpsimd", buf[:, 0:ncols], src[k * 128:(k + 1) * 128, 0:ncols],
                            sem="gp0" if n % 2 == 0 else "gp1", wait=w, tag=f"wbc{idx}_{k}")
                store_tok.append(p.dma("sync",
                      dstb[:, :, k * 128:(k + 1) * 128].rearrange("d p j -> p d j"),
                      buf[:, 0:ncols].rearrange("p (d j) -> p d j", j=128),
                      sem="ld2" if n % 2 == 0 else "ld3",
                      wait=[gpt], tag=f"wbs{idx}_{k}"))
        wbank_tok = [("ld2", p.now("ld2")), ("ld3", p.now("ld3"))]
        wreg.close()

        if stop == "init":
            p.op("sync", lambda e: e.nop(),
                 wait=[(s, p.now(s)) for s in SEMS if s[0:2] in ("ld", "gp", "sc")], tag="fin")
            p.check()
            block = E(nc.Block())
            p.emit(block, sems)
            return nc
        if stop == "dump0":
            finish(E, sems, yA_d)
            return nc

        def emit_ln(ln, in_tiles, out_tiles, gb=None, apply_wait=(), tag="ln"):
            cps, sqs = [], []
            for c in range(8):
                cps.append(p.v(lambda e, c=c: e.tensor_copy(out=ln["xb"][c], in_=in_tiles[c]),
                               inc=True, tag=f"{tag}cp{c}"))
            for c in range(8):
                sqs.append(p.s(lambda e, c=c: e.activation(out=ln["sq"][c], in_=ln["xb"][c],
                                                           func=AF.Square),
                               wait=[("ve", cps[c])], inc=True, tag=f"{tag}sq{c}"))
            for c in range(8):
                p.t(lambda e, c=c: e.matmul(pA[0:1, 0:512], onescol, ln["xb"][c],
                                            start=(c == 0), stop=(c == 7)),
                    wait=[("ve", cps[c])], tag=f"{tag}mmA{c}")
            for c in range(8):
                p.t(lambda e, c=c: e.matmul(pA[0:1, 512:1024], onescol, ln["sq"][c],
                                            start=(c == 0), stop=(c == 7)),
                    wait=[("se", sqs[c])], inc=(c == 7), tag=f"{tag}mmB{c}")
            pe_stats = p.now("pe")
            v1 = p.v(lambda e: e.tensor_scalar_mul(out=ln["rU"], in0=pA[0:1, 0:512],
                                                   scalar1=1.0 / DM),
                     wait=[("pe", pe_stats)], inc=True, tag=f"{tag}mu")
            v2 = p.v(lambda e: e.tensor_scalar_mul(out=ln["rV"], in0=pA[0:1, 512:1024],
                                                   scalar1=1.0 / DM),
                     inc=True, tag=f"{tag}ms")
            v3 = p.v(lambda e: e.scalar_tensor_tensor(out=ln["rA"], in0=ln["rU"], scalar=-1.0,
                                                      in1=ln["rU"], op0=ALU.mult, op1=ALU.mult),
                     wait=[("ve", v1)], inc=True, tag=f"{tag}mu2")
            v4 = p.v(lambda e: e.tensor_add(out=ln["rV"], in0=ln["rV"], in1=ln["rA"]),
                     wait=[("ve", v3), ("ve", v2)], inc=True, tag=f"{tag}var")
            s0 = p.s(lambda e: e.activation(out=ln["rB"], in_=ln["rV"], func=AF.Ln,
                                            bias=epscol[0:1, :]),
                     wait=[("ve", v4)], inc=True, tag=f"{tag}lg")
            s1 = p.s(lambda e: e.activation(out=ln["rV"], in_=ln["rB"], func=AF.Exp,
                                            scale=-0.5),
                     wait=[("se", s0)], inc=True, tag=f"{tag}sd")
            v6 = p.v(lambda e: e.tensor_copy(out=ln["rM"], in_=ln["rV"]),
                     wait=[("se", s1)], inc=True, tag=f"{tag}rstdb")
            v7 = p.v(lambda e: e.tensor_mul(out=ln["rA"], in0=ln["rU"], in1=ln["rV"]),
                     wait=[("ve", v6)], inc=True, tag=f"{tag}mrs")
            v8 = p.v(lambda e: e.tensor_copy(out=ln["rN"], in_=ln["rA"]),
                     wait=[("ve", v7)], inc=True, tag=f"{tag}mrsb")
            p.t(lambda e: e.matmul(pB[:, 0:512], ones1b, ln["rM"], start=True, stop=True),
                wait=[("ve", v6)], tag=f"{tag}bcA")
            p.t(lambda e: e.matmul(pB[:, 512:1024], ones1b, ln["rN"], start=True, stop=True),
                wait=[("ve", v8)], inc=True, tag=f"{tag}bcB")
            pe_bc = p.now("pe")
            outs = []
            for c in range(8):
                w = [("pe", pe_bc)] + (list(apply_wait) if c == 0 else [])
                p.v(lambda e, c=c: e.tensor_mul(out=in_tiles[c], in0=in_tiles[c],
                                                in1=pB[:, 0:512]),
                    wait=w, tag=f"{tag}ap1_{c}")
                if gb is None:
                    outs.append(p.v(
                        lambda e, c=c: e.scalar_tensor_tensor(
                            out=out_tiles[c], in0=pB[:, 512:1024], scalar=-1.0,
                            in1=in_tiles[c], op0=ALU.mult, op1=ALU.add),
                        inc=True, tag=f"{tag}ap2_{c}"))
                else:
                    p.v(lambda e, c=c: e.scalar_tensor_tensor(
                        out=in_tiles[c], in0=pB[:, 512:1024], scalar=-1.0,
                        in1=in_tiles[c], op0=ALU.mult, op1=ALU.add),
                        tag=f"{tag}ap2_{c}")
                    outs.append(p.v(
                        lambda e, c=c: e.tensor_scalar(
                            out=out_tiles[c], in0=in_tiles[c], scalar1=gb[0][c],
                            scalar2=gb[1][c], op0=ALU.mult, op1=ALU.add),
                        inc=True, tag=f"{tag}ap3_{c}"))
            return outs

        def mk_ln_scratch(rg, pfx):
            return {
                "xb": [rg.sb(f"{pfx}xb{c}", [128, 512], BF16) for c in range(8)],
                "sq": [rg.sb(f"{pfx}sq{c}", [128, 512], BF16) for c in range(8)],
                "rU": rg.sb(f"{pfx}rU", [1, 512], F32),
                "rV": rg.sb(f"{pfx}rV", [1, 512], F32),
                "rA": rg.sb(f"{pfx}rA", [1, 512], F32),
                "rB": rg.sb(f"{pfx}rB", [1, 512], F32),
                "rM": rg.sb(f"{pfx}rM", [1, 512], BF16),
                "rN": rg.sb(f"{pfx}rN", [1, 512], BF16),
            }

        def _stageA(s):
            rh = Region(nc)
            QT = [rh.sb(f"QT{c}", [128, SEQ], BF16) for c in range(8)]
            KT = [rh.sb(f"KT{c}", [128, SEQ], BF16) for c in range(8)]
            Vaug = [rh.sb(f"Vg{tc}", [128, NH_A * 65], BF16) for tc in range(16)]

            r1 = Region(nc)
            hlnA = [r1.sb(f"hlA{c}", [128, SEQ], BF16) for c in range(8)]
            inst = [r1.sb(f"inst{i}", [128, 1024], F32) for i in range(4)]
            xf = [r1.sb(f"xfA{c}", [128, 512], F32) for c in range(8)]
            ln1 = mk_ln_scratch(r1, "lnA")
            p.barrier()
            prev_ln = None
            for tt in range(4):
                t0 = s * SEQ + tt * 512
                lds = []
                for i in range(4):
                    wv = [("pe", p.now("pe")), ("ve", p.now("ve"))] if tt > 0 else []
                    lds.append(p.dma("sync", inst[i], xin[t0 + i * 128:t0 + (i + 1) * 128, :],
                               sem="ld4", wait=wv if i == 0 else (), tag=f"A1ld{tt}{i}"))
                evs = []
                for c in range(8):
                    bank = pA if c % 2 == 0 else pB
                    w = [lds[3]]
                    if c >= 2:
                        w.append(("ve", evs[c - 2]))
                    elif prev_ln is not None:
                        w.append(("ve", prev_ln))
                    for i in range(4):
                        p.t(lambda e, c=c, i=i, bank=bank: e.transpose(
                            bank[:, i * 128:(i + 1) * 128],
                            inst[i][:, c * 128:(c + 1) * 128], identf),
                            wait=w if i == 0 else (), inc=(i == 3), tag=f"A1tr{tt}_{c}_{i}")
                    pe_tr = p.now("pe")
                    evs.append(p.v(lambda e, c=c, bank=bank: e.tensor_copy(
                        out=xf[c], in_=bank[:, 0:512]),
                        wait=[("pe", pe_tr)], inc=True, tag=f"A1ev{tt}{c}"))
                for c in range(8):
                    st_xf = p.dma("sync", xa_d[c, :, t0:t0 + 512], xf[c],
                                  sem="ld5", wait=[("ve", evs[c])], tag=f"A1st{tt}{c}")
                p.fence(("tensor",), {"ve": evs[-1]})
                lnouts = emit_ln(ln1, xf, [hlnA[c][:, tt * 512:(tt + 1) * 512] for c in range(8)],
                                 apply_wait=[st_xf], tag=f"A1n{s}{tt}")
                prev_ln = lnouts[-1]
            if stop == "A1":
                finish(E, sems, yA_d)
                return True

            r2 = Region(nc)
            wcol = [r2.sb(f"wcA{i}", [128, 1024], BF16) for i in range(2)]
            vstage = [r2.sb(f"vsA{i}", [128, 512], BF16) for i in range(4)]
            p.barrier()
            prev_mm_done = {}
            prev_ev = {}
            last_vaug = None
            for d in range(24):
                buf = wcol[d % 2]
                w = list(wbank_tok)
                if (d % 2) in prev_mm_done:
                    w.append(("pe", prev_mm_done[d % 2]))
                ld_w = p.dma("sync", buf, wbA_q[d], sem="ld2" if d % 2 == 0 else "ld3",
                             wait=w, tag=f"A2w{d}")
                for tt in range(4):
                    bank = [pA[:, 0:512], pA[:, 512:1024], pB[:, 0:512], pB[:, 512:1024]][tt]
                    wmm = [ld_w]
                    if tt in prev_ev:
                        wmm.append(("ve", prev_ev[tt]))
                    for k in range(8):
                        p.t(lambda e, tt=tt, k=k, bank=bank, buf=buf: e.matmul(
                            bank, buf[:, k * 128:(k + 1) * 128],
                            hlnA[k][:, tt * 512:(tt + 1) * 512],
                            start=(k == 0), stop=(k == 7)),
                            wait=wmm if k == 0 else (), inc=(k == 7), tag=f"A2m{d}_{tt}_{k}")
                    pe_mm = p.now("pe")
                    if d < 8:
                        dst = QT[d][:, tt * 512:(tt + 1) * 512]
                    elif d < 16:
                        dst = KT[d - 8][:, tt * 512:(tt + 1) * 512]
                    else:
                        dst = vstage[tt]
                    wev = [("pe", pe_mm)]
                    if d >= 16 and last_vaug is not None:
                        wev.append(("ve", last_vaug))
                    prev_ev[tt] = p.v(lambda e, dst=dst, bank=bank, d=d: e.tensor_scalar(
                        out=dst, in0=bank, scalar1=bqkvA[:, d:d + 1], scalar2=None,
                        op0=ALU.add),
                        wait=wev, inc=True, tag=f"A2e{d}_{tt}")
                prev_mm_done[d % 2] = p.now("pe")
                if d >= 16:
                    vc = d - 16
                    for tt in range(4):
                        for i in range(4):
                            tc = tt * 4 + i
                            wtr = [("ve", prev_ev[tt])]
                            if last_vaug is not None:
                                wtr.append(("ve", last_vaug))
                            p.t(lambda e, tt=tt, i=i: e.transpose(
                                pFb[:, i * 128:(i + 1) * 128],
                                vstage[tt][:, i * 128:(i + 1) * 128], identb),
                                wait=wtr, inc=True, tag=f"A2vt{d}_{tt}_{i}")
                            pe_t = p.now("pe")
                            last_vaug = p.v(
                                lambda e, tc=tc, vc=vc, i=i: e.tensor_copy(
                                    out=Vaug[tc].rearrange("p (h x) -> p h x", h=NH_A)
                                    [:, 2 * vc:2 * vc + 2, 0:64],
                                    in_=pFb[:, i * 128:(i + 1) * 128]
                                    .rearrange("p (h x) -> p h x", h=2)),
                                wait=[("pe", pe_t)], inc=True, tag=f"A2ve{d}_{tt}_{i}")
            ones_tok = None
            for tc in range(16):
                ones_tok = p.v(lambda e, tc=tc: e.memset(
                    Vaug[tc].rearrange("p (h x) -> p h x", h=NH_A)[:, :, 64:65], 1.0),
                    wait=[("ve", last_vaug)] if tc == 0 else (), inc=True, tag=f"A2on{tc}")
            r2.close()
            r1.close()
            if stop == "A2":
                finish(E, sems, yA_d)
                return True

            roA = Region(nc)
            o_all = [roA.sb(f"oal{c}", [128, SEQ], BF16) for c in range(8)]
            r3 = Region(nc)
            ztA = [r3.sb(f"ztA{i}", [128, 512], BF16) for i in range(16)]
            ktl2 = [r3.sb(f"ktlA{i}", [65, SEQ], BF16) for i in range(2)]
            qtl2 = [r3.sb(f"qtlA{i}", [65, SEQ], BF16) for i in range(2)]
            ost2 = [r3.sb(f"ostA{i}", [64, SEQ], BF16) for i in range(2)]
            mst2 = [r3.sb(f"mstA{i}", [128, 8], F32) for i in range(2)]
            m4b2 = [r3.sb(f"m4A{i}", [4, 128], BF16) for i in range(2)]
            zc65 = r3.sb("zc65", [65, 65], BF16)
            drow = r3.sb("drowA", [65, 512], F32)
            rrow = r3.sb("rrowA", [65, 512], F32)
            rrowb = r3.sb("rrbA", [65, 512], BF16)
            rbcs = r3.sb("rbcA", [64, 512], F32)
            p.barrier()
            zc_tok = p.v(lambda e: e.memset(zc65[:, :], 0.0), inc=True, tag="A3z0")
            k1a = p.v(lambda e: e.memset(ktl2[0][64:65, :], 1.0), inc=True, tag="A3k1")
            k1b = p.v(lambda e: e.memset(ktl2[1][64:65, :], 1.0), inc=True, tag="A3k2")
            qkv_done = p.now("ve")

            units = [(h, qb) for h in range(NH_A) for qb in range(4)]
            bk4 = [pA[:, 0:512], pA[:, 512:1024], pB[:, 0:512], pB[:, 512:1024]]
            gtoks = []
            U = {}
            fin_tok = None
            ost_dma = {0: None, 1: None}
            asm_tok = {}
            prev_mplumb = None
            prev_f4 = [None]
            prev_exp = [None]
            cur_mev = [None]
            m4_dma = {0: None, 1: None}

            def emit_asm(h):
                ch = h // 2
                hb = h % 2
                ktl, qtl = ktl2[hb], qtl2[hb]
                wasm = [("ve", qkv_done), ("pe", p.now("pe")), ("se", p.now("se"))]
                if h % 2 == 0:
                    p.s(lambda e, ch=ch, ktl=ktl: e.activation(
                        out=ktl[0:64, :], in_=KT[ch][0:64, :], func=AF.Copy),
                        wait=wasm, inc=True, tag=f"A3kc{h}")
                    qa = p.s(lambda e, ch=ch, qtl=qtl: e.activation(
                        out=qtl[0:64, :], in_=QT[ch][0:64, :], func=AF.Copy),
                        inc=True, tag=f"A3qc{h}")
                    asm_tok[h] = [("se", qa)]
                else:
                    p.dma("scalar", ktl[0:64, :], KT[ch][64:128, :], sem="sc0",
                          wait=wasm, tag=f"A3kd{h}")
                    asm_tok[h] = [p.dma("scalar", qtl[0:64, :], QT[ch][64:128, :],
                                        sem="sc0", tag=f"A3qd{h}")]

            def emit_p1(u):
                h, qb = units[u]
                ch = h // 2
                rbase = 64 * (h % 2)
                ktl, qtl = ktl2[h % 2], qtl2[h % 2]
                mst = mst2[u % 2]
                st = {}
                qk_red = {}
                for qt in range(4):
                    qtg = qb * 4 + qt
                    width = qb * 512 + (qt + 1) * 128
                    nmst = 0
                    pos = 0
                    red = None
                    while pos < width:
                        seg = min(512, width - pos)
                        pq = bk4[len(gtoks) % 4]
                        wseg = []
                        if len(gtoks) >= 4:
                            wseg = [("ve", gtoks[-4])]
                        final_seg = (pos + seg == width)
                        pieces = []
                        off = 0
                        while off < seg:
                            nn = min(512, seg - off)
                            if final_seg and off + nn == seg and nn > 128:
                                pieces.append((off, nn - 128, False))
                                pieces.append((off + nn - 128, 128, True))
                            elif final_seg and off + nn == seg:
                                pieces.append((off, nn, True))
                            else:
                                pieces.append((off, nn, False))
                            off += nn
                        first = True
                        for off, nn, msk in pieces:
                            last = (off + nn == seg) and not msk
                            p.t(lambda e, pq=pq, off=off, nn=nn, pos=pos, qtg=qtg,
                                ch=ch, rbase=rbase, msk=msk: e.matmul(
                                    pq[:, off:off + nn],
                                    QT[ch][rbase:rbase + 64, qtg * 128:(qtg + 1) * 128],
                                    KT[ch][rbase:rbase + 64, pos + off:pos + off + nn],
                                    start=True, stop=not msk),
                                wait=wseg if first else (), inc=last,
                                tag=f"A3s{h}_{qtg}_{pos+off}")
                            first = False
                            if msk:
                                p.t(lambda e, pq=pq, off=off, nn=nn: e.matmul(
                                    pq[:, off:off + nn], identb, mcu,
                                    start=False, stop=True),
                                    inc=True, tag=f"A3dm{h}_{qtg}")
                        pe_seg = p.now("pe")
                        single = (width <= 512)
                        red = p.v(lambda e, pq=pq, seg=seg, nmst=nmst, mst=mst, qt=qt,
                                  single=single: e.reduce_max(
                            out=mst[:, 4 + qt:5 + qt] if single else mst[:, nmst:nmst + 1],
                            in_=pq[:, 0:seg], axis=AX.X, negate=True),
                            wait=[("pe", pe_seg)], inc=True, tag=f"A3r{h}_{qtg}_{nmst}")
                        gtoks.append(red)
                        nmst += 1
                        pos += seg
                    if nmst == 1:
                        cmb = red
                    else:
                        cmb = p.v(lambda e, nmst=nmst, mst=mst, qt=qt: e.tensor_reduce(
                            out=mst[:, 4 + qt:5 + qt], in_=mst[:, 0:nmst],
                            axis=AX.X, op=ALU.min),
                            wait=[("ve", red)], inc=True, tag=f"A3c{h}_{qtg}")
                    qk_red[qt] = cmb
                st["qk_red"] = qk_red
                st["mst"] = mst
                U[u] = st

            def emit_p1b(u):
                nonlocal prev_mplumb
                h, qb = units[u]
                qtl = qtl2[h % 2]
                mst = U[u]["mst"]
                qk_red = U[u]["qk_red"]
                m4 = m4b2[u % 2]
                wtm = [("ve", qk_red[qt]) for qt in range(4)]
                if cur_mev[0] is not None:
                    wtm.append(("ve", cur_mev[0]))
                if prev_f4[0] is not None:
                    wtm.append(("ve", prev_f4[0]))
                ptm = p.t(lambda e, mst=mst: e.transpose(
                    pFf[0:4, 0:128], mst[:, 4:8], identf),
                    wait=wtm, inc=True, tag=f"A3t{h}_{qb}")
                wm4 = [("pe", ptm)]
                if m4_dma[u % 2] is not None:
                    wm4.append(m4_dma[u % 2])
                mev = p.v(lambda e, m4=m4: e.tensor_copy(out=m4, in_=pFf[0:4, 0:128]),
                          wait=wm4, inc=True, tag=f"A3m{h}_{qb}")
                cur_mev[0] = mev
                m_done = p.dma("gpsimd", qtl[64:65, qb * 512:(qb + 1) * 512], m4,
                               sem="gp0" if u % 2 == 0 else "gp1",
                               wait=[("ve", mev)], tag=f"A3mv{h}_{qb}")
                m4_dma[u % 2] = m_done
                prev_mplumb = m_done
                U[u]["m_done"] = m_done

            def emit_p2(u):
                nonlocal fin_tok, prev_mplumb
                h, qb = units[u]
                ch = h // 2
                rbase = 64 * (h % 2)
                ktl, qtl = ktl2[h % 2], qtl2[h % 2]
                ost = ost2[h % 2]
                m_done = U[u]["m_done"]
                prev_fin = fin_tok
                nkc = 4 * qb + 4
                exp_tok = {}
                for kc in range(nkc):
                    pz = pC if kc % 2 == 0 else pD
                    wz = [m_done] + asm_tok[h]
                    if kc == 0:
                        wz.append(("ve", qkv_done))
                    if kc >= 2:
                        wz.append(("se", exp_tok[kc - 2]))
                    elif prev_exp[0] is not None:
                        wz.append(("se", prev_exp[0]))
                    rel = kc - 4 * qb
                    first_z = True
                    q0 = max(0, rel + 1)
                    if q0 < 4:
                        p.t(lambda e, pz=pz, kc=kc, qb=qb, q0=q0, ktl=ktl, qtl=qtl: e.matmul(
                            pz[:, q0 * 128:512], ktl[:, kc * 128:(kc + 1) * 128],
                            qtl[:, qb * 512 + q0 * 128:(qb + 1) * 512],
                            start=True, stop=True),
                            wait=wz if first_z else (), inc=(rel < 0),
                            tag=f"A3z{h}_{qb}_{kc}")
                        first_z = False
                    if 0 <= rel < 4:
                        p.t(lambda e, pz=pz, kc=kc, qb=qb, rel=rel, ktl=ktl, qtl=qtl: e.matmul(
                            pz[:, rel * 128:(rel + 1) * 128],
                            ktl[:, kc * 128:(kc + 1) * 128],
                            qtl[:, qb * 512 + rel * 128:qb * 512 + (rel + 1) * 128],
                            start=True, stop=False),
                            wait=wz if first_z else (), tag=f"A3zd{h}_{qb}_{kc}")
                        p.t(lambda e, pz=pz, rel=rel: e.matmul(
                            pz[:, rel * 128:(rel + 1) * 128], identb, mcl,
                            start=False, stop=True),
                            inc=True, tag=f"A3xd{h}_{qb}_{kc}")
                    pe_z = p.now("pe")
                    col0 = max(0, rel) * 128
                    exp_tok[kc] = p.s(lambda e, pz=pz, kc=kc, col0=col0: e.activation(
                        out=ztA[kc][:, col0:512], in_=pz[:, col0:512],
                        func=AF.Exp, scale=0.125),
                        wait=[("pe", pe_z)], inc=True, tag=f"A3e{h}_{qb}_{kc}")
                prev_exp[0] = exp_tok[nkc - 1]
                for kc in range(nkc):
                    wav = [("se", exp_tok[kc])]
                    if kc == 0:
                        wav.append(("ve", ones_tok))
                        if fin_tok is not None:
                            wav.append(("ve", fin_tok))
                    col0 = max(0, kc - 4 * qb) * 128
                    p.t(lambda e, kc=kc, h=h, col0=col0: e.matmul(
                        pE[0:65, col0:512],
                        Vaug[kc][:, h * 65:(h + 1) * 65], ztA[kc][:, col0:512],
                        start=(kc == 0), stop=False),
                        wait=wav, tag=f"A3av{h}_{qb}_{kc}")
                p.t(lambda e: e.matmul(
                    pE[0:65, :], zc65[64:65, 0:65], ktl2[0][64:65, 0:512],
                    start=False, stop=True),
                    inc=True, tag=f"A3avz{h}_{qb}")
                pe_av = p.now("pe")
                f2a = p.s(lambda e, h=h: e.activation(
                    out=rrow[64:65, :], in_=pE[64:65, :], func=AF.Ln,
                    bias=ghost[64:65, h:h + 1]),
                    wait=[("pe", pe_av)] + ([ost_dma[h % 2]] if (qb == 0 and ost_dma[h % 2])
                                            else []),
                    inc=True, tag=f"A3lg{h}_{qb}")
                f2 = p.s(lambda e: e.activation(out=rrowb[64:65, :], in_=rrow[64:65, :],
                                                func=AF.Exp, scale=-1.0),
                         wait=[("se", f2a)], inc=True, tag=f"A3rc{h}_{qb}")
                wfb = [("se", f2)]
                if cur_mev[0] is not None:
                    wfb.append(("ve", cur_mev[0]))
                fb = p.t(lambda e: e.matmul(pFf[0:64, 0:512], ones64[64:65, :],
                                            rrowb[64:65, :], start=True, stop=True),
                         wait=wfb, inc=True, tag=f"A3bc{h}_{qb}")
                f4 = p.v(lambda e: e.tensor_copy(out=rbcs, in_=pFf[0:64, 0:512]),
                         wait=[("pe", fb)], inc=True, tag=f"A3be{h}_{qb}")
                prev_f4[0] = f4
                fin_tok = p.v(lambda e, qb=qb, ost=ost: e.tensor_mul(
                    out=ost[:, qb * 512:(qb + 1) * 512], in0=pE[0:64, :], in1=rbcs),
                    wait=[("ve", f4)], inc=True, tag=f"A3o{h}_{qb}")
                if qb == 3:
                    ost_dma[h % 2] = p.dma(
                        "scalar", o_all[ch][rbase:rbase + 64, :], ost,
                        sem="sc1" if h % 2 == 0 else "sc2",
                        wait=[("ve", fin_tok)], tag=f"A3om{h}")

            for u in range(len(units)):
                h, qb = units[u]
                if u == 0:
                    emit_asm(0)
                emit_p1(u)
                if u >= 1:
                    emit_p1b(u - 1)
                if u >= 2:
                    emit_p2(u - 2)
                if qb == 3 and h + 1 < NH_A:
                    emit_asm(h + 1)
            emit_p1b(len(units) - 1)
            emit_p2(len(units) - 2)
            emit_p2(len(units) - 1)
            oall_tok = ("sc1", p.now("sc1"))
            oall_tok2 = ("sc2", p.now("sc2"))
            r3.close()
            if stop == "A3":
                finish(E, sems, yA_d)
                return True

            r4 = Region(nc)
            woAb = [r4.sb(f"woAb{k}", [128, 1024], BF16) for k in range(8)]
            xft = r4.sb("xft", [128, 8 * 512], F32)
            y0t = r4.sb("y0t", [128, 8 * 512], F32)
            ln4 = mk_ln_scratch(r4, "lnY")
            p.barrier()
            for k in range(8):
                ld_wo = p.dma("sync", woAb[k], wbA_o[k], sem="ld4",
                              wait=wbank_tok if k == 0 else (), tag=f"A4w{k}")
            prev_lnout = None
            for tt in range(4):
                t0 = s * SEQ + tt * 512
                wx = [("ve", p.now("ve")), ("ld5", p.now("ld5"))] if tt > 0 else []
                ld_x = p.dma("sync", xft.rearrange("p (c j) -> p c j", j=512),
                             xa_d[:, :, t0:t0 + 512].rearrange("c p j -> p c j"),
                             sem="ld5", wait=wx, tag=f"A4x{tt}")
                ev4 = {}
                for d in range(8):
                    bank = [pA[:, 0:512], pA[:, 512:1024], pB[:, 0:512], pB[:, 512:1024]][d % 4]
                    wmm = [ld_wo, oall_tok, oall_tok2]
                    if prev_lnout is not None:
                        wmm.append(("ve", prev_lnout))
                    if d >= 4:
                        wmm.append(("ve", ev4[d - 4]))
                    for k in range(8):
                        p.t(lambda e, d=d, k=k, bank=bank, tt=tt: e.matmul(
                            bank, woAb[d][:, k * 128:(k + 1) * 128],
                            o_all[k][:, tt * 512:(tt + 1) * 512],
                            start=(k == 0), stop=(k == 7)),
                            wait=wmm if k == 0 else (), inc=(k == 7), tag=f"A4m{tt}_{d}_{k}")
                    pe_mm = p.now("pe")
                    ev4[d] = p.v(lambda e, d=d, bank=bank: e.scalar_tensor_tensor(
                        out=xft[:, d * 512:(d + 1) * 512], in0=bank,
                        scalar=boA[:, d:d + 1], in1=xft[:, d * 512:(d + 1) * 512],
                        op0=ALU.add, op1=ALU.add),
                        wait=[("pe", pe_mm), ld_x], inc=True, tag=f"A4e{tt}_{d}")
                st_x = p.dma("sync", xa_d[:, :, t0:t0 + 512].rearrange("c p j -> p c j"),
                             xft.rearrange("p (c j) -> p c j", j=512),
                             sem="ld5", wait=[("ve", ev4[7])], tag=f"A4sx{tt}")
                p.fence(("tensor",), {"ve": ev4[7]})
                ln_out = emit_ln(ln4, [xft[:, d * 512:(d + 1) * 512] for d in range(8)],
                                 [y0t[:, d * 512:(d + 1) * 512] for d in range(8)],
                                 apply_wait=[st_x], tag=f"A4n{s}{tt}")
                prev_lnout = ln_out[-1]
                p.dma("sync",
                      yA_d[:, :, s * (SEQ + 1) + 1 + tt * 512:s * (SEQ + 1) + 1 + tt * 512 + 512]
                      .rearrange("c p j -> p c j"),
                      y0t.rearrange("p (c j) -> p c j", j=512),
                      sem="ld5", wait=[("ve", ln_out[-1])], tag=f"A4sy{tt}")
            r4.close()
            roA.close()
            rh.close()
            return False

        for s in range(SPC):
            if _stageA(s):
                return nc

        if stop == "A":
            finish(E, sems, yA_d)
            return nc

        def _layer(l):
            p1src, p1dst = yA_d, yB_d
            p2src, p2dst = yB_d, yA_d

            rb = Region(nc)
            ytil = [rb.sb(f"ytB{s}", [128, 8 * 513], F32) for s in range(SPC)]
            hq = [rb.sb(f"hqB{c}", [128, 1024], BF16) for c in range(8)]
            qraw = [rb.sb(f"qrB{c}", [128, 1024], BF16) for c in range(4)]
            kraw = [rb.sb(f"krB{c}", [128, 1024], BF16) for c in range(4)]
            vraw = [rb.sb(f"vrB{c}", [128, 1024], BF16) for c in range(4)]
            kring = [[rb.sb(f"kgB{sl}_{c}", [128, 1024], BF16) for c in range(4)]
                     for sl in range(2)]
            vaugB = [[rb.sb(f"vgB{sl}_{i}", [128, NH_B * 65], BF16) for i in range(8)]
                     for sl in range(2)]
            tabst = [rb.sb(f"tbB{i}", [128, 1024], BF16) for i in range(4)]
            sqB = rb.sb("sqB", [128, 1024], BF16)
            rnr = rb.sb("rnr", [2, 1024], F32)
            rnr2 = rb.sb("rnr2", [2, 1024], F32)
            rnrb = rb.sb("rnrb", [2, 1024], BF16)
            scr1 = rb.sb("scr1", [128, 1024], BF16)
            scr2 = rb.sb("scr2", [128, 1024], BF16)
            ktlB2 = [rb.sb(f"ktlB{i}", [65, 1024], BF16) for i in range(2)]
            zcB = rb.sb("zcB", [65, 65], BF16)
            qtlB2 = [rb.sb(f"qtlB{i}", [65, 512], BF16) for i in range(2)]
            ztB = [rb.sb(f"ztB{i}", [128, 512], BF16) for i in range(8)]
            ostB2 = [rb.sb(f"ostB{i}", [64, NH_B * 512], BF16) for i in range(2)]
            ocat = rb.sb("ocat", [128, 4 * 1024], BF16)
            ynew = rb.sb("ynew", [128, 8 * 512], F32)
            mstB2 = [rb.sb(f"mstB{i}", [128, 8], F32) for i in range(2)]
            m4bB2 = [rb.sb(f"m4B{i}", [4, 128], BF16) for i in range(2)]
            drB = rb.sb("drB", [65, 512], F32)
            rrB = rb.sb("rrB", [65, 512], F32)
            rrbB = rb.sb("rrbB", [65, 512], BF16)
            rbcB = rb.sb("rbcB", [64, 512], F32)
            wcolB = [rb.sb(f"wcB{i}", [128, 1024], BF16) for i in range(2)]
            woBb = [rb.sb(f"woBb{k}", [128, 1024], BF16) for k in range(4)]
            p.barrier()
            p.v(lambda e: e.memset(zcB[:, :], 0.0), inc=True, tag=f"B{l}z0")
            p.v(lambda e: e.memset(ktlB2[0][64:65, :], 1.0), inc=True, tag=f"B{l}k1")
            konceB = p.v(lambda e: e.memset(ktlB2[1][64:65, :], 1.0), inc=True, tag=f"B{l}k2")
            ostB_dma = {0: None, 1: None}
            for k in range(4):
                ld_woB = p.dma("gpsimd", woBb[k], ew["woB"][l][k * 128:(k + 1) * 128, :],
                               sem="gp2", tag=f"B{l}wo{k}")

            fin_tokB = None
            mm_doneB = {}
            UB = {}
            asmB = {}
            prev_mpl = [None]
            prev_f4B = [None]
            prev_expB = [None]
            cur_mevB = [None]
            vt_hist = []
            m4_dmaB = {0: None, 1: None}
            ystore_c = [None]
            kq_cell = [None]
            for w in range(4):
                pe_all = p.now("pe")
                se_all = p.now("se")
                ve_all = p.now("ve")
                sc0T = ("sc0", p.now("sc0"))
                sc1T = ("sc1", p.now("sc1"))
                wtab = [("ve", kq_cell[0])] if kq_cell[0] is not None else []
                for i in range(4):
                    tab_ld = p.dma("gpsimd", tabst[i], ew["tabs"][i][:, w * 1024:(w + 1) * 1024],
                                   sem="gp2", wait=wtab if i == 0 else (), tag=f"B{l}t{w}{i}")
                for s in range(SPC):
                    ld_y = p.dma("sync", ytil[s].rearrange("p (c j) -> p c j", j=513),
                                 p1src[:, :, s * (SEQ + 1) + w * 512:s * (SEQ + 1) + w * 512 + 513]
                                 .rearrange("c p j -> p c j"), sem="ld5",
                                 wait=[("ve", ve_all), ("se", se_all)]
                                 if (w > 0 and s == 0) else (),
                                 tag=f"B{l}y{w}{s}")
                hq_done = None
                for c in range(8):
                    off = 0 if c >= 4 else 1
                    for s in range(SPC):
                        hq_done = p.s(lambda e, c=c, s=s, off=off: e.activation(
                            out=hq[c][:, s * 512:(s + 1) * 512],
                            in_=ytil[s][:, c * 513 + off:c * 513 + off + 512],
                            func=AF.Copy),
                            wait=[ld_y, ("pe", pe_all)] if (c == 0 and s == 0) else (),
                            inc=True, tag=f"B{l}h{w}{c}{s}")
                evB = {}
                for d in range(12):
                    buf = wcolB[d % 2]
                    wv = list(wbank_tok)
                    if (d % 2) in mm_doneB:
                        wv.append(("pe", mm_doneB[d % 2]))
                    ld_w = p.dma("sync", buf, wbB_q[l, d], sem="ld2" if d % 2 == 0 else "ld3",
                                 wait=wv, tag=f"B{l}w{w}{d}")
                    for s in range(SPC):
                        bank = [pA[:, 0:512], pA[:, 512:1024], pB[:, 0:512],
                                pB[:, 512:1024]][2 * (d % 2) + s]
                        wmm = [ld_w, ("se", hq_done)]
                        if (d - 2, s) in evB:
                            wmm.append(("ve", evB[(d - 2, s)]))
                        for k in range(8):
                            p.t(lambda e, k=k, bank=bank, buf=buf, s=s: e.matmul(
                                bank, buf[:, k * 128:(k + 1) * 128],
                                hq[k][:, s * 512:(s + 1) * 512],
                                start=(k == 0), stop=(k == 7)),
                                wait=wmm if k == 0 else (), inc=(k == 7),
                                tag=f"B{l}m{w}_{d}_{s}_{k}")
                        pe_mm = p.now("pe")
                        dst = (qraw[d] if d < 4 else kraw[d - 4] if d < 8 else vraw[d - 8])
                        wevac = [("pe", pe_mm)]
                        if d == 0 and s == 0 and w > 0:
                            wevac += [("se", se_all), sc0T, sc1T]
                        evB[(d, s)] = p.v(lambda e, dst=dst, bank=bank, s=s: e.tensor_copy(
                            out=dst[:, s * 512:(s + 1) * 512], in_=bank),
                            wait=wevac, inc=True, tag=f"B{l}e{w}_{d}_{s}")
                    mm_doneB[d % 2] = p.now("pe")
                qkv_doneB = p.now("ve")

                def rotnorm(raw, dst_tiles, sc_ap, tcs, tss, ev_base, prev, pfx,
                            extra0=()):
                    outs = []
                    for c in range(4):
                        wsq = [("ve", evB[(ev_base + c, 1)])]
                        if outs:
                            wsq.append(("ve", outs[-1]))
                        elif prev is not None:
                            wsq.append(("ve", prev))
                        sq_t = p.s(lambda e, c=c: e.activation(out=sqB, in_=raw[c],
                                                               func=AF.Square),
                                   wait=wsq, inc=True, tag=f"{pfx}sq{c}")
                        p.t(lambda e: e.matmul(pA[0:2, 0:512], e2b, sqB[:, 0:512],
                                               start=True, stop=True),
                            wait=[("se", sq_t), ("ve", qkv_doneB)], tag=f"{pfx}s1{c}")
                        p.t(lambda e: e.matmul(pA[0:2, 512:1024], e2b, sqB[:, 512:1024],
                                               start=True, stop=True),
                            inc=True, tag=f"{pfx}s2{c}")
                        pe_ss = p.now("pe")
                        r1t = p.v(lambda e: e.tensor_scalar_max(out=rnr, in0=pA[0:2, 0:1024],
                                                                scalar1=1e-24),
                                  wait=[("pe", pe_ss)], inc=True, tag=f"{pfx}cl{c}")
                        r2t = p.s(lambda e: e.activation(out=rnr2, in_=rnr, func=AF.Ln),
                                  wait=[("ve", r1t)], inc=True, tag=f"{pfx}sr{c}")
                        r3t = p.s(lambda e: e.activation(out=rnr, in_=rnr2, func=AF.Exp,
                                                         scale=-0.5),
                                  wait=[("se", r2t)], inc=True, tag=f"{pfx}rc{c}")
                        r4t = p.v(lambda e: e.tensor_copy(out=rnrb, in_=rnr),
                                  wait=[("se", r3t)], inc=True, tag=f"{pfx}rb{c}")
                        p.t(lambda e: e.matmul(pB[:, 0:512], bo2b, rnrb[:, 0:512],
                                               start=True, stop=True),
                            wait=[("ve", r4t)], tag=f"{pfx}b1{c}")
                        p.t(lambda e: e.matmul(pB[:, 512:1024], bo2b, rnrb[:, 512:1024],
                                               start=True, stop=True),
                            inc=True, tag=f"{pfx}b2{c}")
                        pe_bc = p.now("pe")
                        n1 = p.v(lambda e, c=c: e.scalar_tensor_tensor(
                            out=raw[c], in0=raw[c], scalar=sc_ap, in1=pB[:, 0:1024],
                            op0=ALU.mult, op1=ALU.mult),
                            wait=[("pe", pe_bc)] + (list(extra0) if c == 0 else []),
                            inc=True, tag=f"{pfx}nm{c}")
                        p.t(lambda e, c=c: e.matmul(pA[:, 0:512], r32, raw[c][:, 0:512],
                                                    start=True, stop=True),
                            wait=[("ve", n1)], tag=f"{pfx}r1{c}")
                        p.t(lambda e, c=c: e.matmul(pA[:, 512:1024], r32,
                                                    raw[c][:, 512:1024],
                                                    start=True, stop=True),
                            inc=True, tag=f"{pfx}r2{c}")
                        pe_rot = p.now("pe")
                        c1 = p.v(lambda e, c=c: e.tensor_mul(out=scr1, in0=raw[c], in1=tcs),
                                 wait=[tab_ld], inc=True, tag=f"{pfx}c1{c}")
                        c2 = p.v(lambda e: e.tensor_mul(out=scr2[:, 0:512], in0=pA[:, 0:512],
                                                        in1=tss[:, 0:512]),
                                 wait=[("pe", pe_rot)], inc=True, tag=f"{pfx}c2{c}")
                        c3 = p.v(lambda e: e.tensor_mul(out=scr2[:, 512:1024],
                                                        in0=pA[:, 512:1024],
                                                        in1=tss[:, 512:1024]),
                                 inc=True, tag=f"{pfx}c3{c}")
                        outs.append(p.v(lambda e, c=c: e.tensor_add(
                            out=dst_tiles[c], in0=scr1, in1=scr2),
                            wait=[("ve", c1), ("ve", c3)] +
                            (list(extra0) if c == 0 else []),
                            inc=True, tag=f"{pfx}fn{c}"))
                    return outs

                exq = [sc0T, sc1T] if w > 0 else []
                exk = ([("pe", pe_all), sc0T, sc1T] if w > 0 else [])
                q_out = rotnorm(qraw, qraw, qs_sb[:, l:l + 1], tabst[0], tabst[1],
                                0, None, f"B{l}q{w}_", extra0=exq)
                k_out = rotnorm(kraw, kring[w % 2], ks_sb[:, l:l + 1], tabst[2], tabst[3],
                                4, q_out[-1], f"B{l}K{w}_", extra0=exk)
                kq_cell[0] = k_out[-1]
                vt_tok = None
                for c in range(4):
                    for s in range(SPC):
                        for i in range(4):
                            wv = [("ve", evB[(8 + c, s)])]
                            if vt_hist:
                                wv.append(("ve", vt_hist[-1]))
                            if (c, s, i) == (0, 0, 0) and fin_tokB is not None:
                                wv.append(("ve", fin_tokB))
                            p.t(lambda e, c=c, s=s, i=i: e.transpose(
                                pFb[:, i * 128:(i + 1) * 128],
                                vraw[c][:, s * 512 + i * 128:s * 512 + (i + 1) * 128],
                                identb),
                                wait=wv, inc=True, tag=f"B{l}vt{w}_{c}{s}{i}")
                            pe_t = p.now("pe")
                            vt_tok = p.v(lambda e, c=c, s=s, i=i, w=w: e.tensor_copy(
                                out=vaugB[w % 2][s * 4 + i]
                                .rearrange("p (h x) -> p h x", h=NH_B)[:, 2 * c:2 * c + 2, 0:64],
                                in_=pFb[:, i * 128:(i + 1) * 128]
                                .rearrange("p (h x) -> p h x", h=2)),
                                wait=[("pe", pe_t)], inc=True, tag=f"B{l}ve{w}_{c}{s}{i}")
                            vt_hist.append(vt_tok)
                vones = None
                for i in range(8):
                    vones = p.v(lambda e, i=i, w=w: e.memset(
                        vaugB[w % 2][i].rearrange("p (h x) -> p h x", h=NH_B)[:, :, 64:65], 1.0),
                        wait=[("ve", vt_tok)] if i == 0 else (), inc=True, tag=f"B{l}vo{w}{i}")
                rot_done = p.now("ve")
                kq_done = {"q": q_out[-1], "k": k_out[-1]}

                def emit_asmB(u):
                    s, h = divmod(u, NH_B)
                    ch = h // 2
                    ktlB, qtlB = ktlB2[u % 2], qtlB2[u % 2]
                    wk = [("ve", rot_done), ("pe", p.now("pe"))]
                    if fin_tokB is not None:
                        wk.append(("ve", fin_tokB))
                    if h % 2 == 0:
                        if w > 0:
                            p.s(lambda e, ch=ch, s=s, w=w, ktlB=ktlB: e.activation(
                                out=ktlB[0:64, 0:512],
                                in_=kring[(w - 1) % 2][ch][0:64, s * 512:(s + 1) * 512],
                                func=AF.Copy),
                                wait=wk, inc=True, tag=f"B{l}ka{w}{s}{h}")
                        p.s(lambda e, ch=ch, s=s, w=w, ktlB=ktlB: e.activation(
                            out=ktlB[0:64, 512:1024],
                            in_=kring[w % 2][ch][0:64, s * 512:(s + 1) * 512],
                            func=AF.Copy),
                            wait=wk, inc=True, tag=f"B{l}kb{w}{s}{h}")
                        qa = p.s(lambda e, ch=ch, s=s, qtlB=qtlB: e.activation(
                            out=qtlB[0:64, :], in_=qraw[ch][0:64, s * 512:(s + 1) * 512],
                            func=AF.Copy),
                            inc=True, tag=f"B{l}qa{w}{s}{h}")
                        asmB[u] = [("se", qa)]
                    else:
                        sem = "sc0" if (u % 2 == 0) else "sc1"
                        if w > 0:
                            p.dma("scalar", ktlB[0:64, 0:512],
                                  kring[(w - 1) % 2][ch][64:128, s * 512:(s + 1) * 512],
                                  sem=sem, wait=wk, tag=f"B{l}ka{w}{s}{h}")
                        p.dma("scalar", ktlB[0:64, 512:1024],
                              kring[w % 2][ch][64:128, s * 512:(s + 1) * 512],
                              sem=sem, wait=wk, tag=f"B{l}kb{w}{s}{h}")
                        asmB[u] = [p.dma("scalar", qtlB[0:64, :],
                                         qraw[ch][64:128, s * 512:(s + 1) * 512],
                                         sem=sem, tag=f"B{l}qa{w}{s}{h}")]

                def emit_p1B(u):
                    s, h = divmod(u, NH_B)
                    ch = h // 2
                    rbase = 64 * (h % 2)
                    mstB = mstB2[u % 2]
                    qk_redB = {}
                    prev_qk = UB[u - 1]["qk"] if u > 0 else None
                    for qt in range(4):
                        pq = pA if qt % 2 == 0 else pB
                        width = 640 if w > 0 else qt * 128 + 128
                        wseg = [("ve", rot_done)]
                        if qt >= 2:
                            wseg.append(("ve", qk_redB[qt - 2]))
                        elif prev_qk is not None:
                            wseg.append(("ve", prev_qk[qt + 2]))
                        segs = []
                        if w > 0:
                            segs.append((kring[(w - 1) % 2][ch], qt * 128, 0, 512 - qt * 128))
                            segs.append((kring[w % 2][ch], 0, 512 - qt * 128, qt * 128 + 128))
                        else:
                            segs.append((kring[0][ch], 0, 0, qt * 128 + 128))
                        cuts = {0, width, width - 128, 512}
                        if w > 0:
                            cuts.add(128)
                        mm_list = []
                        for (src, so, po, n) in segs:
                            bnds = sorted(c for c in cuts if po < c < po + n)
                            prev = po
                            for c in bnds + [po + n]:
                                mm_list.append((src, so + prev - po, prev, c - prev))
                                prev = c
                        masked0 = (0, 128) if w > 0 else None
                        masked1 = (width - 128, width)
                        plain = [t_ for t_ in mm_list
                                 if not ((masked0 and t_[2] == masked0[0]) or
                                         t_[2] == masked1[0])]
                        pairs = [t_ for t_ in mm_list if t_ not in plain]
                        first = True
                        for (src, so, po, nn) in plain:
                            p.t(lambda e, pq=pq, src=src, so=so, po=po, nn=nn, qt=qt,
                                s=s, ch=ch, rbase=rbase: e.matmul(
                                    pq[:, po:po + nn],
                                    qraw[ch][rbase:rbase + 64,
                                             s * 512 + qt * 128:s * 512 + (qt + 1) * 128],
                                    src[rbase:rbase + 64, s * 512 + so:s * 512 + so + nn],
                                    start=True, stop=True),
                                wait=wseg if first else (), tag=f"B{l}s{w}_{s}{h}{qt}_{po}")
                            first = False
                        for (src, so, po, nn) in pairs:
                            mask = mcu if (masked0 and po == masked0[0]) else mcl
                            p.t(lambda e, pq=pq, src=src, so=so, po=po, nn=nn, qt=qt,
                                s=s, ch=ch, rbase=rbase: e.matmul(
                                    pq[:, po:po + nn],
                                    qraw[ch][rbase:rbase + 64,
                                             s * 512 + qt * 128:s * 512 + (qt + 1) * 128],
                                    src[rbase:rbase + 64, s * 512 + so:s * 512 + so + nn],
                                    start=True, stop=False),
                                wait=wseg if first else (), tag=f"B{l}sm{w}_{s}{h}{qt}_{po}")
                            first = False
                            p.t(lambda e, pq=pq, po=po, mask=mask: e.matmul(
                                pq[:, po:po + 128], identb, mask,
                                start=False, stop=True),
                                inc=(po == masked1[0]), tag=f"B{l}dm{w}_{s}{h}{qt}_{po}")
                        pe_seg = p.now("pe")
                        red = p.v(lambda e, pq=pq, width=width, qt=qt, mstB=mstB: e.reduce_max(
                            out=mstB[:, qt:qt + 1], in_=pq[:, 0:width],
                            axis=AX.X, negate=True),
                            wait=[("pe", pe_seg)], inc=True, tag=f"B{l}r{w}_{s}{h}{qt}")
                        qk_redB[qt] = red
                    UB[u] = {"qk": qk_redB, "mst": mstB}

                def emit_p1bB(u):
                    s, h = divmod(u, NH_B)
                    qtlB = qtlB2[u % 2]
                    mstB = UB[u]["mst"]
                    qk_redB = UB[u]["qk"]
                    m4 = m4bB2[u % 2]
                    wtm = [("ve", qk_redB[qt]) for qt in range(4)]
                    if cur_mevB[0] is not None:
                        wtm.append(("ve", cur_mevB[0]))
                    if prev_f4B[0] is not None:
                        wtm.append(("ve", prev_f4B[0]))
                    ptm = p.t(lambda e, mstB=mstB: e.transpose(
                        pFf[0:4, 0:128], mstB[:, 0:4], identf),
                        wait=wtm, inc=True, tag=f"B{l}tm{w}_{s}{h}")
                    wm4 = [("pe", ptm)]
                    if m4_dmaB[u % 2] is not None:
                        wm4.append(m4_dmaB[u % 2])
                    mev = p.v(lambda e, m4=m4: e.tensor_copy(out=m4, in_=pFf[0:4, 0:128]),
                              wait=wm4, inc=True, tag=f"B{l}mm{w}_{s}{h}")
                    cur_mevB[0] = mev
                    m_doneB = p.dma("gpsimd", qtlB[64:65, 0:512], m4,
                                    sem="gp0" if u % 2 == 0 else "gp1",
                                    wait=[("ve", mev)], tag=f"B{l}mv{w}_{s}{h}")
                    m4_dmaB[u % 2] = m_doneB
                    prev_mpl[0] = m_doneB
                    UB[u]["m"] = m_doneB

                def emit_p2B(u):
                    nonlocal fin_tokB
                    s, h = divmod(u, NH_B)
                    ch = h // 2
                    rbase = 64 * (h % 2)
                    ktlB, qtlB = ktlB2[u % 2], qtlB2[u % 2]
                    ostB = ostB2[s % 2]
                    m_doneB = UB[u]["m"]
                    prev_fin = fin_tokB
                    kc0 = 0 if w > 0 else 4
                    exp_tokB = {}
                    for kc in range(kc0, 8):
                        pz = pC if kc % 2 == 0 else pD
                        wz = [m_doneB] + asmB[u]
                        if kc == kc0:
                            wz.append(("ve", rot_done))
                        if kc - kc0 >= 2:
                            wz.append(("se", exp_tokB[kc - 2]))
                        elif prev_expB[0] is not None:
                            wz.append(("se", prev_expB[0]))
                        kinds = []
                        for qt in range(4):
                            if kc < qt or kc > qt + 4:
                                kinds.append("fill")
                            elif kc == qt and w > 0:
                                kinds.append("mcu")
                            elif kc == qt + 4:
                                kinds.append("mcl")
                            else:
                                kinds.append("clean")
                        first_z = True
                        q0 = 0
                        while q0 < 4:
                            if kinds[q0] != "clean":
                                q0 += 1
                                continue
                            q1 = q0
                            while q1 + 1 < 4 and kinds[q1 + 1] == "clean":
                                q1 += 1
                            p.t(lambda e, pz=pz, kc=kc, q0=q0, q1=q1, ktlB=ktlB,
                                qtlB=qtlB: e.matmul(
                                    pz[:, q0 * 128:(q1 + 1) * 128],
                                    ktlB[:, kc * 128:(kc + 1) * 128],
                                    qtlB[:, q0 * 128:(q1 + 1) * 128],
                                    start=True, stop=True),
                                wait=wz if first_z else (), tag=f"B{l}z{w}_{s}{h}{kc}{q0}")
                            first_z = False
                            q0 = q1 + 1
                        ndiag = sum(1 for k_ in kinds if k_ in ("mcu", "mcl"))
                        di = 0
                        for qt in range(4):
                            if kinds[qt] not in ("mcu", "mcl"):
                                continue
                            mask = mcu if kinds[qt] == "mcu" else mcl
                            p.t(lambda e, pz=pz, kc=kc, qt=qt, ktlB=ktlB, qtlB=qtlB: e.matmul(
                                pz[:, qt * 128:(qt + 1) * 128],
                                ktlB[:, kc * 128:(kc + 1) * 128],
                                qtlB[:, qt * 128:(qt + 1) * 128],
                                start=True, stop=False),
                                wait=wz if first_z else (), tag=f"B{l}zd{w}_{s}{h}{kc}{qt}")
                            first_z = False
                            di += 1
                            p.t(lambda e, pz=pz, qt=qt, mask=mask: e.matmul(
                                pz[:, qt * 128:(qt + 1) * 128], identb, mask,
                                start=False, stop=True),
                                inc=(di == ndiag), tag=f"B{l}fd{w}_{s}{h}{kc}{qt}")
                        pe_z = p.now("pe")
                        col0 = max(0, kc - 4) * 128
                        col1 = min(4, kc + 1) * 128
                        exp_tokB[kc] = p.s(lambda e, pz=pz, kc=kc, col0=col0,
                                           col1=col1: e.activation(
                            out=ztB[kc][:, col0:col1], in_=pz[:, col0:col1],
                            func=AF.Exp, scale=8.0),
                            wait=[("pe", pe_z)], inc=True, tag=f"B{l}x{w}_{s}{h}{kc}")
                    prev_expB[0] = exp_tokB[7]
                    if w > 0:
                        wop = [("ve", vones)]
                        if prev_fin is not None:
                            wop.append(("ve", prev_fin))
                        p.t(lambda e: e.matmul(
                            pE[0:65, :], zcB[64:65, 0:65], ktlB2[0][64:65, 0:512],
                            start=True, stop=False),
                            wait=wop, tag=f"B{l}ao{w}_{s}{h}")
                    for kc in range(kc0, 8):
                        slot = (w - 1) % 2 if kc < 4 else w % 2
                        tile = s * 4 + (kc if kc < 4 else kc - 4)
                        wav = [("se", exp_tokB[kc])]
                        if kc == kc0 and w == 0:
                            wav.append(("ve", vones))
                            if prev_fin is not None:
                                wav.append(("ve", prev_fin))
                        col0 = max(0, kc - 4) * 128
                        col1 = min(4, kc + 1) * 128
                        p.t(lambda e, kc=kc, slot=slot, tile=tile, h=h, kc0=kc0,
                            col0=col0, col1=col1, w=w: e.matmul(
                            pE[0:65, col0:col1],
                            vaugB[slot][tile][:, h * 65:(h + 1) * 65],
                            ztB[kc][:, col0:col1],
                            start=(kc == kc0 and w == 0), stop=False),
                            wait=wav, tag=f"B{l}a{w}_{s}{h}{kc}")
                    p.t(lambda e: e.matmul(
                        pE[0:65, :], zcB[64:65, 0:65], ktlB2[0][64:65, 0:512],
                        start=False, stop=True),
                        inc=True, tag=f"B{l}az{w}_{s}{h}")
                    pe_av = p.now("pe")
                    wf1 = [("pe", pe_av)]
                    if h == 0 and ostB_dma[s % 2] is not None:
                        wf1.append(ostB_dma[s % 2])
                    f2a = p.s(lambda e: e.activation(out=rrB[64:65, :], in_=pE[64:65, :],
                                                     func=AF.Ln),
                              wait=wf1, inc=True, tag=f"B{l}lg{w}_{s}{h}")
                    f2 = p.s(lambda e: e.activation(out=rrbB[64:65, :], in_=rrB[64:65, :],
                                                    func=AF.Exp, scale=-1.0),
                             wait=[("se", f2a)], inc=True, tag=f"B{l}rc{w}_{s}{h}")
                    wfb = [("se", f2)]
                    if cur_mevB[0] is not None:
                        wfb.append(("ve", cur_mevB[0]))
                    fb = p.t(lambda e: e.matmul(pFf[0:64, 0:512], ones64[64:65, :],
                                                rrbB[64:65, :], start=True, stop=True),
                             wait=wfb, inc=True, tag=f"B{l}bc{w}_{s}{h}")
                    f4 = p.v(lambda e: e.tensor_copy(out=rbcB, in_=pFf[0:64, 0:512]),
                             wait=[("pe", fb)], inc=True, tag=f"B{l}be{w}_{s}{h}")
                    prev_f4B[0] = f4
                    fin_tokB = p.v(lambda e, h=h, ostB=ostB: e.tensor_mul(
                        out=ostB[:, h * 512:(h + 1) * 512], in0=pE[0:64, :], in1=rbcB),
                        wait=[("ve", f4)], inc=True, tag=f"B{l}o{w}_{s}{h}")
                    if h == NH_B - 1:
                        sem = "sc2" if s % 2 == 0 else "sc3"
                        p.dma("scalar",
                              ocat[0:64, :].rearrange("p (c j) -> p c j", j=1024)
                              [:, :, s * 512:(s + 1) * 512],
                              ostB.rearrange("p (c x j) -> p c x j", c=4, x=2)[:, :, 0, :],
                              sem=sem, wait=[("ve", fin_tokB)], tag=f"B{l}oe{w}{s}")
                        ostB_dma[s % 2] = p.dma("scalar",
                              ocat[64:128, :].rearrange("p (c j) -> p c j", j=1024)
                              [:, :, s * 512:(s + 1) * 512],
                              ostB.rearrange("p (c x j) -> p c x j", c=4, x=2)[:, :, 1, :],
                              sem=sem, tag=f"B{l}oo{w}{s}")

                nuB = SPC * NH_B
                for u in range(nuB):
                    emit_p1B(u)
                    if u >= 1:
                        emit_p1bB(u - 1)
                    if u >= 2:
                        emit_p2B(u - 2)
                    emit_asmB(u)
                emit_p1bB(nuB - 1)
                emit_p2B(nuB - 2)
                emit_p2B(nuB - 1)
                ocat_tok = [("sc2", p.now("sc2")), ("sc3", p.now("sc3"))]

                ev_wo = {}
                ystore = ystore_c[0]
                for s in range(SPC):
                    for d in range(8):
                        key = d % 4
                        bank = [pA[:, 0:512], pA[:, 512:1024], pB[:, 0:512],
                                pB[:, 512:1024]][key]
                        wmm = [ld_woB] + ocat_tok
                        if key in ev_wo:
                            wmm.append(("ve", ev_wo[key]))
                        for k in range(4):
                            p.t(lambda e, k=k, d=d, s=s, bank=bank: e.matmul(
                                bank, woBb[k][:, d * 128:(d + 1) * 128],
                                ocat[:, k * 1024 + s * 512:k * 1024 + (s + 1) * 512],
                                start=(k == 0), stop=(k == 3)),
                                wait=wmm if k == 0 else (), inc=(k == 3),
                                tag=f"B{l}wm{w}_{s}{d}_{k}")
                        pe_mm = p.now("pe")
                        wev = [("pe", pe_mm)]
                        if ystore is not None and d < 4:
                            wev.append(ystore)
                        ev_wo[key] = p.v(lambda e, d=d, s=s, bank=bank: e.tensor_add(
                            out=ynew[:, d * 512:(d + 1) * 512],
                            in0=bank, in1=ytil[s][:, d * 513 + 1:d * 513 + 513]),
                            wait=wev, inc=True, tag=f"B{l}we{w}_{s}{d}")
                    ynew_tok = p.now("ve")
                    ystore = p.dma("sync",
                          p1dst[:, :, s * (SEQ + 1) + 1 + w * 512:s * (SEQ + 1) + 1 + w * 512 + 512]
                          .rearrange("c p j -> p c j"),
                          ynew.rearrange("p (c j) -> p c j", j=512),
                          sem="ld7", wait=[("ve", ynew_tok)], tag=f"B{l}ys{w}{s}")
                ystore_c[0] = ystore
            rb.close()

            if stop == f"p1_{l}":
                finish(E, sems, p1dst)
                return True

            r2a = Region(nc)
            hlnB = [r2a.sb(f"hlB{c}", [128, T], BF16) for c in range(8)]
            ytilB2 = [r2a.sb(f"ytilB{i}", [128, 8 * 513], F32) for i in range(2)]
            ln2 = mk_ln_scratch(r2a, "lnB")
            wcol2 = [r2a.sb(f"w1c{i}", [128, 8 * 256], BF16) for i in range(2)]
            g2st = r2a.sb("g2st", [128, 512], BF16)
            hgst = r2a.sb("hgst", [128, T], BF16)
            p.barrier()
            apl = {}
            for s in range(SPC):
                for tt in range(4):
                    n = s * 4 + tt
                    ytilB = ytilB2[n % 2]
                    wv = [("ve", apl[n - 2])] if (n - 2) in apl else []
                    ld_y = p.dma("sync", ytilB.rearrange("p (c j) -> p c j", j=513),
                          p2src[:, :, s * (SEQ + 1) + tt * 512:s * (SEQ + 1) + tt * 512 + 513]
                          .rearrange("c p j -> p c j"), sem="ld5" if n % 2 == 0 else "ld6",
                          wait=wv, tag=f"C{l}y{s}{tt}")
                    ints = []
                    for c in range(8):
                        off = 0 if c >= 4 else 1
                        ints.append(ytilB[:, c * 513 + off:c * 513 + off + 512])
                    p.fence(("vector", "scalar", "tensor"), {ld_y[0]: ld_y[1]})
                    louts = emit_ln(ln2, ints,
                                    [hlnB[c][:, (s * 4 + tt) * 512:(s * 4 + tt + 1) * 512]
                                     for c in range(8)],
                                    gb=([lng_sb[:, 8 * l + c:8 * l + c + 1] for c in range(8)],
                                        [lnb_sb[:, 8 * l + c:8 * l + c + 1] for c in range(8)]),
                                    tag=f"C{l}n{s}{tt}")
                    apl[n] = louts[-1]
            hln_done = p.now("ve")
            w1_mm_done = {}
            prev_hg = None
            prev_st = None
            for c2 in range(W2C):
                buf = wcol2[c2 % 2]
                wv = []
                if (c2 % 2) in w1_mm_done:
                    wv.append(("pe", w1_mm_done[c2 % 2]))
                ld_w = p.dma("gpsimd", buf.rearrange("p (k j) -> p k j", j=256),
                      ew["w1"][l][:, c2 * 256:(c2 + 1) * 256]
                      .rearrange("(k p) j -> p k j", p=128),
                      sem="gp0" if c2 % 2 == 0 else "gp1", wait=wv, tag=f"C{l}w{c2}")
                hg_tok = {}
                for tt8 in range(8):
                    bka = pA[:, 0:512] if tt8 % 2 == 0 else pB[:, 0:512]
                    bkg = pA[:, 512:1024] if tt8 % 2 == 0 else pB[:, 512:1024]
                    wmm = [ld_w, ("ve", hln_done)]
                    if tt8 >= 2:
                        wmm.append(("ve", hg_tok[tt8 - 2]))
                    elif prev_hg is not None:
                        wmm.append(("ve", prev_hg[6 + tt8]))
                    for k in range(8):
                        p.t(lambda e, k=k, buf=buf, tt8=tt8, bka=bka: e.matmul(
                            bka, buf[:, k * 256:k * 256 + 128],
                            hlnB[k][:, tt8 * 512:(tt8 + 1) * 512],
                            start=(k == 0), stop=(k == 7)),
                            wait=wmm if k == 0 else (), tag=f"C{l}a{c2}_{tt8}_{k}")
                    for k in range(8):
                        p.t(lambda e, k=k, buf=buf, tt8=tt8, bkg=bkg: e.matmul(
                            bkg, buf[:, k * 256 + 128:k * 256 + 256],
                            hlnB[k][:, tt8 * 512:(tt8 + 1) * 512],
                            start=(k == 0), stop=(k == 7)),
                            inc=(k == 7), tag=f"C{l}g{c2}_{tt8}_{k}")
                    pe_mm = p.now("pe")
                    wsg = [("pe", pe_mm)]
                    if tt8 >= 1:
                        wsg.append(("ve", hg_tok[tt8 - 1]))
                    elif prev_hg is not None:
                        wsg.append(("ve", prev_hg[7]))
                    sg = p.s(lambda e, bkg=bkg, c2=c2: e.activation(
                        out=g2st, in_=bkg, func=AF.Gelu,
                        bias=b1_sb[:, 2 * W2C * l + 2 * c2 + 1:2 * W2C * l + 2 * c2 + 2]),
                        wait=wsg, inc=True, tag=f"C{l}gg{c2}_{tt8}")
                    whg = [("se", sg)]
                    if tt8 < 2 and c2 > 0 and prev_st is not None:
                        whg.append(prev_st)
                    hg_tok[tt8] = p.v(lambda e, bka=bka, c2=c2, tt8=tt8: e.scalar_tensor_tensor(
                        out=hgst[:, tt8 * 512:(tt8 + 1) * 512], in0=bka,
                        scalar=b1_sb[:, 2 * W2C * l + 2 * c2:2 * W2C * l + 2 * c2 + 1],
                        in1=g2st, op0=ALU.add, op1=ALU.mult),
                        wait=whg, inc=True, tag=f"C{l}hg{c2}_{tt8}")
                w1_mm_done[c2 % 2] = p.now("pe")
                prev_hg = hg_tok
                prev_st = p.dma("sync", hg_d[c2], hgst, sem="ld7",
                                wait=[("ve", hg_tok[7])], tag=f"C{l}hs{c2}")
            hg_st_tok = prev_st
            r2a.close()

            r2b = Region(nc)
            w2b = [r2b.sb(f"w2b{k}", [128, 1024], BF16) for k in range(W2C)]
            hgt2 = [r2b.sb(f"hgt{i}", [128, W2C * 512], BF16) for i in range(2)]
            ytil22 = [r2b.sb(f"ytil2{i}", [128, 8 * 513], F32) for i in range(2)]
            ynew2 = r2b.sb("ynew2", [128, 8 * 512], F32)
            xat = r2b.sb("xat", [128, 8 * 512], F32)
            outst2 = [r2b.sb(f"outst{i}", [128, 1024], F32) for i in range(2)]
            p.barrier()
            for k in range(W2C):
                ld_w2 = p.dma("gpsimd", w2b[k], ew["w2"][l][k * 128:(k + 1) * 128, :],
                              sem="gp2", tag=f"D{l}w{k}")
            out_dma = None
            od2 = {0: None, 1: None}
            mm_pe_tok = {}
            ev7_tok = {}
            fadd7_tok = {}
            for tt8 in range(8):
                s, tt = tt8 // 4, tt8 % 4
                t0 = tt8 * 512
                hgt = hgt2[tt8 % 2]
                ytil2 = ytil22[tt8 % 2]
                wv = [hg_st_tok]
                if tt8 >= 2:
                    wv.append(("pe", mm_pe_tok[tt8 - 2]))
                ld_hgt = p.dma("sync", hgt.rearrange("p (c j) -> p c j", j=512),
                        hg_d[:, :, t0:t0 + 512].rearrange("c p j -> p c j"),
                        sem="ld4", wait=wv, tag=f"D{l}h{tt8}")
                wy = [("ve", ev7_tok[tt8 - 2])] if tt8 >= 2 else []
                ld_y2 = p.dma("sync", ytil2.rearrange("p (c j) -> p c j", j=513),
                            p2src[:, :, s * (SEQ + 1) + tt * 512:s * (SEQ + 1) + tt * 512 + 513]
                            .rearrange("c p j -> p c j"), sem="ld5", wait=wy,
                            tag=f"D{l}y{tt8}")
                ld_x2 = None
                if l == 1:
                    wx = [("ve", fadd7_tok[tt8 - 1])] if tt8 >= 1 else []
                    ld_x2 = p.dma("sync", xat.rearrange("p (c j) -> p c j", j=512),
                                xa_d[:, :, t0:t0 + 512].rearrange("c p j -> p c j"),
                                sem="ld6", wait=wx, tag=f"D{l}x{tt8}")
                ev2 = {}
                for d in range(8):
                    bank = [pA[:, 0:512], pA[:, 512:1024], pB[:, 0:512], pB[:, 512:1024]][d % 4]
                    wmm = [ld_w2, ld_hgt]
                    if d >= 4:
                        wmm.append(("ve", ev2[d - 4]))
                    for k in range(W2C):
                        p.t(lambda e, k=k, d=d, bank=bank, hgt=hgt: e.matmul(
                            bank, w2b[k][:, d * 128:(d + 1) * 128],
                            hgt[:, k * 512:(k + 1) * 512],
                            start=(k == 0), stop=(k == W2C - 1)),
                            wait=wmm if k == 0 else (), inc=(k == W2C - 1),
                            tag=f"D{l}m{tt8}_{d}_{k}")
                    pe_mm = p.now("pe")
                    wev2 = [("pe", pe_mm)]
                    if d == 0:
                        wev2.append(ld_y2)
                        if tt8 > 0:
                            wev2.append(("ld7", p.now("ld7")))
                    ev2[d] = p.v(lambda e, d=d, bank=bank, l=l, ytil2=ytil2:
                                 e.scalar_tensor_tensor(
                        out=ynew2[:, d * 512:(d + 1) * 512], in0=bank,
                        scalar=b2_sb[:, 8 * l + d:8 * l + d + 1],
                        in1=ytil2[:, d * 513 + 1:d * 513 + 513],
                        op0=ALU.add, op1=ALU.add),
                        wait=wev2, inc=True, tag=f"D{l}e{tt8}_{d}")
                mm_pe_tok[tt8] = p.now("pe")
                ev7_tok[tt8] = ev2[7]
                if l == 0:
                    p.dma("sync",
                          p2dst[:, :, s * (SEQ + 1) + 1 + tt * 512:s * (SEQ + 1) + 1 + tt * 512 + 512]
                          .rearrange("c p j -> p c j"),
                          ynew2.rearrange("p (c j) -> p c j", j=512),
                          sem="ld7", wait=[("ve", ev2[7])], tag=f"D{l}s{tt8}")
                else:
                    fadd = {}
                    for d in range(8):
                        wfa = [("ve", ev2[d])]
                        if d == 0:
                            wfa.append(ld_x2)
                        fadd[d] = p.v(lambda e, d=d: e.tensor_add(
                            out=ynew2[:, d * 512:(d + 1) * 512],
                            in0=ynew2[:, d * 512:(d + 1) * 512],
                            in1=xat[:, d * 512:(d + 1) * 512]),
                            wait=wfa, inc=True, tag=f"D{l}fa{tt8}_{d}")
                    fadd7_tok[tt8] = fadd[7]
                    for i in range(4):
                        evo = {}
                        outst = outst2[i % 2]
                        for chalf in range(2):
                            bank = pA if chalf == 0 else pB
                            for cc in range(4):
                                c = chalf * 4 + cc
                                wtr = []
                                if cc == 0:
                                    wtr = [("ve", fadd[7])]
                                    if out_dma is not None:
                                        wtr.append(out_dma)
                                p.t(lambda e, c=c, cc=cc, i=i, bank=bank: e.transpose(
                                    bank[:, cc * 128:(cc + 1) * 128],
                                    ynew2[:, c * 512 + i * 128:c * 512 + (i + 1) * 128],
                                    identf),
                                    wait=wtr, inc=(cc == 3), tag=f"D{l}tr{tt8}_{i}_{c}")
                            pe_tr = p.now("pe")
                            evo[chalf] = p.v(lambda e, chalf=chalf, bank=bank,
                                             outst=outst: e.tensor_copy(
                                out=outst[:, chalf * 512:(chalf + 1) * 512], in_=bank[:, 0:512]),
                                wait=[("pe", pe_tr)] + ([od2[i % 2]]
                                                        if od2[i % 2] is not None else []),
                                inc=True, tag=f"D{l}te{tt8}_{i}_{chalf}")
                        out_dma = p.dma("sync", out[t0 + i * 128:t0 + (i + 1) * 128, :], outst,
                                        sem="ld7", wait=[("ve", evo[1])], tag=f"D{l}os{tt8}_{i}")
                        od2[i % 2] = out_dma
            r2b.close()

            if stop == f"p2_{l}":
                finish(E, sems, p2dst)
                return True

            return False

        for l in range(2):
            if _layer(l):
                return nc

        p.op("sync", lambda e: e.nop(),
             wait=[(s, p.now(s)) for s in SEMS if s[0:2] in ("ld", "gp", "sc")], tag="fin")
        p.check()
        block = E(nc.Block())
        p.emit(block, sems)
    return nc


def kernel(**inputs):
    w = _prep_weights(inputs)
    x = np.asarray(inputs["L0_attn_inputs"], np.float32)
    nc = build_program()
    in_maps = []
    for core in range(NCORE):
        m = {"xin": np.ascontiguousarray(x[core * SPC:(core + 1) * SPC].reshape(T, DM))}
        m.update(w)
        m.update(CONSTS)
        in_maps.append(m)
    res = run_bass_kernel_spmd(nc, in_maps, core_ids=list(range(NCORE)))
    outs = [np.asarray(res.results[c]["out"]).reshape(SPC, SEQ, DM) for c in range(NCORE)]
    return np.concatenate(outs, axis=0)


# version 15
# speedup vs baseline: 1.0347x; 1.0118x over previous
import contextlib
import numpy as np
import ml_dtypes
import concourse.bass as bass
import concourse.mybir as mybir
from concourse.bass_utils import run_bass_kernel_spmd

F32 = mybir.dt.float32
BF16 = mybir.dt.bfloat16
AF = mybir.ActivationFunctionType
ALU = mybir.AluOpType
AX = mybir.AxisListType
BF = ml_dtypes.bfloat16

NCORE = 8
SEQ = 2048
DM = 1024
SPC = 2
T = SPC * SEQ
NH_A = 16
HD = 64
NH_B = 8
WIN = 512
INNER = 512
HID = 2730
HIDP = 2816
W2C = HIDP // 128
NEG = -1.0e30
EPS = 1e-5
YW = SPC * (SEQ + 1)

ENGINES = ("tensor", "vector", "scalar", "gpsimd", "sync")
SEMS = ("pe", "ve", "se", "ld0", "ld1", "ld2", "ld3", "ld4", "ld5", "ld6", "ld7",
        "gp0", "gp1", "gp2", "sc0", "sc1", "sc2", "sc3")

DBG_STOP = "full"
A3LIM = None
A3SUB = 9
A3QT = 4


class Sched:
    def __init__(self):
        self.st = {e: [] for e in ENGINES}
        self.cnt = {s: 0 for s in SEMS}
        self.pend = {e: [] for e in ENGINES}

    def op(self, eng, fn, wait=(), inc=None, inc_amt=1, tag=""):
        w = list(self.pend[eng]) + list(wait)
        self.pend[eng] = []
        incs = []
        if inc is not None:
            incs.append((inc, inc_amt))
            self.cnt[inc] += inc_amt
        self.st[eng].append((fn, tuple(w), tuple(incs), tag))
        return self.cnt[inc] if inc is not None else None

    def t(self, fn, wait=(), inc=False, tag=""):
        return self.op("tensor", fn, wait, inc="pe" if inc else None, tag=tag)

    def v(self, fn, wait=(), inc=False, tag=""):
        w = list(wait) + [("ve", self.cnt["ve"])]
        return self.op("vector", fn, w, inc="ve", tag=tag)

    def s(self, fn, wait=(), inc=False, tag=""):
        w = list(wait) + [("se", self.cnt["se"])]
        return self.op("scalar", fn, w, inc="se", tag=tag)

    def dma(self, queue, out, in_, sem, wait=(), tag="", slow=False):
        fn = lambda e, out=out, in_=in_, slow=slow: e.dma_start(
            out=out, in_=in_, allow_slow_non_contiguous=slow)
        c = self.op(queue, fn, wait, inc=sem, inc_amt=16, tag=tag)
        return (sem, c)

    def now(self, sem):
        return self.cnt[sem]

    def tok(self):
        return dict(self.cnt)

    def fence(self, engines, token):
        w = [(s, v) for s, v in token.items() if v > 0]
        for e in engines:
            self.pend[e].extend(w)

    def barrier(self):
        self.fence(ENGINES, self.tok())

    def check(self):
        ptr = {e: 0 for e in ENGINES}
        cnt = {s: 0 for s in SEMS}
        total = sum(len(v) for v in self.st.values())
        done = 0
        progress = True
        strict = []
        est = {}
        while progress:
            progress = False
            for e in ENGINES:
                while ptr[e] < len(self.st[e]):
                    fn, waits, incs, tag = self.st[e][ptr[e]]
                    if any(cnt[s] < v for s, v in waits):
                        break
                    for s, v in waits:
                        if s[0:2] in ("ld", "gp", "sc"):
                            if v < cnt[s] and v > est.get(s, 0) and len(strict) < 20:
                                strict.append(f"{e}@{ptr[e]} tag={tag} waits {s}>={v} "
                                              f"issued={cnt[s]} est={est.get(s, 0)}")
                            est[s] = max(est.get(s, 0), v)
                    for s, a in incs:
                        cnt[s] += a
                    ptr[e] += 1
                    done += 1
                    progress = True
        if strict:
            raise RuntimeError("Strict DMA-wait violations:\n  " + "\n  ".join(strict))
        if done != total:
            lines = []
            for e in ENGINES:
                if ptr[e] < len(self.st[e]):
                    fn, waits, incs, tag = self.st[e][ptr[e]]
                    unmet = [(s, v, cnt[s]) for s, v in waits if cnt[s] < v]
                    lines.append(f"  {e}@{ptr[e]}/{len(self.st[e])} tag={tag} unmet={unmet}")
            raise RuntimeError("Sched deadlock:\n" + "\n".join(lines))

    def emit(self, block, sems):
        for ename in ENGINES:
            items = self.st[ename]

            def body(eng, items=items):
                waited = {}
                for fn, waits, incs, tag in items:
                    for s, v in waits:
                        if v > 0 and waited.get(s, -1) < v:
                            eng.wait_ge(sems[s], v)
                            waited[s] = v
                    ins = fn(eng)
                    for s, a in incs:
                        ins.then_inc(sems[s], a)

            getattr(block, ename)(body)


_NAME_CTR = [0]


class Region:
    def __init__(self, nc):
        self.nc = nc
        self.stack = contextlib.ExitStack()

    def sb(self, name, shape, dtype):
        _NAME_CTR[0] += 1
        uname = f"{name}_{_NAME_CTR[0]}"
        return self.stack.enter_context(self.nc.sbuf_tensor(uname, shape, dtype)).ap()

    def close(self):
        self.stack.close()


def _prep_weights(inp):
    w = {}
    w["wqkvA"] = np.ascontiguousarray(np.asarray(inp["L0_attn_Wqkv_w"]).T)
    w["woA"] = np.ascontiguousarray(np.asarray(inp["L0_attn_Wo_w"]).T)
    w["bqkvA"] = np.ascontiguousarray(np.asarray(inp["L0_attn_Wqkv_b"]).reshape(24, 128).T)
    w["boA"] = np.ascontiguousarray(np.asarray(inp["L0_attn_Wo_b"]).reshape(8, 128).T)
    w["ghost"] = np.ascontiguousarray(np.tile(np.asarray(inp["L0_attn_ghost"])[None, :], (128, 1)))
    w["wqkvB"] = np.ascontiguousarray(np.asarray(inp["L0_ffn_wqkv"]))
    w["woB"] = np.ascontiguousarray(np.asarray(inp["L0_ffn_wo"]))
    w["qs"] = np.ascontiguousarray(np.tile(np.asarray(inp["L0_ffn_q_scale"]), (1, 2)).T)
    w["ks"] = np.ascontiguousarray(np.tile(np.asarray(inp["L0_ffn_k_scale"]), (1, 2)).T)
    w["lng"] = np.ascontiguousarray(np.asarray(inp["L0_ffn_ln_g"]).reshape(2, 8, 128).transpose(2, 0, 1).reshape(128, 16))
    w["lnb"] = np.ascontiguousarray(np.asarray(inp["L0_ffn_ln_b"]).reshape(2, 8, 128).transpose(2, 0, 1).reshape(128, 16))
    w1 = np.asarray(inp["L0_ffn_w1"])
    a = np.zeros((2, DM, HIDP), np.float32)
    g = np.zeros((2, DM, HIDP), np.float32)
    a[:, :, :HID] = w1[:, :, :HID]
    g[:, :, :HID] = w1[:, :, HID:]
    w1p = np.zeros((2, DM, 2 * HIDP), np.float32)
    for c in range(W2C):
        w1p[:, :, c * 256:c * 256 + 128] = a[:, :, c * 128:(c + 1) * 128]
        w1p[:, :, c * 256 + 128:c * 256 + 256] = g[:, :, c * 128:(c + 1) * 128]
    w["w1"] = np.ascontiguousarray(w1p)
    b1 = np.asarray(inp["L0_ffn_b1"])
    b1a = np.zeros((2, HIDP), np.float32)
    b1g = np.zeros((2, HIDP), np.float32)
    b1a[:, :HID] = b1[:, :HID]
    b1g[:, :HID] = b1[:, HID:]
    b1p = np.zeros((2, 128, 2 * W2C), np.float32)
    for c in range(W2C):
        b1p[:, :, 2 * c] = b1a[:, c * 128:(c + 1) * 128]
        b1p[:, :, 2 * c + 1] = b1g[:, c * 128:(c + 1) * 128]
    w["b1"] = np.ascontiguousarray(b1p.transpose(1, 0, 2).reshape(128, 2 * 2 * W2C))
    w2 = np.zeros((2, HIDP, DM), np.float32)
    w2[:, :HID, :] = np.asarray(inp["L0_ffn_w2"])
    w["w2"] = np.ascontiguousarray(w2)
    w["b2"] = np.ascontiguousarray(np.asarray(inp["L0_ffn_b2"]).reshape(2, 8, 128).transpose(2, 0, 1).reshape(128, 16))
    return w


def _consts():
    c = {}
    c["identf"] = np.eye(128, dtype=np.float32)
    c["identb"] = np.eye(128, dtype=BF)
    i = np.arange(128)[:, None]
    j = np.arange(128)[None, :]
    c["mcu"] = np.where(j > i, NEG, 0.0).astype(BF)
    c["mcl"] = np.where(j < i, NEG, 0.0).astype(BF)
    c["mcf"] = np.full((128, 128), NEG, BF)
    r = np.zeros((128, 128), np.float32)
    for o in (0, 64):
        for m in range(32):
            r[o + m + 32, o + m] = -1.0
            r[o + m, o + m + 32] = 1.0
    c["r32"] = r.astype(BF)
    c["onescol"] = np.ones((128, 1), BF)
    c["ones1b"] = np.ones((1, 128), BF)
    o64 = np.zeros((65, 64), BF)
    o64[64, :] = 1.0
    c["ones64"] = o64
    e2 = np.zeros((128, 2), BF)
    e2[0:64, 0] = 1.0
    e2[64:128, 1] = 1.0
    c["e2b"] = e2
    bo2 = np.zeros((2, 128), BF)
    bo2[0, 0:64] = 1.0
    bo2[1, 64:128] = 1.0
    c["bo2b"] = bo2
    half = np.arange(0, HD, 2, dtype=np.float64)
    inv_freq = 1.0 / 10000.0 ** (half / HD)
    t = np.arange(SEQ, dtype=np.float64)
    freqs = t[:, None] * inv_freq[None, :]
    freqs = np.concatenate([freqs, freqs], -1)
    base = (half + 0.4 * HD) / (1.4 * HD)
    power = (t - SEQ // 2) / (WIN // 2)
    scale = base[None, :] ** power[:, None]
    scale = np.concatenate([scale, scale], -1)
    cos, sin = np.cos(freqs), np.sin(freqs)
    tabs = np.stack([cos * scale, sin * scale, cos / scale, sin / scale])
    tt = np.zeros((4, 128, 4096), np.float64)
    for wdw in range(4):
        sl = tabs[:, wdw * 512:(wdw + 1) * 512, :].transpose(0, 2, 1)
        sl = np.tile(sl, (1, 2, 1))
        tt[:, :, wdw * 1024:wdw * 1024 + 512] = sl
        tt[:, :, wdw * 1024 + 512:wdw * 1024 + 1024] = sl
    c["tabs"] = tt.astype(BF)
    return c


CONSTS = _consts()


def build_program(stop=None):
    stop = stop or DBG_STOP
    nc = bass.Bass("TRN2", target_bir_lowering=False)
    p = Sched()

    xin = nc.dram_tensor("xin", [T, DM], F32, kind="ExternalInput").ap()
    ew = {}
    for name, shape, dt in [
        ("wqkvA", [DM, 3072], F32), ("woA", [DM, DM], F32),
        ("bqkvA", [128, 24], F32), ("boA", [128, 8], F32), ("ghost", [128, 16], F32),
        ("wqkvB", [2, DM, 1536], F32), ("woB", [2, INNER, DM], F32),
        ("qs", [128, 2], F32), ("ks", [128, 2], F32),
        ("lng", [128, 16], F32), ("lnb", [128, 16], F32),
        ("w1", [2, DM, 2 * HIDP], F32), ("b1", [128, 2 * 2 * W2C], F32),
        ("w2", [2, HIDP, DM], F32), ("b2", [128, 16], F32),
        ("identf", [128, 128], F32), ("identb", [128, 128], BF16),
        ("mcu", [128, 128], BF16), ("mcl", [128, 128], BF16), ("mcf", [128, 128], BF16),
        ("r32", [128, 128], BF16), ("onescol", [128, 1], BF16),
        ("ones1b", [1, 128], BF16), ("ones64", [65, 64], BF16),
        ("e2b", [128, 2], BF16), ("bo2b", [2, 128], BF16),
        ("tabs", [4, 128, 4096], BF16),
    ]:
        ew[name] = nc.dram_tensor(name, shape, dt, kind="ExternalInput").ap()
    out = nc.dram_tensor("out", [T, DM], F32, kind="ExternalOutput").ap()

    xa_d = nc.dram_tensor("xa_d", [8, 128, T], F32, kind="Internal").ap()
    yA_d = nc.dram_tensor("yA_d", [8, 128, YW], F32, kind="Internal").ap()
    yB_d = nc.dram_tensor("yB_d", [8, 128, YW], F32, kind="Internal").ap()
    wbA_q = nc.dram_tensor("wbA_q", [24, 128, 1024], BF16, kind="Internal").ap()
    wbA_o = nc.dram_tensor("wbA_o", [8, 128, 1024], BF16, kind="Internal").ap()
    wbB_q = nc.dram_tensor("wbB_q", [2, 12, 128, 1024], BF16, kind="Internal").ap()
    hg_d = nc.dram_tensor("hg_d", [W2C, 128, T], BF16, kind="Internal").ap()

    dbg = {}
    if stop != "full":
        dbg["xa"] = nc.dram_tensor("dbg_xa", [8, 128, T], F32, kind="ExternalOutput").ap()
        dbg["y"] = nc.dram_tensor("dbg_y", [8, 128, YW], F32, kind="ExternalOutput").ap()

    def finish(E, sems, ydump):
        p.barrier()
        for c in range(8):
            p.dma("sync", dbg["xa"][c], xa_d[c], sem="ld0", tag=f"dxa{c}")
            p.dma("sync", dbg["y"][c], ydump[c], sem="ld0", tag=f"dy{c}")
        p.op("sync", lambda e: e.nop(),
             wait=[(s, p.now(s)) for s in SEMS if s[0:2] in ("ld", "gp", "sc")], tag="fin")
        p.check()
        block = E(nc.Block())
        p.emit(block, sems)

    with contextlib.ExitStack() as ctx:
        E = ctx.enter_context
        sems = {s: E(nc.semaphore(s)) for s in SEMS}

        pr = Region(nc)
        identf = pr.sb("c_identf", [128, 128], F32)
        identb = pr.sb("c_identb", [128, 128], BF16)
        mcu = pr.sb("c_mcu", [128, 128], BF16)
        mcl = pr.sb("c_mcl", [128, 128], BF16)
        mcf = pr.sb("c_mcf", [128, 128], BF16)
        r32 = pr.sb("c_r32", [128, 128], BF16)
        onescol = pr.sb("c_onescol", [128, 1], BF16)
        ones1b = pr.sb("c_ones1b", [1, 128], BF16)
        ones64 = pr.sb("c_ones64", [65, 64], BF16)
        e2b = pr.sb("c_e2b", [128, 2], BF16)
        bo2b = pr.sb("c_bo2b", [2, 128], BF16)
        bqkvA = pr.sb("c_bqkvA", [128, 24], F32)
        boA = pr.sb("c_boA", [128, 8], F32)
        ghost = pr.sb("c_ghost", [128, 16], F32)
        qs_sb = pr.sb("c_qs", [128, 2], F32)
        ks_sb = pr.sb("c_ks", [128, 2], F32)
        lng_sb = pr.sb("c_lng", [128, 16], F32)
        lnb_sb = pr.sb("c_lnb", [128, 16], F32)
        b1_sb = pr.sb("c_b1", [128, 2 * 2 * W2C], F32)
        b2_sb = pr.sb("c_b2", [128, 16], F32)
        zcol = pr.sb("c_zcol", [128, 1], F32)
        epscol = pr.sb("c_eps", [128, 1], F32)

        pA = E(nc.psum_tensor("pA", [128, 1024], F32)).ap()
        pB = E(nc.psum_tensor("pB", [128, 1024], F32)).ap()
        pC = E(nc.psum_tensor("pC", [128, 512], F32)).ap()
        pD = E(nc.psum_tensor("pD", [128, 512], F32)).ap()
        pE = E(nc.psum_tensor("pE", [128, 512], F32)).ap()
        pFb = E(nc.psum_tensor("pFb", [128, 1024], BF16)).ap()
        pFf = pFb.bitcast(F32)

        for name, dst in [("identf", identf), ("identb", identb), ("mcu", mcu),
                          ("mcl", mcl), ("mcf", mcf), ("r32", r32),
                          ("onescol", onescol), ("ones1b", ones1b), ("ones64", ones64),
                          ("e2b", e2b), ("bo2b", bo2b), ("bqkvA", bqkvA),
                          ("boA", boA), ("ghost", ghost)]:
            p.dma("sync", dst, ew[name], sem="ld0", tag="const")
        p.dma("sync", qs_sb, ew["qs"], sem="ld0", tag="const")
        p.dma("sync", ks_sb, ew["ks"], sem="ld0", tag="const")
        p.dma("sync", lng_sb, ew["lng"], sem="ld0", tag="const")
        p.dma("sync", lnb_sb, ew["lnb"], sem="ld0", tag="const")
        p.dma("sync", b1_sb, ew["b1"], sem="ld0", tag="const")
        ld_consts = p.dma("sync", b2_sb, ew["b2"], sem="ld0", tag="const")

        zv = p.v(lambda e: e.memset(zcol, 0.0), wait=[ld_consts], inc=True, tag="zcol")
        p.v(lambda e: e.memset(epscol, EPS), inc=True, tag="eps")
        for yd in (yA_d, yB_d):
            for c in range(8):
                for s in range(SPC):
                    p.dma("sync", yd[c, :, s * (SEQ + 1):s * (SEQ + 1) + 1], zcol,
                          sem="ld1", wait=[("ve", zv)], tag="guard", slow=True)

        wreg = Region(nc)
        wst = [wreg.sb(f"wst{i}", [128, 3072], BF16) for i in range(2)]
        store_tok = []
        for idx, (src, dstb, nd) in enumerate(
            [(ew["wqkvA"], wbA_q, 24), (ew["woA"], wbA_o, 8),
             (ew["wqkvB"][0], wbB_q[0], 12), (ew["wqkvB"][1], wbB_q[1], 12)]):
            for k in range(8):
                n = idx * 8 + k
                buf = wst[n % 2]
                ncols = nd * 128
                w = [store_tok[-2]] if len(store_tok) >= 2 else []
                gpt = p.dma("gpsimd", buf[:, 0:ncols], src[k * 128:(k + 1) * 128, 0:ncols],
                            sem="gp0" if n % 2 == 0 else "gp1", wait=w, tag=f"wbc{idx}_{k}")
                store_tok.append(p.dma("sync",
                      dstb[:, :, k * 128:(k + 1) * 128].rearrange("d p j -> p d j"),
                      buf[:, 0:ncols].rearrange("p (d j) -> p d j", j=128),
                      sem="ld2" if n % 2 == 0 else "ld3",
                      wait=[gpt], tag=f"wbs{idx}_{k}"))
        wbank_tok = [("ld2", p.now("ld2")), ("ld3", p.now("ld3"))]
        wreg.close()

        if stop == "init":
            p.op("sync", lambda e: e.nop(),
                 wait=[(s, p.now(s)) for s in SEMS if s[0:2] in ("ld", "gp", "sc")], tag="fin")
            p.check()
            block = E(nc.Block())
            p.emit(block, sems)
            return nc
        if stop == "dump0":
            finish(E, sems, yA_d)
            return nc

        def emit_ln(ln, in_tiles, out_tiles, gb=None, apply_wait=(), tag="ln"):
            cps, sqs = [], []
            for c in range(8):
                cps.append(p.v(lambda e, c=c: e.tensor_copy(out=ln["xb"][c], in_=in_tiles[c]),
                               inc=True, tag=f"{tag}cp{c}"))
            for c in range(8):
                sqs.append(p.s(lambda e, c=c: e.activation(out=ln["sq"][c], in_=ln["xb"][c],
                                                           func=AF.Square),
                               wait=[("ve", cps[c])], inc=True, tag=f"{tag}sq{c}"))
            for c in range(8):
                p.t(lambda e, c=c: e.matmul(pA[0:1, 0:512], onescol, ln["xb"][c],
                                            start=(c == 0), stop=(c == 7)),
                    wait=[("ve", cps[c])], tag=f"{tag}mmA{c}")
            for c in range(8):
                p.t(lambda e, c=c: e.matmul(pA[0:1, 512:1024], onescol, ln["sq"][c],
                                            start=(c == 0), stop=(c == 7)),
                    wait=[("se", sqs[c])], inc=(c == 7), tag=f"{tag}mmB{c}")
            pe_stats = p.now("pe")
            v1 = p.v(lambda e: e.tensor_scalar_mul(out=ln["rU"], in0=pA[0:1, 0:512],
                                                   scalar1=1.0 / DM),
                     wait=[("pe", pe_stats)], inc=True, tag=f"{tag}mu")
            v2 = p.v(lambda e: e.tensor_scalar_mul(out=ln["rV"], in0=pA[0:1, 512:1024],
                                                   scalar1=1.0 / DM),
                     inc=True, tag=f"{tag}ms")
            v3 = p.v(lambda e: e.scalar_tensor_tensor(out=ln["rA"], in0=ln["rU"], scalar=-1.0,
                                                      in1=ln["rU"], op0=ALU.mult, op1=ALU.mult),
                     wait=[("ve", v1)], inc=True, tag=f"{tag}mu2")
            v4 = p.v(lambda e: e.tensor_add(out=ln["rV"], in0=ln["rV"], in1=ln["rA"]),
                     wait=[("ve", v3), ("ve", v2)], inc=True, tag=f"{tag}var")
            s0 = p.s(lambda e: e.activation(out=ln["rB"], in_=ln["rV"], func=AF.Ln,
                                            bias=epscol[0:1, :]),
                     wait=[("ve", v4)], inc=True, tag=f"{tag}lg")
            s1 = p.s(lambda e: e.activation(out=ln["rV"], in_=ln["rB"], func=AF.Exp,
                                            scale=-0.5),
                     wait=[("se", s0)], inc=True, tag=f"{tag}sd")
            v6 = p.v(lambda e: e.tensor_copy(out=ln["rM"], in_=ln["rV"]),
                     wait=[("se", s1)], inc=True, tag=f"{tag}rstdb")
            v7 = p.v(lambda e: e.tensor_mul(out=ln["rA"], in0=ln["rU"], in1=ln["rV"]),
                     wait=[("ve", v6)], inc=True, tag=f"{tag}mrs")
            v8 = p.v(lambda e: e.tensor_copy(out=ln["rN"], in_=ln["rA"]),
                     wait=[("ve", v7)], inc=True, tag=f"{tag}mrsb")
            p.t(lambda e: e.matmul(pB[:, 0:512], ones1b, ln["rM"], start=True, stop=True),
                wait=[("ve", v6)], tag=f"{tag}bcA")
            p.t(lambda e: e.matmul(pB[:, 512:1024], ones1b, ln["rN"], start=True, stop=True),
                wait=[("ve", v8)], inc=True, tag=f"{tag}bcB")
            pe_bc = p.now("pe")
            outs = []
            for c in range(8):
                w = [("pe", pe_bc)] + (list(apply_wait) if c == 0 else [])
                p.v(lambda e, c=c: e.tensor_mul(out=in_tiles[c], in0=in_tiles[c],
                                                in1=pB[:, 0:512]),
                    wait=w, tag=f"{tag}ap1_{c}")
                if gb is None:
                    outs.append(p.v(
                        lambda e, c=c: e.scalar_tensor_tensor(
                            out=out_tiles[c], in0=pB[:, 512:1024], scalar=-1.0,
                            in1=in_tiles[c], op0=ALU.mult, op1=ALU.add),
                        inc=True, tag=f"{tag}ap2_{c}"))
                else:
                    p.v(lambda e, c=c: e.scalar_tensor_tensor(
                        out=in_tiles[c], in0=pB[:, 512:1024], scalar=-1.0,
                        in1=in_tiles[c], op0=ALU.mult, op1=ALU.add),
                        tag=f"{tag}ap2_{c}")
                    outs.append(p.v(
                        lambda e, c=c: e.tensor_scalar(
                            out=out_tiles[c], in0=in_tiles[c], scalar1=gb[0][c],
                            scalar2=gb[1][c], op0=ALU.mult, op1=ALU.add),
                        inc=True, tag=f"{tag}ap3_{c}"))
            return outs

        def mk_ln_scratch(rg, pfx):
            return {
                "xb": [rg.sb(f"{pfx}xb{c}", [128, 512], BF16) for c in range(8)],
                "sq": [rg.sb(f"{pfx}sq{c}", [128, 512], BF16) for c in range(8)],
                "rU": rg.sb(f"{pfx}rU", [1, 512], F32),
                "rV": rg.sb(f"{pfx}rV", [1, 512], F32),
                "rA": rg.sb(f"{pfx}rA", [1, 512], F32),
                "rB": rg.sb(f"{pfx}rB", [1, 512], F32),
                "rM": rg.sb(f"{pfx}rM", [1, 512], BF16),
                "rN": rg.sb(f"{pfx}rN", [1, 512], BF16),
            }

        def _stageA(s):
            rh = Region(nc)
            QT = [rh.sb(f"QT{c}", [128, SEQ], BF16) for c in range(8)]
            KT = [rh.sb(f"KT{c}", [128, SEQ], BF16) for c in range(8)]
            Vaug = [rh.sb(f"Vg{tc}", [128, NH_A * 65], BF16) for tc in range(16)]

            r1 = Region(nc)
            hlnA = [r1.sb(f"hlA{c}", [128, SEQ], BF16) for c in range(8)]
            inst = [r1.sb(f"inst{i}", [128, 1024], F32) for i in range(4)]
            xf = [r1.sb(f"xfA{c}", [128, 512], F32) for c in range(8)]
            ln1 = mk_ln_scratch(r1, "lnA")
            p.barrier()
            prev_ln = None
            for tt in range(4):
                t0 = s * SEQ + tt * 512
                lds = []
                for i in range(4):
                    wv = [("pe", p.now("pe")), ("ve", p.now("ve"))] if tt > 0 else []
                    lds.append(p.dma("sync", inst[i], xin[t0 + i * 128:t0 + (i + 1) * 128, :],
                               sem="ld4", wait=wv if i == 0 else (), tag=f"A1ld{tt}{i}"))
                evs = []
                for c in range(8):
                    bank = pA if c % 2 == 0 else pB
                    w = [lds[3]]
                    if c >= 2:
                        w.append(("ve", evs[c - 2]))
                    elif prev_ln is not None:
                        w.append(("ve", prev_ln))
                    for i in range(4):
                        p.t(lambda e, c=c, i=i, bank=bank: e.transpose(
                            bank[:, i * 128:(i + 1) * 128],
                            inst[i][:, c * 128:(c + 1) * 128], identf),
                            wait=w if i == 0 else (), inc=(i == 3), tag=f"A1tr{tt}_{c}_{i}")
                    pe_tr = p.now("pe")
                    evs.append(p.v(lambda e, c=c, bank=bank: e.tensor_copy(
                        out=xf[c], in_=bank[:, 0:512]),
                        wait=[("pe", pe_tr)], inc=True, tag=f"A1ev{tt}{c}"))
                for c in range(8):
                    st_xf = p.dma("sync", xa_d[c, :, t0:t0 + 512], xf[c],
                                  sem="ld5", wait=[("ve", evs[c])], tag=f"A1st{tt}{c}")
                p.fence(("tensor",), {"ve": evs[-1]})
                lnouts = emit_ln(ln1, xf, [hlnA[c][:, tt * 512:(tt + 1) * 512] for c in range(8)],
                                 apply_wait=[st_xf], tag=f"A1n{s}{tt}")
                prev_ln = lnouts[-1]
            if stop == "A1":
                finish(E, sems, yA_d)
                return True

            r2 = Region(nc)
            wcol = [r2.sb(f"wcA{i}", [128, 1024], BF16) for i in range(2)]
            vstage = [r2.sb(f"vsA{i}", [128, 512], BF16) for i in range(4)]
            p.barrier()
            prev_mm_done = {}
            prev_ev = {}
            last_vaug = None
            for d in range(24):
                buf = wcol[d % 2]
                w = list(wbank_tok)
                if (d % 2) in prev_mm_done:
                    w.append(("pe", prev_mm_done[d % 2]))
                ld_w = p.dma("sync", buf, wbA_q[d], sem="ld2" if d % 2 == 0 else "ld3",
                             wait=w, tag=f"A2w{d}")
                for tt in range(4):
                    bank = [pA[:, 0:512], pA[:, 512:1024], pB[:, 0:512], pB[:, 512:1024]][tt]
                    wmm = [ld_w]
                    if tt in prev_ev:
                        wmm.append(("ve", prev_ev[tt]))
                    for k in range(8):
                        p.t(lambda e, tt=tt, k=k, bank=bank, buf=buf: e.matmul(
                            bank, buf[:, k * 128:(k + 1) * 128],
                            hlnA[k][:, tt * 512:(tt + 1) * 512],
                            start=(k == 0), stop=(k == 7)),
                            wait=wmm if k == 0 else (), inc=(k == 7), tag=f"A2m{d}_{tt}_{k}")
                    pe_mm = p.now("pe")
                    if d < 8:
                        dst = QT[d][:, tt * 512:(tt + 1) * 512]
                    elif d < 16:
                        dst = KT[d - 8][:, tt * 512:(tt + 1) * 512]
                    else:
                        dst = vstage[tt]
                    wev = [("pe", pe_mm)]
                    if d >= 16 and last_vaug is not None:
                        wev.append(("ve", last_vaug))
                    prev_ev[tt] = p.v(lambda e, dst=dst, bank=bank, d=d: e.tensor_scalar(
                        out=dst, in0=bank, scalar1=bqkvA[:, d:d + 1], scalar2=None,
                        op0=ALU.add),
                        wait=wev, inc=True, tag=f"A2e{d}_{tt}")
                prev_mm_done[d % 2] = p.now("pe")
                if d >= 16:
                    vc = d - 16
                    for tt in range(4):
                        for i in range(4):
                            tc = tt * 4 + i
                            wtr = [("ve", prev_ev[tt])]
                            if last_vaug is not None:
                                wtr.append(("ve", last_vaug))
                            p.t(lambda e, tt=tt, i=i: e.transpose(
                                pFb[:, i * 128:(i + 1) * 128],
                                vstage[tt][:, i * 128:(i + 1) * 128], identb),
                                wait=wtr, inc=True, tag=f"A2vt{d}_{tt}_{i}")
                            pe_t = p.now("pe")
                            last_vaug = p.v(
                                lambda e, tc=tc, vc=vc, i=i: e.tensor_copy(
                                    out=Vaug[tc].rearrange("p (h x) -> p h x", h=NH_A)
                                    [:, 2 * vc:2 * vc + 2, 0:64],
                                    in_=pFb[:, i * 128:(i + 1) * 128]
                                    .rearrange("p (h x) -> p h x", h=2)),
                                wait=[("pe", pe_t)], inc=True, tag=f"A2ve{d}_{tt}_{i}")
            ones_tok = None
            for tc in range(16):
                ones_tok = p.v(lambda e, tc=tc: e.memset(
                    Vaug[tc].rearrange("p (h x) -> p h x", h=NH_A)[:, :, 64:65], 1.0),
                    wait=[("ve", last_vaug)] if tc == 0 else (), inc=True, tag=f"A2on{tc}")
            r2.close()
            r1.close()
            if stop == "A2":
                finish(E, sems, yA_d)
                return True

            roA = Region(nc)
            o_all = [roA.sb(f"oal{c}", [128, SEQ], BF16) for c in range(8)]
            r3 = Region(nc)
            ztA = [r3.sb(f"ztA{i}", [128, 512], BF16) for i in range(16)]
            ktl2 = [r3.sb(f"ktlA{i}", [65, SEQ], BF16) for i in range(2)]
            qtl2 = [r3.sb(f"qtlA{i}", [65, SEQ], BF16) for i in range(2)]
            ost2 = [r3.sb(f"ostA{i}", [64, SEQ], BF16) for i in range(2)]
            mst2 = [r3.sb(f"mstA{i}", [128, 8], F32) for i in range(2)]
            m4b2 = [r3.sb(f"m4A{i}", [4, 128], BF16) for i in range(2)]
            zc65 = r3.sb("zc65", [65, 65], BF16)
            rrow = r3.sb("rrowA", [65, 512], F32)
            rrowb = r3.sb("rrbA", [65, 512], BF16)
            rbcs = r3.sb("rbcA", [64, 512], F32)
            p.barrier()
            zc_tok = p.v(lambda e: e.memset(zc65[:, :], 0.0), inc=True, tag="A3z0")
            k1a = p.v(lambda e: e.memset(ktl2[0][64:65, :], 1.0), inc=True, tag="A3k1")
            k1b = p.v(lambda e: e.memset(ktl2[1][64:65, :], 1.0), inc=True, tag="A3k2")
            qkv_done = p.now("ve")

            units = [(h, qb) for h in range(NH_A) for qb in range(4)]
            bk4 = [pA[:, 0:512], pA[:, 512:1024], pB[:, 0:512], pB[:, 512:1024]]
            gtoks = []
            U = {}
            fin_tok = None
            ost_dma = {0: None, 1: None}
            asm_tok = {}
            prev_mplumb = None
            prev_f4 = [None]
            prev_exp = [None]
            cur_mev = [None]
            m4_dma = {0: None, 1: None}

            def emit_asm(h):
                ch = h // 2
                hb = h % 2
                ktl, qtl = ktl2[hb], qtl2[hb]
                wasm = [("ve", qkv_done), ("pe", p.now("pe")), ("se", p.now("se"))]
                if h % 2 == 0:
                    p.s(lambda e, ch=ch, ktl=ktl: e.activation(
                        out=ktl[0:64, :], in_=KT[ch][0:64, :], func=AF.Copy),
                        wait=wasm, inc=True, tag=f"A3kc{h}")
                    qa = p.s(lambda e, ch=ch, qtl=qtl: e.activation(
                        out=qtl[0:64, :], in_=QT[ch][0:64, :], func=AF.Copy),
                        inc=True, tag=f"A3qc{h}")
                    asm_tok[h] = [("se", qa)]
                else:
                    p.dma("scalar", ktl[0:64, :], KT[ch][64:128, :], sem="sc0",
                          wait=wasm, tag=f"A3kd{h}")
                    asm_tok[h] = [p.dma("scalar", qtl[0:64, :], QT[ch][64:128, :],
                                        sem="sc0", tag=f"A3qd{h}")]

            def emit_p1(u):
                h, qb = units[u]
                ch = h // 2
                rbase = 64 * (h % 2)
                ktl, qtl = ktl2[h % 2], qtl2[h % 2]
                mst = mst2[u % 2]
                st = {}
                qk_red = {}
                for qt in range(4):
                    qtg = qb * 4 + qt
                    width = qb * 512 + (qt + 1) * 128
                    nmst = 0
                    pos = 0
                    red = None
                    while pos < width:
                        seg = min(512, width - pos)
                        pq = bk4[len(gtoks) % 4]
                        wseg = []
                        if len(gtoks) >= 4:
                            wseg = [("ve", gtoks[-4])]
                        final_seg = (pos + seg == width)
                        pieces = []
                        off = 0
                        while off < seg:
                            nn = min(512, seg - off)
                            if final_seg and off + nn == seg and nn > 128:
                                pieces.append((off, nn - 128, False))
                                pieces.append((off + nn - 128, 128, True))
                            elif final_seg and off + nn == seg:
                                pieces.append((off, nn, True))
                            else:
                                pieces.append((off, nn, False))
                            off += nn
                        first = True
                        for off, nn, msk in pieces:
                            last = (off + nn == seg) and not msk
                            p.t(lambda e, pq=pq, off=off, nn=nn, pos=pos, qtg=qtg,
                                ch=ch, rbase=rbase, msk=msk: e.matmul(
                                    pq[:, off:off + nn],
                                    QT[ch][rbase:rbase + 64, qtg * 128:(qtg + 1) * 128],
                                    KT[ch][rbase:rbase + 64, pos + off:pos + off + nn],
                                    start=True, stop=not msk),
                                wait=wseg if first else (), inc=last,
                                tag=f"A3s{h}_{qtg}_{pos+off}")
                            first = False
                            if msk:
                                p.t(lambda e, pq=pq, off=off, nn=nn: e.matmul(
                                    pq[:, off:off + nn], identb, mcu,
                                    start=False, stop=True),
                                    inc=True, tag=f"A3dm{h}_{qtg}")
                        pe_seg = p.now("pe")
                        single = (width <= 512)
                        red = p.v(lambda e, pq=pq, seg=seg, nmst=nmst, mst=mst, qt=qt,
                                  single=single: e.reduce_max(
                            out=mst[:, 4 + qt:5 + qt] if single else mst[:, nmst:nmst + 1],
                            in_=pq[:, 0:seg], axis=AX.X, negate=True),
                            wait=[("pe", pe_seg)], inc=True, tag=f"A3r{h}_{qtg}_{nmst}")
                        gtoks.append(red)
                        nmst += 1
                        pos += seg
                    if nmst == 1:
                        cmb = red
                    else:
                        cmb = p.v(lambda e, nmst=nmst, mst=mst, qt=qt: e.tensor_reduce(
                            out=mst[:, 4 + qt:5 + qt], in_=mst[:, 0:nmst],
                            axis=AX.X, op=ALU.min),
                            wait=[("ve", red)], inc=True, tag=f"A3c{h}_{qtg}")
                    qk_red[qt] = cmb
                st["qk_red"] = qk_red
                st["mst"] = mst
                U[u] = st

            def emit_p1b(u):
                nonlocal prev_mplumb
                h, qb = units[u]
                qtl = qtl2[h % 2]
                mst = U[u]["mst"]
                qk_red = U[u]["qk_red"]
                m4 = m4b2[u % 2]
                wtm = [("ve", qk_red[qt]) for qt in range(4)]
                if cur_mev[0] is not None:
                    wtm.append(("ve", cur_mev[0]))
                if prev_f4[0] is not None:
                    wtm.append(("ve", prev_f4[0]))
                ptm = p.t(lambda e, mst=mst: e.transpose(
                    pFf[0:4, 0:128], mst[:, 4:8], identf),
                    wait=wtm, inc=True, tag=f"A3t{h}_{qb}")
                wm4 = [("pe", ptm)]
                if m4_dma[u % 2] is not None:
                    wm4.append(m4_dma[u % 2])
                mev = p.v(lambda e, m4=m4: e.tensor_copy(out=m4, in_=pFf[0:4, 0:128]),
                          wait=wm4, inc=True, tag=f"A3m{h}_{qb}")
                cur_mev[0] = mev
                m_done = p.dma("gpsimd", qtl[64:65, qb * 512:(qb + 1) * 512], m4,
                               sem="gp0" if u % 2 == 0 else "gp1",
                               wait=[("ve", mev)], tag=f"A3mv{h}_{qb}")
                m4_dma[u % 2] = m_done
                prev_mplumb = m_done
                U[u]["m_done"] = m_done

            def emit_p2(u):
                nonlocal fin_tok, prev_mplumb
                h, qb = units[u]
                ch = h // 2
                rbase = 64 * (h % 2)
                ktl, qtl = ktl2[h % 2], qtl2[h % 2]
                ost = ost2[h % 2]
                m_done = U[u]["m_done"]
                prev_fin = fin_tok
                nkc = 4 * qb + 4
                exp_tok = {}
                for kc in range(nkc):
                    pz = pC if kc % 2 == 0 else pD
                    wz = [m_done] + asm_tok[h]
                    if kc == 0:
                        wz.append(("ve", qkv_done))
                    if kc >= 2:
                        wz.append(("se", exp_tok[kc - 2]))
                    elif prev_exp[0] is not None:
                        wz.append(("se", prev_exp[0]))
                    rel = kc - 4 * qb
                    first_z = True
                    q0 = max(0, rel + 1)
                    if q0 < 4:
                        p.t(lambda e, pz=pz, kc=kc, qb=qb, q0=q0, ktl=ktl, qtl=qtl: e.matmul(
                            pz[:, q0 * 128:512], ktl[:, kc * 128:(kc + 1) * 128],
                            qtl[:, qb * 512 + q0 * 128:(qb + 1) * 512],
                            start=True, stop=True),
                            wait=wz if first_z else (), inc=(rel < 0),
                            tag=f"A3z{h}_{qb}_{kc}")
                        first_z = False
                    if 0 <= rel < 4:
                        p.t(lambda e, pz=pz, kc=kc, qb=qb, rel=rel, ktl=ktl, qtl=qtl: e.matmul(
                            pz[:, rel * 128:(rel + 1) * 128],
                            ktl[:, kc * 128:(kc + 1) * 128],
                            qtl[:, qb * 512 + rel * 128:qb * 512 + (rel + 1) * 128],
                            start=True, stop=False),
                            wait=wz if first_z else (), tag=f"A3zd{h}_{qb}_{kc}")
                        p.t(lambda e, pz=pz, rel=rel: e.matmul(
                            pz[:, rel * 128:(rel + 1) * 128], identb, mcl,
                            start=False, stop=True),
                            inc=True, tag=f"A3xd{h}_{qb}_{kc}")
                    pe_z = p.now("pe")
                    col0 = max(0, rel) * 128
                    exp_tok[kc] = p.s(lambda e, pz=pz, kc=kc, col0=col0: e.activation(
                        out=ztA[kc][:, col0:512], in_=pz[:, col0:512],
                        func=AF.Exp, scale=0.125),
                        wait=[("pe", pe_z)], inc=True, tag=f"A3e{h}_{qb}_{kc}")
                prev_exp[0] = exp_tok[nkc - 1]
                for kc in range(nkc):
                    wav = [("se", exp_tok[kc])]
                    if kc == 0:
                        wav.append(("ve", ones_tok))
                        if fin_tok is not None:
                            wav.append(("ve", fin_tok))
                    col0 = max(0, kc - 4 * qb) * 128
                    p.t(lambda e, kc=kc, h=h, col0=col0: e.matmul(
                        pE[0:65, col0:512],
                        Vaug[kc][:, h * 65:(h + 1) * 65], ztA[kc][:, col0:512],
                        start=(kc == 0), stop=False),
                        wait=wav, tag=f"A3av{h}_{qb}_{kc}")
                p.t(lambda e: e.matmul(
                    pE[0:65, :], zc65[64:65, 0:65], ktl2[0][64:65, 0:512],
                    start=False, stop=True),
                    inc=True, tag=f"A3avz{h}_{qb}")
                pe_av = p.now("pe")
                f2a = p.s(lambda e, h=h: e.activation(
                    out=rrow[64:65, :], in_=pE[64:65, :], func=AF.Ln,
                    bias=ghost[64:65, h:h + 1]),
                    wait=[("pe", pe_av)] + ([ost_dma[h % 2]] if (qb == 0 and ost_dma[h % 2])
                                            else []),
                    inc=True, tag=f"A3lg{h}_{qb}")
                f2 = p.s(lambda e: e.activation(out=rrowb[64:65, :], in_=rrow[64:65, :],
                                                func=AF.Exp, scale=-1.0),
                         wait=[("se", f2a)], inc=True, tag=f"A3rc{h}_{qb}")
                wfb = [("se", f2)]
                if cur_mev[0] is not None:
                    wfb.append(("ve", cur_mev[0]))
                fb = p.t(lambda e: e.matmul(pFf[0:64, 0:512], ones64[64:65, :],
                                            rrowb[64:65, :], start=True, stop=True),
                         wait=wfb, inc=True, tag=f"A3bc{h}_{qb}")
                f4 = p.v(lambda e: e.tensor_copy(out=rbcs, in_=pFf[0:64, 0:512]),
                         wait=[("pe", fb)], inc=True, tag=f"A3be{h}_{qb}")
                prev_f4[0] = f4
                fin_tok = p.v(lambda e, qb=qb, ost=ost: e.tensor_mul(
                    out=ost[:, qb * 512:(qb + 1) * 512], in0=pE[0:64, :], in1=rbcs),
                    wait=[("ve", f4)], inc=True, tag=f"A3o{h}_{qb}")
                if qb == 3:
                    ost_dma[h % 2] = p.dma(
                        "scalar", o_all[ch][rbase:rbase + 64, :], ost,
                        sem="sc1" if h % 2 == 0 else "sc2",
                        wait=[("ve", fin_tok)], tag=f"A3om{h}")

            for u in range(len(units)):
                h, qb = units[u]
                if u == 0:
                    emit_asm(0)
                emit_p1(u)
                if u >= 1:
                    emit_p1b(u - 1)
                if u >= 2:
                    emit_p2(u - 2)
                if qb == 3 and h + 1 < NH_A:
                    emit_asm(h + 1)
            emit_p1b(len(units) - 1)
            emit_p2(len(units) - 2)
            emit_p2(len(units) - 1)
            oall_tok = ("sc1", p.now("sc1"))
            oall_tok2 = ("sc2", p.now("sc2"))
            r3.close()
            if stop == "A3":
                finish(E, sems, yA_d)
                return True

            r4 = Region(nc)
            woAb = [r4.sb(f"woAb{k}", [128, 1024], BF16) for k in range(8)]
            xft = r4.sb("xft", [128, 8 * 512], F32)
            y0t = r4.sb("y0t", [128, 8 * 512], F32)
            ln4 = mk_ln_scratch(r4, "lnY")
            p.barrier()
            for k in range(8):
                ld_wo = p.dma("sync", woAb[k], wbA_o[k], sem="ld4",
                              wait=wbank_tok if k == 0 else (), tag=f"A4w{k}")
            prev_lnout = None
            for tt in range(4):
                t0 = s * SEQ + tt * 512
                wx = [("ve", p.now("ve")), ("ld5", p.now("ld5"))] if tt > 0 else []
                ld_x = p.dma("sync", xft.rearrange("p (c j) -> p c j", j=512),
                             xa_d[:, :, t0:t0 + 512].rearrange("c p j -> p c j"),
                             sem="ld5", wait=wx, tag=f"A4x{tt}")
                ev4 = {}
                for d in range(8):
                    bank = [pA[:, 0:512], pA[:, 512:1024], pB[:, 0:512], pB[:, 512:1024]][d % 4]
                    wmm = [ld_wo, oall_tok, oall_tok2]
                    if prev_lnout is not None:
                        wmm.append(("ve", prev_lnout))
                    if d >= 4:
                        wmm.append(("ve", ev4[d - 4]))
                    for k in range(8):
                        p.t(lambda e, d=d, k=k, bank=bank, tt=tt: e.matmul(
                            bank, woAb[d][:, k * 128:(k + 1) * 128],
                            o_all[k][:, tt * 512:(tt + 1) * 512],
                            start=(k == 0), stop=(k == 7)),
                            wait=wmm if k == 0 else (), inc=(k == 7), tag=f"A4m{tt}_{d}_{k}")
                    pe_mm = p.now("pe")
                    ev4[d] = p.v(lambda e, d=d, bank=bank: e.scalar_tensor_tensor(
                        out=xft[:, d * 512:(d + 1) * 512], in0=bank,
                        scalar=boA[:, d:d + 1], in1=xft[:, d * 512:(d + 1) * 512],
                        op0=ALU.add, op1=ALU.add),
                        wait=[("pe", pe_mm), ld_x], inc=True, tag=f"A4e{tt}_{d}")
                st_x = p.dma("sync", xa_d[:, :, t0:t0 + 512].rearrange("c p j -> p c j"),
                             xft.rearrange("p (c j) -> p c j", j=512),
                             sem="ld5", wait=[("ve", ev4[7])], tag=f"A4sx{tt}")
                p.fence(("tensor",), {"ve": ev4[7]})
                ln_out = emit_ln(ln4, [xft[:, d * 512:(d + 1) * 512] for d in range(8)],
                                 [y0t[:, d * 512:(d + 1) * 512] for d in range(8)],
                                 apply_wait=[st_x], tag=f"A4n{s}{tt}")
                prev_lnout = ln_out[-1]
                p.dma("sync",
                      yA_d[:, :, s * (SEQ + 1) + 1 + tt * 512:s * (SEQ + 1) + 1 + tt * 512 + 512]
                      .rearrange("c p j -> p c j"),
                      y0t.rearrange("p (c j) -> p c j", j=512),
                      sem="ld5", wait=[("ve", ln_out[-1])], tag=f"A4sy{tt}")
            r4.close()
            roA.close()
            rh.close()
            return False

        for s in range(SPC):
            if _stageA(s):
                return nc

        if stop == "A":
            finish(E, sems, yA_d)
            return nc

        def _layer(l):
            p1src, p1dst = yA_d, yB_d
            p2src, p2dst = yB_d, yA_d

            rb = Region(nc)
            ytil2w = [[rb.sb(f"ytB{sl}_{s}", [128, 8 * 513], F32) for s in range(SPC)]
                      for sl in range(2)]
            hq = [rb.sb(f"hqB{c}", [128, 1024], BF16) for c in range(8)]
            qraw = [rb.sb(f"qrB{c}", [128, 1024], BF16) for c in range(4)]
            kring = [[rb.sb(f"kgB{sl}_{c}", [128, 1024], BF16) for c in range(4)]
                     for sl in range(2)]
            vaugB = [[rb.sb(f"vgB{sl}_{i}", [128, NH_B * 65], BF16) for i in range(8)]
                     for sl in range(2)]
            tabst = [rb.sb(f"tbB{i}", [128, 1024], BF16) for i in range(4)]
            rnr2 = rb.sb("rnr2", [2, 1024], F32)
            rnrb = rb.sb("rnrb", [2, 1024], BF16)
            e24 = rb.sb("e24", [2, 1], F32)
            scr1 = hq[0]
            scr2 = hq[1]
            sqB = scr2
            ktlB2 = [rb.sb(f"ktlB{i}", [65, 1024], BF16) for i in range(2)]
            zcB = rb.sb("zcB", [65, 65], BF16)
            qtlB2 = [rb.sb(f"qtlB{i}", [65, 512], BF16) for i in range(2)]
            ztB = [rb.sb(f"ztB{i}", [128, (min(4, i + 1) - max(0, i - 4)) * 128], BF16)
                   for i in range(8)]
            ostB2 = [rb.sb(f"ostB{i}", [64, NH_B * 512], BF16) for i in range(2)]
            ocat = rb.sb("ocat", [128, 4 * 1024], BF16)
            vraw = [ocat[:, c * 1024:(c + 1) * 1024] for c in range(4)]
            ynew = rb.sb("ynew", [128, 8 * 512], F32)
            mstB2 = [rb.sb(f"mstB{i}", [128, 8], F32) for i in range(2)]
            m4bB2 = [rb.sb(f"m4B{i}", [4, 128], BF16) for i in range(2)]
            rrB = rb.sb("rrB", [65, 512], F32)
            rrbB = rb.sb("rrbB", [65, 512], BF16)
            rbcB = rb.sb("rbcB", [64, 512], F32)
            wcolB = [rb.sb(f"wcB{i}", [128, 1024], BF16) for i in range(2)]
            woBb = [rb.sb(f"woBb{k}", [128, 1024], BF16) for k in range(4)]
            p.barrier()
            p.v(lambda e: e.memset(zcB[:, :], 0.0), inc=True, tag=f"B{l}z0")
            p.v(lambda e: e.memset(e24[:, :], 1e-24), inc=True, tag=f"B{l}e24")
            p.v(lambda e: e.memset(ktlB2[0][64:65, :], 1.0), inc=True, tag=f"B{l}k1")
            konceB = p.v(lambda e: e.memset(ktlB2[1][64:65, :], 1.0), inc=True, tag=f"B{l}k2")
            ostB_dma = {0: None, 1: None}
            for k in range(4):
                ld_woB = p.dma("gpsimd", woBb[k], ew["woB"][l][k * 128:(k + 1) * 128, :],
                               sem="gp2", tag=f"B{l}wo{k}")

            fin_tokB = None
            mm_doneB = {}
            UB = {}
            asmB = {}
            prev_mpl = [None]
            prev_f4B = [None]
            prev_expB = [None]
            cur_mevB = [None]
            vt_hist = []
            m4_dmaB = {0: None, 1: None}
            ytil_tok = {}
            ystore_c = [None]
            kq_cell = [None]
            for w in range(4):
                pe_all = p.now("pe")
                se_all = p.now("se")
                ve_all = p.now("ve")
                sc0T = ("sc0", p.now("sc0"))
                sc1T = ("sc1", p.now("sc1"))
                wtab = [("ve", kq_cell[0])] if kq_cell[0] is not None else []
                for i in range(4):
                    tab_ld = p.dma("gpsimd", tabst[i], ew["tabs"][i][:, w * 1024:(w + 1) * 1024],
                                   sem="gp2", wait=wtab if i == 0 else (), tag=f"B{l}t{w}{i}")
                ytil = ytil2w[w % 2]
                for s in range(SPC):
                    ld_y = p.dma("sync", ytil[s].rearrange("p (c j) -> p c j", j=513),
                                 p1src[:, :, s * (SEQ + 1) + w * 512:s * (SEQ + 1) + w * 512 + 513]
                                 .rearrange("c p j -> p c j"),
                                 sem="ld5" if w % 2 == 0 else "ld6",
                                 wait=ytil_tok.get(w % 2, ()) if s == 0 else (),
                                 tag=f"B{l}y{w}{s}")
                hq_done = None
                for c in range(8):
                    off = 0 if c >= 4 else 1
                    for s in range(SPC):
                        hq_done = p.s(lambda e, c=c, s=s, off=off, ytil=ytil: e.activation(
                            out=hq[c][:, s * 512:(s + 1) * 512],
                            in_=ytil[s][:, c * 513 + off:c * 513 + off + 512],
                            func=AF.Copy),
                            wait=[ld_y, ("pe", pe_all), ("ve", ve_all)]
                            if (c == 0 and s == 0) else (),
                            inc=True, tag=f"B{l}h{w}{c}{s}")
                evB = {}
                for d in range(12):
                    buf = wcolB[d % 2]
                    wv = list(wbank_tok)
                    if (d % 2) in mm_doneB:
                        wv.append(("pe", mm_doneB[d % 2]))
                    ld_w = p.dma("sync", buf, wbB_q[l, d], sem="ld2" if d % 2 == 0 else "ld3",
                                 wait=wv, tag=f"B{l}w{w}{d}")
                    for s in range(SPC):
                        bank = [pA[:, 0:512], pA[:, 512:1024], pB[:, 0:512],
                                pB[:, 512:1024]][2 * (d % 2) + s]
                        wmm = [ld_w, ("se", hq_done)]
                        if (d - 2, s) in evB:
                            wmm.append(("ve", evB[(d - 2, s)]))
                        for k in range(8):
                            p.t(lambda e, k=k, bank=bank, buf=buf, s=s: e.matmul(
                                bank, buf[:, k * 128:(k + 1) * 128],
                                hq[k][:, s * 512:(s + 1) * 512],
                                start=(k == 0), stop=(k == 7)),
                                wait=wmm if k == 0 else (), inc=(k == 7),
                                tag=f"B{l}m{w}_{d}_{s}_{k}")
                        pe_mm = p.now("pe")
                        dst = (qraw[d] if d < 4
                               else kring[w % 2][d - 4] if d < 8 else vraw[d - 8])
                        wevac = [("pe", pe_mm)]
                        if d == 0 and s == 0 and w > 0:
                            wevac += [("se", se_all), sc0T, sc1T]
                        evB[(d, s)] = p.v(lambda e, dst=dst, bank=bank, s=s: e.tensor_copy(
                            out=dst[:, s * 512:(s + 1) * 512], in_=bank),
                            wait=wevac, inc=True, tag=f"B{l}e{w}_{d}_{s}")
                    mm_doneB[d % 2] = p.now("pe")
                qkv_doneB = p.now("ve")

                def rotnorm(raw, dst_tiles, sc_ap, tcs, tss, ev_base, prev, pfx,
                            extra0=()):
                    outs = []
                    for c in range(4):
                        wsq = [("ve", evB[(ev_base + c, 1)])]
                        if c == 0 and prev is None:
                            wsq.append(("ve", qkv_doneB))
                        if outs:
                            wsq.append(("ve", outs[-1]))
                        elif prev is not None:
                            wsq.append(("ve", prev))
                        sq_t = p.s(lambda e, c=c: e.activation(out=sqB, in_=raw[c],
                                                               func=AF.Square),
                                   wait=wsq, inc=True, tag=f"{pfx}sq{c}")
                        p.t(lambda e: e.matmul(pA[0:2, 0:512], e2b, sqB[:, 0:512],
                                               start=True, stop=True),
                            wait=[("se", sq_t), ("ve", qkv_doneB)], tag=f"{pfx}s1{c}")
                        p.t(lambda e: e.matmul(pA[0:2, 512:1024], e2b, sqB[:, 512:1024],
                                               start=True, stop=True),
                            inc=True, tag=f"{pfx}s2{c}")
                        pe_ss = p.now("pe")
                        p.s(lambda e: e.activation(out=rnr2[:, 0:512], in_=pA[0:2, 0:512],
                                                   func=AF.Ln, bias=e24[0:2, :]),
                            wait=[("pe", pe_ss)], inc=True, tag=f"{pfx}sr{c}")
                        p.s(lambda e: e.activation(out=rnr2[:, 512:1024],
                                                   in_=pA[0:2, 512:1024],
                                                   func=AF.Ln, bias=e24[0:2, :]),
                            inc=True, tag=f"{pfx}sq2{c}")
                        r3t = p.s(lambda e: e.activation(out=rnrb, in_=rnr2, func=AF.Exp,
                                                         scale=-0.5),
                                  inc=True, tag=f"{pfx}rc{c}")
                        p.t(lambda e: e.matmul(pB[:, 0:512], bo2b, rnrb[:, 0:512],
                                               start=True, stop=True),
                            wait=[("se", r3t)], tag=f"{pfx}b1{c}")
                        p.t(lambda e: e.matmul(pB[:, 512:1024], bo2b, rnrb[:, 512:1024],
                                               start=True, stop=True),
                            inc=True, tag=f"{pfx}b2{c}")
                        pe_bc = p.now("pe")
                        n1 = p.v(lambda e, c=c: e.scalar_tensor_tensor(
                            out=raw[c], in0=raw[c], scalar=sc_ap, in1=pB[:, 0:1024],
                            op0=ALU.mult, op1=ALU.mult),
                            wait=[("pe", pe_bc)] + (list(extra0) if c == 0 else []),
                            inc=True, tag=f"{pfx}nm{c}")
                        p.t(lambda e, c=c: e.matmul(pA[:, 0:512], r32, raw[c][:, 0:512],
                                                    start=True, stop=True),
                            wait=[("ve", n1)], tag=f"{pfx}r1{c}")
                        p.t(lambda e, c=c: e.matmul(pA[:, 512:1024], r32,
                                                    raw[c][:, 512:1024],
                                                    start=True, stop=True),
                            inc=True, tag=f"{pfx}r2{c}")
                        pe_rot = p.now("pe")
                        c1 = p.v(lambda e, c=c: e.tensor_mul(out=scr1, in0=raw[c], in1=tcs),
                                 wait=[tab_ld], inc=True, tag=f"{pfx}c1{c}")
                        c2 = p.v(lambda e: e.tensor_mul(out=scr2, in0=pA[:, 0:1024],
                                                        in1=tss),
                                 wait=[("pe", pe_rot)], inc=True, tag=f"{pfx}c2{c}")
                        outs.append(p.v(lambda e, c=c: e.tensor_add(
                            out=dst_tiles[c], in0=scr1, in1=scr2),
                            wait=[("ve", c1), ("ve", c2)] +
                            (list(extra0) if c == 0 else []),
                            inc=True, tag=f"{pfx}fn{c}"))
                    return outs

                exq = [sc0T, sc1T] if w > 0 else []
                exk = ([("pe", pe_all), sc0T, sc1T] if w > 0 else [])
                q_out = rotnorm(qraw, qraw, qs_sb[:, l:l + 1], tabst[0], tabst[1],
                                0, None, f"B{l}q{w}_", extra0=exq)
                k_out = rotnorm(kring[w % 2], kring[w % 2], ks_sb[:, l:l + 1],
                                tabst[2], tabst[3],
                                4, q_out[-1], f"B{l}K{w}_", extra0=exk)
                kq_cell[0] = k_out[-1]
                vt_tok = None
                for c in range(4):
                    for s in range(SPC):
                        for i in range(4):
                            wv = [("ve", evB[(8 + c, s)])]
                            if vt_hist:
                                wv.append(("ve", vt_hist[-1]))
                            if (c, s, i) == (0, 0, 0) and fin_tokB is not None:
                                wv.append(("ve", fin_tokB))
                            p.t(lambda e, c=c, s=s, i=i: e.transpose(
                                pFb[:, i * 128:(i + 1) * 128],
                                vraw[c][:, s * 512 + i * 128:s * 512 + (i + 1) * 128],
                                identb),
                                wait=wv, inc=True, tag=f"B{l}vt{w}_{c}{s}{i}")
                            pe_t = p.now("pe")
                            vt_tok = p.v(lambda e, c=c, s=s, i=i, w=w: e.tensor_copy(
                                out=vaugB[w % 2][s * 4 + i]
                                .rearrange("p (h x) -> p h x", h=NH_B)[:, 2 * c:2 * c + 2, 0:64],
                                in_=pFb[:, i * 128:(i + 1) * 128]
                                .rearrange("p (h x) -> p h x", h=2)),
                                wait=[("pe", pe_t)], inc=True, tag=f"B{l}ve{w}_{c}{s}{i}")
                            vt_hist.append(vt_tok)
                vones = None
                for i in range(8):
                    vones = p.v(lambda e, i=i, w=w: e.memset(
                        vaugB[w % 2][i].rearrange("p (h x) -> p h x", h=NH_B)[:, :, 64:65], 1.0),
                        wait=[("ve", vt_tok)] if i == 0 else (), inc=True, tag=f"B{l}vo{w}{i}")
                rot_done = p.now("ve")
                kq_done = {"q": q_out[-1], "k": k_out[-1]}

                def emit_asmB(u):
                    s, h = divmod(u, NH_B)
                    ch = h // 2
                    ktlB, qtlB = ktlB2[u % 2], qtlB2[u % 2]
                    wk = [("ve", rot_done), ("pe", p.now("pe"))]
                    if fin_tokB is not None:
                        wk.append(("ve", fin_tokB))
                    if h % 2 == 0:
                        if w > 0:
                            p.s(lambda e, ch=ch, s=s, w=w, ktlB=ktlB: e.activation(
                                out=ktlB[0:64, 0:512],
                                in_=kring[(w - 1) % 2][ch][0:64, s * 512:(s + 1) * 512],
                                func=AF.Copy),
                                wait=wk, inc=True, tag=f"B{l}ka{w}{s}{h}")
                        p.s(lambda e, ch=ch, s=s, w=w, ktlB=ktlB: e.activation(
                            out=ktlB[0:64, 512:1024],
                            in_=kring[w % 2][ch][0:64, s * 512:(s + 1) * 512],
                            func=AF.Copy),
                            wait=wk, inc=True, tag=f"B{l}kb{w}{s}{h}")
                        qa = p.s(lambda e, ch=ch, s=s, qtlB=qtlB: e.activation(
                            out=qtlB[0:64, :], in_=qraw[ch][0:64, s * 512:(s + 1) * 512],
                            func=AF.Copy),
                            inc=True, tag=f"B{l}qa{w}{s}{h}")
                        asmB[u] = [("se", qa)]
                    else:
                        sem = "sc0" if (u % 2 == 0) else "sc1"
                        if w > 0:
                            p.dma("scalar", ktlB[0:64, 0:512],
                                  kring[(w - 1) % 2][ch][64:128, s * 512:(s + 1) * 512],
                                  sem=sem, wait=wk, tag=f"B{l}ka{w}{s}{h}")
                        p.dma("scalar", ktlB[0:64, 512:1024],
                              kring[w % 2][ch][64:128, s * 512:(s + 1) * 512],
                              sem=sem, wait=wk, tag=f"B{l}kb{w}{s}{h}")
                        asmB[u] = [p.dma("scalar", qtlB[0:64, :],
                                         qraw[ch][64:128, s * 512:(s + 1) * 512],
                                         sem=sem, tag=f"B{l}qa{w}{s}{h}")]

                def emit_p1B(u):
                    s, h = divmod(u, NH_B)
                    ch = h // 2
                    rbase = 64 * (h % 2)
                    mstB = mstB2[u % 2]
                    qk_redB = {}
                    prev_qk = UB[u - 1]["qk"] if u > 0 else None
                    for qt in range(4):
                        pq = pA if qt % 2 == 0 else pB
                        width = 640 if w > 0 else qt * 128 + 128
                        wseg = [("ve", rot_done)]
                        if qt >= 2:
                            wseg.append(("ve", qk_redB[qt - 2]))
                        elif prev_qk is not None:
                            wseg.append(("ve", prev_qk[qt + 2]))
                        segs = []
                        if w > 0:
                            segs.append((kring[(w - 1) % 2][ch], qt * 128, 0, 512 - qt * 128))
                            segs.append((kring[w % 2][ch], 0, 512 - qt * 128, qt * 128 + 128))
                        else:
                            segs.append((kring[0][ch], 0, 0, qt * 128 + 128))
                        cuts = {0, width, width - 128, 512}
                        if w > 0:
                            cuts.add(128)
                        mm_list = []
                        for (src, so, po, n) in segs:
                            bnds = sorted(c for c in cuts if po < c < po + n)
                            prev = po
                            for c in bnds + [po + n]:
                                mm_list.append((src, so + prev - po, prev, c - prev))
                                prev = c
                        masked0 = (0, 128) if w > 0 else None
                        masked1 = (width - 128, width)
                        plain = [t_ for t_ in mm_list
                                 if not ((masked0 and t_[2] == masked0[0]) or
                                         t_[2] == masked1[0])]
                        pairs = [t_ for t_ in mm_list if t_ not in plain]
                        first = True
                        for (src, so, po, nn) in plain:
                            p.t(lambda e, pq=pq, src=src, so=so, po=po, nn=nn, qt=qt,
                                s=s, ch=ch, rbase=rbase: e.matmul(
                                    pq[:, po:po + nn],
                                    qraw[ch][rbase:rbase + 64,
                                             s * 512 + qt * 128:s * 512 + (qt + 1) * 128],
                                    src[rbase:rbase + 64, s * 512 + so:s * 512 + so + nn],
                                    start=True, stop=True),
                                wait=wseg if first else (), tag=f"B{l}s{w}_{s}{h}{qt}_{po}")
                            first = False
                        for (src, so, po, nn) in pairs:
                            mask = mcu if (masked0 and po == masked0[0]) else mcl
                            p.t(lambda e, pq=pq, src=src, so=so, po=po, nn=nn, qt=qt,
                                s=s, ch=ch, rbase=rbase: e.matmul(
                                    pq[:, po:po + nn],
                                    qraw[ch][rbase:rbase + 64,
                                             s * 512 + qt * 128:s * 512 + (qt + 1) * 128],
                                    src[rbase:rbase + 64, s * 512 + so:s * 512 + so + nn],
                                    start=True, stop=False),
                                wait=wseg if first else (), tag=f"B{l}sm{w}_{s}{h}{qt}_{po}")
                            first = False
                            p.t(lambda e, pq=pq, po=po, mask=mask: e.matmul(
                                pq[:, po:po + 128], identb, mask,
                                start=False, stop=True),
                                inc=(po == masked1[0]), tag=f"B{l}dm{w}_{s}{h}{qt}_{po}")
                        pe_seg = p.now("pe")
                        red = p.v(lambda e, pq=pq, width=width, qt=qt, mstB=mstB: e.reduce_max(
                            out=mstB[:, qt:qt + 1], in_=pq[:, 0:width],
                            axis=AX.X, negate=True),
                            wait=[("pe", pe_seg)], inc=True, tag=f"B{l}r{w}_{s}{h}{qt}")
                        qk_redB[qt] = red
                    UB[u] = {"qk": qk_redB, "mst": mstB}

                def emit_p1bB(u):
                    s, h = divmod(u, NH_B)
                    qtlB = qtlB2[u % 2]
                    mstB = UB[u]["mst"]
                    qk_redB = UB[u]["qk"]
                    m4 = m4bB2[u % 2]
                    wtm = [("ve", qk_redB[qt]) for qt in range(4)]
                    if cur_mevB[0] is not None:
                        wtm.append(("ve", cur_mevB[0]))
                    if prev_f4B[0] is not None:
                        wtm.append(("ve", prev_f4B[0]))
                    ptm = p.t(lambda e, mstB=mstB: e.transpose(
                        pFf[0:4, 0:128], mstB[:, 0:4], identf),
                        wait=wtm, inc=True, tag=f"B{l}tm{w}_{s}{h}")
                    wm4 = [("pe", ptm)]
                    if m4_dmaB[u % 2] is not None:
                        wm4.append(m4_dmaB[u % 2])
                    mev = p.v(lambda e, m4=m4: e.tensor_copy(out=m4, in_=pFf[0:4, 0:128]),
                              wait=wm4, inc=True, tag=f"B{l}mm{w}_{s}{h}")
                    cur_mevB[0] = mev
                    m_doneB = p.dma("gpsimd", qtlB[64:65, 0:512], m4,
                                    sem="gp0" if u % 2 == 0 else "gp1",
                                    wait=[("ve", mev)], tag=f"B{l}mv{w}_{s}{h}")
                    m4_dmaB[u % 2] = m_doneB
                    prev_mpl[0] = m_doneB
                    UB[u]["m"] = m_doneB

                def emit_p2B(u):
                    nonlocal fin_tokB
                    s, h = divmod(u, NH_B)
                    ch = h // 2
                    rbase = 64 * (h % 2)
                    ktlB, qtlB = ktlB2[u % 2], qtlB2[u % 2]
                    ostB = ostB2[s % 2]
                    m_doneB = UB[u]["m"]
                    prev_fin = fin_tokB
                    kc0 = 0 if w > 0 else 4
                    exp_tokB = {}
                    for kc in range(kc0, 8):
                        pz = pC if kc % 2 == 0 else pD
                        wz = [m_doneB] + asmB[u]
                        if kc == kc0:
                            wz.append(("ve", rot_done))
                        if kc - kc0 >= 2:
                            wz.append(("se", exp_tokB[kc - 2]))
                        elif prev_expB[0] is not None:
                            wz.append(("se", prev_expB[0]))
                        kinds = []
                        for qt in range(4):
                            if kc < qt or kc > qt + 4:
                                kinds.append("fill")
                            elif kc == qt and w > 0:
                                kinds.append("mcu")
                            elif kc == qt + 4:
                                kinds.append("mcl")
                            else:
                                kinds.append("clean")
                        first_z = True
                        q0 = 0
                        while q0 < 4:
                            if kinds[q0] != "clean":
                                q0 += 1
                                continue
                            q1 = q0
                            while q1 + 1 < 4 and kinds[q1 + 1] == "clean":
                                q1 += 1
                            p.t(lambda e, pz=pz, kc=kc, q0=q0, q1=q1, ktlB=ktlB,
                                qtlB=qtlB: e.matmul(
                                    pz[:, q0 * 128:(q1 + 1) * 128],
                                    ktlB[:, kc * 128:(kc + 1) * 128],
                                    qtlB[:, q0 * 128:(q1 + 1) * 128],
                                    start=True, stop=True),
                                wait=wz if first_z else (), tag=f"B{l}z{w}_{s}{h}{kc}{q0}")
                            first_z = False
                            q0 = q1 + 1
                        ndiag = sum(1 for k_ in kinds if k_ in ("mcu", "mcl"))
                        di = 0
                        for qt in range(4):
                            if kinds[qt] not in ("mcu", "mcl"):
                                continue
                            mask = mcu if kinds[qt] == "mcu" else mcl
                            p.t(lambda e, pz=pz, kc=kc, qt=qt, ktlB=ktlB, qtlB=qtlB: e.matmul(
                                pz[:, qt * 128:(qt + 1) * 128],
                                ktlB[:, kc * 128:(kc + 1) * 128],
                                qtlB[:, qt * 128:(qt + 1) * 128],
                                start=True, stop=False),
                                wait=wz if first_z else (), tag=f"B{l}zd{w}_{s}{h}{kc}{qt}")
                            first_z = False
                            di += 1
                            p.t(lambda e, pz=pz, qt=qt, mask=mask: e.matmul(
                                pz[:, qt * 128:(qt + 1) * 128], identb, mask,
                                start=False, stop=True),
                                inc=(di == ndiag), tag=f"B{l}fd{w}_{s}{h}{kc}{qt}")
                        pe_z = p.now("pe")
                        col0 = max(0, kc - 4) * 128
                        col1 = min(4, kc + 1) * 128
                        exp_tokB[kc] = p.s(lambda e, pz=pz, kc=kc, col0=col0,
                                           col1=col1: e.activation(
                            out=ztB[kc][:, 0:col1 - col0], in_=pz[:, col0:col1],
                            func=AF.Exp, scale=8.0),
                            wait=[("pe", pe_z)], inc=True, tag=f"B{l}x{w}_{s}{h}{kc}")
                    prev_expB[0] = exp_tokB[7]
                    if w > 0:
                        wop = [("ve", vones)]
                        if prev_fin is not None:
                            wop.append(("ve", prev_fin))
                        p.t(lambda e: e.matmul(
                            pE[0:65, :], zcB[64:65, 0:65], ktlB2[0][64:65, 0:512],
                            start=True, stop=False),
                            wait=wop, tag=f"B{l}ao{w}_{s}{h}")
                    for kc in range(kc0, 8):
                        slot = (w - 1) % 2 if kc < 4 else w % 2
                        tile = s * 4 + (kc if kc < 4 else kc - 4)
                        wav = [("se", exp_tokB[kc])]
                        if kc == kc0 and w == 0:
                            wav.append(("ve", vones))
                            if prev_fin is not None:
                                wav.append(("ve", prev_fin))
                        col0 = max(0, kc - 4) * 128
                        col1 = min(4, kc + 1) * 128
                        p.t(lambda e, kc=kc, slot=slot, tile=tile, h=h, kc0=kc0,
                            col0=col0, col1=col1, w=w: e.matmul(
                            pE[0:65, col0:col1],
                            vaugB[slot][tile][:, h * 65:(h + 1) * 65],
                            ztB[kc][:, 0:col1 - col0],
                            start=(kc == kc0 and w == 0), stop=False),
                            wait=wav, tag=f"B{l}a{w}_{s}{h}{kc}")
                    p.t(lambda e: e.matmul(
                        pE[0:65, :], zcB[64:65, 0:65], ktlB2[0][64:65, 0:512],
                        start=False, stop=True),
                        inc=True, tag=f"B{l}az{w}_{s}{h}")
                    pe_av = p.now("pe")
                    wf1 = [("pe", pe_av)]
                    if h == 0 and ostB_dma[s % 2] is not None:
                        wf1.append(ostB_dma[s % 2])
                    f2a = p.s(lambda e: e.activation(out=rrB[64:65, :], in_=pE[64:65, :],
                                                     func=AF.Ln),
                              wait=wf1, inc=True, tag=f"B{l}lg{w}_{s}{h}")
                    f2 = p.s(lambda e: e.activation(out=rrbB[64:65, :], in_=rrB[64:65, :],
                                                    func=AF.Exp, scale=-1.0),
                             wait=[("se", f2a)], inc=True, tag=f"B{l}rc{w}_{s}{h}")
                    wfb = [("se", f2)]
                    if cur_mevB[0] is not None:
                        wfb.append(("ve", cur_mevB[0]))
                    fb = p.t(lambda e: e.matmul(pFf[0:64, 0:512], ones64[64:65, :],
                                                rrbB[64:65, :], start=True, stop=True),
                             wait=wfb, inc=True, tag=f"B{l}bc{w}_{s}{h}")
                    f4 = p.v(lambda e: e.tensor_copy(out=rbcB, in_=pFf[0:64, 0:512]),
                             wait=[("pe", fb)], inc=True, tag=f"B{l}be{w}_{s}{h}")
                    prev_f4B[0] = f4
                    fin_tokB = p.v(lambda e, h=h, ostB=ostB: e.tensor_mul(
                        out=ostB[:, h * 512:(h + 1) * 512], in0=pE[0:64, :], in1=rbcB),
                        wait=[("ve", f4)], inc=True, tag=f"B{l}o{w}_{s}{h}")
                    if h == NH_B - 1:
                        sem = "sc2" if s % 2 == 0 else "sc3"
                        p.dma("scalar",
                              ocat[0:64, :].rearrange("p (c j) -> p c j", j=1024)
                              [:, :, s * 512:(s + 1) * 512],
                              ostB.rearrange("p (c x j) -> p c x j", c=4, x=2)[:, :, 0, :],
                              sem=sem, wait=[("ve", fin_tokB)], tag=f"B{l}oe{w}{s}")
                        ostB_dma[s % 2] = p.dma("scalar",
                              ocat[64:128, :].rearrange("p (c j) -> p c j", j=1024)
                              [:, :, s * 512:(s + 1) * 512],
                              ostB.rearrange("p (c x j) -> p c x j", c=4, x=2)[:, :, 1, :],
                              sem=sem, tag=f"B{l}oo{w}{s}")

                nuB = SPC * NH_B
                for u in range(nuB):
                    emit_p1B(u)
                    if u >= 1:
                        emit_p1bB(u - 1)
                    if u >= 2:
                        emit_p2B(u - 2)
                    emit_asmB(u)
                emit_p1bB(nuB - 1)
                emit_p2B(nuB - 2)
                emit_p2B(nuB - 1)
                ocat_tok = [("sc2", p.now("sc2")), ("sc3", p.now("sc3"))]

                ev_wo = {}
                ystore = ystore_c[0]
                for s in range(SPC):
                    for d in range(8):
                        key = d % 4
                        bank = [pA[:, 0:512], pA[:, 512:1024], pB[:, 0:512],
                                pB[:, 512:1024]][key]
                        wmm = [ld_woB] + ocat_tok
                        if key in ev_wo:
                            wmm.append(("ve", ev_wo[key]))
                        for k in range(4):
                            p.t(lambda e, k=k, d=d, s=s, bank=bank: e.matmul(
                                bank, woBb[k][:, d * 128:(d + 1) * 128],
                                ocat[:, k * 1024 + s * 512:k * 1024 + (s + 1) * 512],
                                start=(k == 0), stop=(k == 3)),
                                wait=wmm if k == 0 else (), inc=(k == 3),
                                tag=f"B{l}wm{w}_{s}{d}_{k}")
                        pe_mm = p.now("pe")
                        wev = [("pe", pe_mm)]
                        if ystore is not None and d < 4:
                            wev.append(ystore)
                        ev_wo[key] = p.v(lambda e, d=d, s=s, bank=bank,
                                         ytil=ytil: e.tensor_add(
                            out=ynew[:, d * 512:(d + 1) * 512],
                            in0=bank, in1=ytil[s][:, d * 513 + 1:d * 513 + 513]),
                            wait=wev, inc=True, tag=f"B{l}we{w}_{s}{d}")
                    ynew_tok = p.now("ve")
                    ystore = p.dma("sync",
                          p1dst[:, :, s * (SEQ + 1) + 1 + w * 512:s * (SEQ + 1) + 1 + w * 512 + 512]
                          .rearrange("c p j -> p c j"),
                          ynew.rearrange("p (c j) -> p c j", j=512),
                          sem="ld7", wait=[("ve", ynew_tok)], tag=f"B{l}ys{w}{s}")
                ystore_c[0] = ystore
                ytil_tok[w % 2] = [("ve", p.now("ve")), ("se", p.now("se"))]
            rb.close()

            if stop == f"p1_{l}":
                finish(E, sems, p1dst)
                return True

            r2a = Region(nc)
            hlnB = [r2a.sb(f"hlB{c}", [128, T], BF16) for c in range(8)]
            ytilB2 = [r2a.sb(f"ytilB{i}", [128, 8 * 513], F32) for i in range(2)]
            ln2 = mk_ln_scratch(r2a, "lnB")
            wcol2 = [r2a.sb(f"w1c{i}", [128, 8 * 256], BF16) for i in range(2)]
            g2st = r2a.sb("g2st", [128, 512], BF16)
            hgst = r2a.sb("hgst", [128, T], BF16)
            p.barrier()
            apl = {}
            for s in range(SPC):
                for tt in range(4):
                    n = s * 4 + tt
                    ytilB = ytilB2[n % 2]
                    wv = [("ve", apl[n - 2])] if (n - 2) in apl else []
                    ld_y = p.dma("sync", ytilB.rearrange("p (c j) -> p c j", j=513),
                          p2src[:, :, s * (SEQ + 1) + tt * 512:s * (SEQ + 1) + tt * 512 + 513]
                          .rearrange("c p j -> p c j"), sem="ld5" if n % 2 == 0 else "ld6",
                          wait=wv, tag=f"C{l}y{s}{tt}")
                    ints = []
                    for c in range(8):
                        off = 0 if c >= 4 else 1
                        ints.append(ytilB[:, c * 513 + off:c * 513 + off + 512])
                    p.fence(("vector", "scalar", "tensor"), {ld_y[0]: ld_y[1]})
                    louts = emit_ln(ln2, ints,
                                    [hlnB[c][:, (s * 4 + tt) * 512:(s * 4 + tt + 1) * 512]
                                     for c in range(8)],
                                    gb=([lng_sb[:, 8 * l + c:8 * l + c + 1] for c in range(8)],
                                        [lnb_sb[:, 8 * l + c:8 * l + c + 1] for c in range(8)]),
                                    tag=f"C{l}n{s}{tt}")
                    apl[n] = louts[-1]
            hln_done = p.now("ve")
            w1_mm_done = {}
            prev_hg = None
            prev_st = None
            for c2 in range(W2C):
                buf = wcol2[c2 % 2]
                wv = []
                if (c2 % 2) in w1_mm_done:
                    wv.append(("pe", w1_mm_done[c2 % 2]))
                ld_w = p.dma("gpsimd", buf.rearrange("p (k j) -> p k j", j=256),
                      ew["w1"][l][:, c2 * 256:(c2 + 1) * 256]
                      .rearrange("(k p) j -> p k j", p=128),
                      sem="gp0" if c2 % 2 == 0 else "gp1", wait=wv, tag=f"C{l}w{c2}")
                hg_tok = {}
                for tt8 in range(8):
                    bka = pA[:, 0:512] if tt8 % 2 == 0 else pB[:, 0:512]
                    bkg = pA[:, 512:1024] if tt8 % 2 == 0 else pB[:, 512:1024]
                    wmm = [ld_w, ("ve", hln_done)]
                    if tt8 >= 2:
                        wmm.append(("ve", hg_tok[tt8 - 2]))
                    elif prev_hg is not None:
                        wmm.append(("ve", prev_hg[6 + tt8]))
                    for k in range(8):
                        p.t(lambda e, k=k, buf=buf, tt8=tt8, bka=bka: e.matmul(
                            bka, buf[:, k * 256:k * 256 + 128],
                            hlnB[k][:, tt8 * 512:(tt8 + 1) * 512],
                            start=(k == 0), stop=(k == 7)),
                            wait=wmm if k == 0 else (), tag=f"C{l}a{c2}_{tt8}_{k}")
                    for k in range(8):
                        p.t(lambda e, k=k, buf=buf, tt8=tt8, bkg=bkg: e.matmul(
                            bkg, buf[:, k * 256 + 128:k * 256 + 256],
                            hlnB[k][:, tt8 * 512:(tt8 + 1) * 512],
                            start=(k == 0), stop=(k == 7)),
                            inc=(k == 7), tag=f"C{l}g{c2}_{tt8}_{k}")
                    pe_mm = p.now("pe")
                    wsg = [("pe", pe_mm)]
                    if tt8 >= 1:
                        wsg.append(("ve", hg_tok[tt8 - 1]))
                    elif prev_hg is not None:
                        wsg.append(("ve", prev_hg[7]))
                    sg = p.s(lambda e, bkg=bkg, c2=c2: e.activation(
                        out=g2st, in_=bkg, func=AF.Gelu,
                        bias=b1_sb[:, 2 * W2C * l + 2 * c2 + 1:2 * W2C * l + 2 * c2 + 2]),
                        wait=wsg, inc=True, tag=f"C{l}gg{c2}_{tt8}")
                    whg = [("se", sg)]
                    if tt8 < 2 and c2 > 0 and prev_st is not None:
                        whg.append(prev_st)
                    hg_tok[tt8] = p.v(lambda e, bka=bka, c2=c2, tt8=tt8: e.scalar_tensor_tensor(
                        out=hgst[:, tt8 * 512:(tt8 + 1) * 512], in0=bka,
                        scalar=b1_sb[:, 2 * W2C * l + 2 * c2:2 * W2C * l + 2 * c2 + 1],
                        in1=g2st, op0=ALU.add, op1=ALU.mult),
                        wait=whg, inc=True, tag=f"C{l}hg{c2}_{tt8}")
                w1_mm_done[c2 % 2] = p.now("pe")
                prev_hg = hg_tok
                prev_st = p.dma("sync", hg_d[c2], hgst, sem="ld7",
                                wait=[("ve", hg_tok[7])], tag=f"C{l}hs{c2}")
            hg_st_tok = prev_st
            r2a.close()

            r2b = Region(nc)
            w2b = [r2b.sb(f"w2b{k}", [128, 1024], BF16) for k in range(W2C)]
            hgt2 = [r2b.sb(f"hgt{i}", [128, W2C * 512], BF16) for i in range(2)]
            ytil22 = [r2b.sb(f"ytil2{i}", [128, 8 * 513], F32) for i in range(2)]
            ynew2 = r2b.sb("ynew2", [128, 8 * 512], F32)
            xat = r2b.sb("xat", [128, 8 * 512], F32)
            outst2 = [r2b.sb(f"outst{i}", [128, 1024], F32) for i in range(2)]
            p.barrier()
            for k in range(W2C):
                ld_w2 = p.dma("gpsimd", w2b[k], ew["w2"][l][k * 128:(k + 1) * 128, :],
                              sem="gp2", tag=f"D{l}w{k}")
            out_dma = None
            od2 = {0: None, 1: None}
            mm_pe_tok = {}
            ev7_tok = {}
            fadd7_tok = {}
            for tt8 in range(8):
                s, tt = tt8 // 4, tt8 % 4
                t0 = tt8 * 512
                hgt = hgt2[tt8 % 2]
                ytil2 = ytil22[tt8 % 2]
                wv = [hg_st_tok]
                if tt8 >= 2:
                    wv.append(("pe", mm_pe_tok[tt8 - 2]))
                ld_hgt = p.dma("sync", hgt.rearrange("p (c j) -> p c j", j=512),
                        hg_d[:, :, t0:t0 + 512].rearrange("c p j -> p c j"),
                        sem="ld4", wait=wv, tag=f"D{l}h{tt8}")
                wy = [("ve", ev7_tok[tt8 - 2])] if tt8 >= 2 else []
                ld_y2 = p.dma("sync", ytil2.rearrange("p (c j) -> p c j", j=513),
                            p2src[:, :, s * (SEQ + 1) + tt * 512:s * (SEQ + 1) + tt * 512 + 513]
                            .rearrange("c p j -> p c j"), sem="ld5", wait=wy,
                            tag=f"D{l}y{tt8}")
                ld_x2 = None
                if l == 1:
                    wx = [("ve", fadd7_tok[tt8 - 1])] if tt8 >= 1 else []
                    ld_x2 = p.dma("sync", xat.rearrange("p (c j) -> p c j", j=512),
                                xa_d[:, :, t0:t0 + 512].rearrange("c p j -> p c j"),
                                sem="ld6", wait=wx, tag=f"D{l}x{tt8}")
                ev2 = {}
                for d in range(8):
                    bank = [pA[:, 0:512], pA[:, 512:1024], pB[:, 0:512], pB[:, 512:1024]][d % 4]
                    wmm = [ld_w2, ld_hgt]
                    if d >= 4:
                        wmm.append(("ve", ev2[d - 4]))
                    for k in range(W2C):
                        p.t(lambda e, k=k, d=d, bank=bank, hgt=hgt: e.matmul(
                            bank, w2b[k][:, d * 128:(d + 1) * 128],
                            hgt[:, k * 512:(k + 1) * 512],
                            start=(k == 0), stop=(k == W2C - 1)),
                            wait=wmm if k == 0 else (), inc=(k == W2C - 1),
                            tag=f"D{l}m{tt8}_{d}_{k}")
                    pe_mm = p.now("pe")
                    wev2 = [("pe", pe_mm)]
                    if d == 0:
                        wev2.append(ld_y2)
                        if tt8 > 0:
                            wev2.append(("ld7", p.now("ld7")))
                    ev2[d] = p.v(lambda e, d=d, bank=bank, l=l, ytil2=ytil2:
                                 e.scalar_tensor_tensor(
                        out=ynew2[:, d * 512:(d + 1) * 512], in0=bank,
                        scalar=b2_sb[:, 8 * l + d:8 * l + d + 1],
                        in1=ytil2[:, d * 513 + 1:d * 513 + 513],
                        op0=ALU.add, op1=ALU.add),
                        wait=wev2, inc=True, tag=f"D{l}e{tt8}_{d}")
                mm_pe_tok[tt8] = p.now("pe")
                ev7_tok[tt8] = ev2[7]
                if l == 0:
                    p.dma("sync",
                          p2dst[:, :, s * (SEQ + 1) + 1 + tt * 512:s * (SEQ + 1) + 1 + tt * 512 + 512]
                          .rearrange("c p j -> p c j"),
                          ynew2.rearrange("p (c j) -> p c j", j=512),
                          sem="ld7", wait=[("ve", ev2[7])], tag=f"D{l}s{tt8}")
                else:
                    fadd = {}
                    for d in range(8):
                        wfa = [("ve", ev2[d])]
                        if d == 0:
                            wfa.append(ld_x2)
                        fadd[d] = p.v(lambda e, d=d: e.tensor_add(
                            out=ynew2[:, d * 512:(d + 1) * 512],
                            in0=ynew2[:, d * 512:(d + 1) * 512],
                            in1=xat[:, d * 512:(d + 1) * 512]),
                            wait=wfa, inc=True, tag=f"D{l}fa{tt8}_{d}")
                    fadd7_tok[tt8] = fadd[7]
                    for i in range(4):
                        evo = {}
                        outst = outst2[i % 2]
                        for chalf in range(2):
                            bank = pA if chalf == 0 else pB
                            for cc in range(4):
                                c = chalf * 4 + cc
                                wtr = []
                                if cc == 0:
                                    wtr = [("ve", fadd[7])]
                                    if out_dma is not None:
                                        wtr.append(out_dma)
                                p.t(lambda e, c=c, cc=cc, i=i, bank=bank: e.transpose(
                                    bank[:, cc * 128:(cc + 1) * 128],
                                    ynew2[:, c * 512 + i * 128:c * 512 + (i + 1) * 128],
                                    identf),
                                    wait=wtr, inc=(cc == 3), tag=f"D{l}tr{tt8}_{i}_{c}")
                            pe_tr = p.now("pe")
                            evo[chalf] = p.v(lambda e, chalf=chalf, bank=bank,
                                             outst=outst: e.tensor_copy(
                                out=outst[:, chalf * 512:(chalf + 1) * 512], in_=bank[:, 0:512]),
                                wait=[("pe", pe_tr)] + ([od2[i % 2]]
                                                        if od2[i % 2] is not None else []),
                                inc=True, tag=f"D{l}te{tt8}_{i}_{chalf}")
                        out_dma = p.dma("sync", out[t0 + i * 128:t0 + (i + 1) * 128, :], outst,
                                        sem="ld7", wait=[("ve", evo[1])], tag=f"D{l}os{tt8}_{i}")
                        od2[i % 2] = out_dma
            r2b.close()

            if stop == f"p2_{l}":
                finish(E, sems, p2dst)
                return True

            return False

        for l in range(2):
            if _layer(l):
                return nc

        p.op("sync", lambda e: e.nop(),
             wait=[(s, p.now(s)) for s in SEMS if s[0:2] in ("ld", "gp", "sc")], tag="fin")
        p.check()
        block = E(nc.Block())
        p.emit(block, sems)
    return nc


def kernel(**inputs):
    w = _prep_weights(inputs)
    x = np.asarray(inputs["L0_attn_inputs"], np.float32)
    nc = build_program()
    in_maps = []
    for core in range(NCORE):
        m = {"xin": np.ascontiguousarray(x[core * SPC:(core + 1) * SPC].reshape(T, DM))}
        m.update(w)
        m.update(CONSTS)
        in_maps.append(m)
    res = run_bass_kernel_spmd(nc, in_maps, core_ids=list(range(NCORE)))
    outs = [np.asarray(res.results[c]["out"]).reshape(SPC, SEQ, DM) for c in range(NCORE)]
    return np.concatenate(outs, axis=0)
